# version 09
# baseline (speedup 1.0000x reference)
import numpy as np
import ml_dtypes
from contextlib import ExitStack

import concourse.bass as bass
import concourse.mybir as mybir
from concourse.bass_utils import run_bass_kernel_spmd

F32 = mybir.dt.float32
BF16 = mybir.dt.bfloat16
AF = mybir.ActivationFunctionType
ALU = mybir.AluOpType
AX = mybir.AxisListType.X
BF = ml_dtypes.bfloat16

CORES = 8
T = 4096
D = 1024
NH, HD = 16, 64
NE = 8
DHID = 4096
NEG = -1.0e9

ENGS = ("tensor", "vector", "scalar", "gpsimd", "sync")


class Rec:

    def __init__(self, nc, es, block=None):
        self.nc = nc
        self.es = es
        self.block = block
        self.sems = {}
        self.cnt = {}
        self.ev = {e: [] for e in ENGS}
        self.waited = {e: {} for e in ENGS}
        self.lw = {}
        self.lr = {}
        for e in ENGS:
            self._mksem("q_" + e)

    def _mksem(self, name):
        self.sems[name] = self.es.enter_context(self.nc.semaphore(name))
        self.cnt[name] = 0

    def sem(self, name):
        if name not in self.sems:
            self._mksem(name)
        return name

    def _wait(self, eng, tok):
        s, v = tok
        if self.waited[eng].get(s, 0) >= v:
            return
        self.waited[eng][s] = v
        h = self.sems[s]
        self.ev[eng].append(lambda e, h=h, v=v: e.wait_ge(h, v))

    def _deps(self, eng, reads, writes):
        toks = []
        for b in reads:
            t = self.lw.get(b)
            if t is not None:
                toks.append(t)
        for b in writes:
            t = self.lw.get(b)
            if t is not None:
                toks.append(t)
            toks.extend(self.lr.get(b, {}).items())
        for t in sorted(set(toks)):
            self._wait(eng, t)

    def _register(self, tok, reads, writes):
        if tok is None:
            return
        s, v = tok
        for b in reads:
            d = self.lr.setdefault(b, {})
            d[s] = max(d.get(s, 0), v)
        for b in writes:
            self.lw[b] = tok
            self.lr[b] = {}

    def op(self, eng, fn, reads=(), writes=(), sig=True):
        self._deps(eng, reads, writes)
        if sig:
            s = "q_" + eng
            self.cnt[s] += 1
            v = self.cnt[s]
            h = self.sems[s]
            self.ev[eng].append(lambda e, fn=fn, h=h: fn(e).then_inc(h, 1))
            tok = (s, v)
        else:
            self.ev[eng].append(lambda e, fn=fn: fn(e))
            tok = None
        self._register(tok, reads, writes)
        return tok

    def mmgroup(self, fns, reads=(), writes=()):
        self._deps("tensor", reads, writes)
        for fn in fns[:-1]:
            self.ev["tensor"].append(lambda e, fn=fn: fn(e))
        s = "q_tensor"
        self.cnt[s] += 1
        v = self.cnt[s]
        h = self.sems[s]
        fnl = fns[-1]
        self.ev["tensor"].append(lambda e, fn=fnl, h=h: fn(e).then_inc(h, 1))
        tok = (s, v)
        self._register(tok, reads, writes)
        return tok

    def dma(self, q, out, in_, reads=(), writes=(), sem=None, **kw):
        if sem is None:
            key = writes[0] if writes else (reads[0] if reads else "misc")
            sem = "d_" + str(key)
        s = self.sem(sem)
        self._deps(q, reads, writes)
        self.cnt[s] += 16
        v = self.cnt[s]
        h = self.sems[s]
        self.ev[q].append(
            lambda e, out=out, in_=in_, h=h, kw=kw: e.dma_start(out, in_, **kw).then_inc(h, 16))
        tok = (s, v)
        self._register(tok, reads, writes)
        return tok

    def barrier(self):
        toks = [(s, c) for s, c in self.cnt.items() if c > 0]
        for e in ENGS:
            for t in toks:
                self._wait(e, t)

    def emit(self, block):
        for e in ENGS:
            if self.ev[e]:
                evs = self.ev[e]
                self.ev[e] = []

                def section(eng, evs=evs):
                    for f in evs:
                        f(eng)

                getattr(block, e)(section)


def layernorm_tile(r, u, xtile, stt, sqrow, sk, sq_key, xt_key):
    r.op("vector", lambda e: e.reduce_sum(stt[:, 0:1], xtile, axis=AX), reads=[xt_key], writes=[sk])
    r.op("scalar", lambda e: e.activation(sqrow, xtile, AF.Square), reads=[xt_key], writes=[sq_key])
    r.op("vector", lambda e: e.reduce_sum(stt[:, 1:2], sqrow, axis=AX), reads=[sq_key], writes=[sk])
    r.op("vector", lambda e: e.tensor_scalar_mul(stt[:, 2:3], stt[:, 0:1], 1.0 / D), reads=[sk], writes=[sk])
    r.op("vector", lambda e: e.tensor_scalar_mul(stt[:, 3:4], stt[:, 1:2], 1.0 / D), reads=[sk], writes=[sk])
    r.op("vector", lambda e: e.tensor_tensor(stt[:, 4:5], stt[:, 2:3], stt[:, 2:3], ALU.mult), reads=[sk], writes=[sk])
    r.op("vector", lambda e: e.tensor_sub(stt[:, 4:5], stt[:, 3:4], stt[:, 4:5]), reads=[sk], writes=[sk])
    r.op("vector", lambda e: e.tensor_scalar_add(stt[:, 4:5], stt[:, 4:5], 1e-5), reads=[sk], writes=[sk])
    r.op("scalar", lambda e: e.activation(stt[:, 5:6], stt[:, 4:5], AF.Sqrt), reads=[sk], writes=[sk])
    r.op("vector", lambda e: e.reciprocal(stt[:, 6:7], stt[:, 5:6]), reads=[sk], writes=[sk])
    r.op("vector", lambda e: e.tensor_scalar_mul(stt[:, 7:8], stt[:, 2:3], -1.0), reads=[sk], writes=[sk])


def emit_l0ffn(r, nc, g, xin, xout, W, u, nexperts=NE, su=None):
    su = su or u
    with ExitStack() as es:
        sb = lambda n, s, d: es.enter_context(nc.sbuf_tensor(n + u, s, d))
        ps = lambda n, s, d: es.enter_context(nc.psum_tensor(n + u, s, d))

        xnT = sb("xnT", [128, 8, 2048], BF16)
        y_acc = sb("yacc", [128, 16, 1024], F32)
        c_g = sb("cg", [128, 16, 8], F32)
        cT = sb("cT", [8, 2048], F32)
        w0b = sb("w0b", [128, 2, 8, 512], BF16)
        w1b = sb("w1b", [128, 2, 8, 512], BF16)
        w2b = sb("w2b", [128, 2, 4, 1024], BF16)
        wg_b = sb("wgb", [128, 8, 8], BF16)
        bg_bc = sb("bgbc", [128, 8], F32)
        b0_pp = sb("b0pp", [128, 8, 32], F32)
        b1_pp = sb("b1pp", [128, 8, 32], F32)
        b2_sb = sb("b2sb", [8, 1024], F32)
        xt = sb("xt", [128, 2, 1024], F32)
        sqb = sb("sq", [128, 2, 1024], F32)
        xnt = sb("xnt", [128, 2, 1024], BF16)
        stt = sb("stt", [128, 2, 8], F32)
        s_sb = sb("ssb", [128, 2, 512], F32)
        pTb = sb("pTb", [128, 2, 4, 512], BF16)
        xo = sb("xo", [128, 2, 1024], F32)
        gt = sb("gt", [128, 2, 8], F32)

        pt_b = ps("ptb", [128, 128], BF16)
        ph = [ps("ph%d" % i, [128, 512], F32) for i in range(2)]
        pg = [ps("pg%d" % i, [128, 512], F32) for i in range(2)]
        py = ps("py", [128, 1024], F32)

        r.dma("gpsimd", wg_b[:], W["Wg"].rearrange("(c p) n -> p c n", p=128),
              writes=[("wgb", u)], sem="d_cwg" + su)
        r.dma("sync", bg_bc[:], W["bg_bc"], writes=[("bgbc", u)], sem="d_cbg" + su)
        r.dma("sync", b0_pp[:], W["b0_pp"], writes=[("b0pp", u)], sem="d_cb0" + su)
        r.dma("sync", b1_pp[:], W["b1_pp"], writes=[("b1pp", u)], sem="d_cb1" + su)
        r.dma("sync", b2_sb[:], W["b2"], writes=[("b2sb", u)], sem="d_cb2" + su)

        for G in range(2):
            t0 = G * 16
            KX = ("xnT", u)
            for tt in range(16):
                p = tt % 2
                xkey = ("xt", u, p)
                xtile = xt[:, p, :]
                r.dma("sync", xtile, xin[(t0 + tt) * 128:(t0 + tt + 1) * 128, :],
                      writes=[xkey], sem="d_xt%d%s" % (p, su))
                layernorm_tile(r, u, xtile, stt[:, p, :], sqb[:, p, :], ("stt", u, p), ("sq", u, p), xkey)
                sk = ("stt", u, p)
                r.op("vector", lambda e, p=p: e.tensor_scalar(
                    xnt[:, p, :], xt[:, p, :], stt[:, p, 7:8], stt[:, p, 6:7], ALU.add, ALU.mult),
                     reads=[xkey, sk], writes=[("xnt", u, p)])
                for dc in range(8):
                    r.op("tensor", lambda e, p=p, dc=dc: e.transpose(
                        pt_b[:], xnt[:, p, dc * 128:(dc + 1) * 128], g["idsb"][:]),
                         reads=[("xnt", u, p), "idsb"], writes=[("ptb", u)])
                    r.op("vector", lambda e, dc=dc, tt=tt: e.tensor_copy(
                        xnT[:, dc, tt * 128:(tt + 1) * 128], pt_b[:]),
                         reads=[("ptb", u)], writes=[KX])

            for tt in range(16):
                p = tt % 2
                gk = ("gt", u, p)
                sk = ("stt", u, p)
                qk = ("sq", u, p)
                r.mmgroup([
                    (lambda e, dc=dc, tt=tt: e.matmul(
                        ph[0][:, 0:8], xnT[:, dc, tt * 128:(tt + 1) * 128], wg_b[:, dc, :],
                        start=(dc == 0), stop=(dc == 7)))
                    for dc in range(8)],
                    reads=[KX, ("wgb", u)], writes=[("ph0", u)])
                r.op("vector", lambda e, p=p: e.tensor_add(gt[:, p, :], ph[0][:, 0:8], bg_bc[:]),
                     reads=[("ph0", u), ("bgbc", u)], writes=[gk])
                r.op("vector", lambda e, p=p: e.reduce_max(stt[:, p, 0:1], gt[:, p, :], axis=AX),
                     reads=[gk], writes=[sk])
                r.op("vector", lambda e, p=p: e.tensor_scalar(
                    sqb[:, p, 0:8], gt[:, p, :], stt[:, p, 0:1], None, ALU.is_equal),
                     reads=[gk, sk], writes=[qk])
                r.op("vector", lambda e, p=p: e.scalar_tensor_tensor(
                    sqb[:, p, 8:16], sqb[:, p, 0:8], -1e30, gt[:, p, :], ALU.mult, ALU.add),
                     reads=[qk, gk], writes=[qk])
                r.op("vector", lambda e, p=p: e.reduce_max(stt[:, p, 1:2], sqb[:, p, 8:16], axis=AX),
                     reads=[qk], writes=[sk])
                r.op("vector", lambda e, p=p: e.tensor_scalar(
                    sqb[:, p, 16:24], sqb[:, p, 8:16], stt[:, p, 1:2], None, ALU.is_equal),
                     reads=[qk, sk], writes=[qk])
                r.op("vector", lambda e, p=p: e.tensor_scalar_mul(stt[:, p, 2:3], stt[:, p, 0:1], -1.0),
                     reads=[sk], writes=[sk])
                r.op("scalar", lambda e, p=p: e.activation(
                    stt[:, p, 3:4], stt[:, p, 1:2], AF.Exp, bias=stt[:, p, 2:3]),
                     reads=[sk], writes=[sk])
                r.op("vector", lambda e, p=p: e.tensor_scalar_add(stt[:, p, 4:5], stt[:, p, 3:4], 1.0),
                     reads=[sk], writes=[sk])
                r.op("vector", lambda e, p=p: e.reciprocal(stt[:, p, 4:5], stt[:, p, 4:5]),
                     reads=[sk], writes=[sk])
                r.op("vector", lambda e, p=p: e.tensor_scalar_mul(stt[:, p, 5:6], stt[:, p, 4:5], 0.5),
                     reads=[sk], writes=[sk])
                r.op("vector", lambda e, p=p: e.tensor_tensor(
                    stt[:, p, 6:7], stt[:, p, 3:4], stt[:, p, 5:6], ALU.mult),
                     reads=[sk], writes=[sk])
                r.op("vector", lambda e, p=p: e.tensor_scalar(
                    sqb[:, p, 24:32], sqb[:, p, 0:8], stt[:, p, 5:6], None, ALU.mult),
                     reads=[qk, sk], writes=[qk])
                r.op("vector", lambda e, p=p, tt=tt: e.scalar_tensor_tensor(
                    c_g[:, tt, :], sqb[:, p, 16:24], stt[:, p, 6:7], sqb[:, p, 24:32], ALU.mult, ALU.add),
                     reads=[qk, sk], writes=[("cg", u)])
                r.op("tensor", lambda e, tt=tt: e.transpose(
                    pg[0][0:8, 0:128], c_g[:, tt, :], g["idsf"][:]),
                     reads=[("cg", u), "idsf"], writes=[("pg0", u)])
                r.op("vector", lambda e, tt=tt: e.tensor_copy(
                    cT[:, tt * 128:(tt + 1) * 128], pg[0][0:8, 0:128]),
                     reads=[("pg0", u)], writes=[("cTb", u)])

            for td in range(16):
                for half in range(2):
                    r.mmgroup([lambda e, td=td, half=half: e.matmul(
                        py[:, half * 512:(half + 1) * 512], cT[:, td * 128:(td + 1) * 128],
                        b2_sb[:, half * 512:(half + 1) * 512], start=True, stop=True)],
                        reads=[("cTb", u), ("b2sb", u)], writes=[("py", u)])
                r.op("vector", lambda e, td=td: e.tensor_copy(y_acc[:, td, :], py[:]),
                     reads=[("py", u)], writes=[("ya", u, td)])

            def emit_td_unit(unit):
                ei_, ts_, td_, buf_ = unit
                gtd = ts_ * 4 + td_
                for half in range(2):
                    r.mmgroup([
                        (lambda e, jc=jc, td=td_, half=half, ts=ts_, buf=buf_: e.matmul(
                            py[:, half * 512:(half + 1) * 512],
                            pTb[:, ts % 2, jc, td * 128:(td + 1) * 128],
                            w2b[:, buf, jc, half * 512:(half + 1) * 512],
                            start=(jc == 0), stop=(jc == 3)))
                        for jc in range(4)],
                        reads=[("pT", u, ts_ % 2, 0), ("pT", u, ts_ % 2, 1),
                               ("pT", u, ts_ % 2, 2), ("pT", u, ts_ % 2, 3),
                               ("w2b", u, buf_)],
                        writes=[("py", u)])
                r.op("vector", lambda e, gtd=gtd, ei=ei_: e.scalar_tensor_tensor(
                    y_acc[:, gtd, :], py[:], c_g[:, gtd:gtd + 1, ei:ei + 1],
                    y_acc[:, gtd, :], ALU.mult, ALU.add),
                     reads=[("py", u), ("cg", u)], writes=[("ya", u, gtd)])

            pend = []
            for ei in range(nexperts):
                for j in range(8):
                    buf = (ei * 8 + j) % 2
                    js, je = j * 512, (j + 1) * 512
                    r.dma("gpsimd", w0b[:, buf, :, :], W["W0"][ei, j],
                          writes=[("w0b", u, buf)], sem="d_w0%d%s" % (buf, su))
                    r.dma("gpsimd", w1b[:, buf, :, :], W["W1"][ei, j],
                          writes=[("w1b", u, buf)], sem="d_w1%d%s" % (buf, su))
                    r.dma("gpsimd", w2b[:, buf, :, :], W["W2"][ei, j],
                          writes=[("w2b", u, buf)], sem="d_w2%d%s" % (buf, su))
                    for ts in range(4):
                        ns, ne_ = ts * 512, (ts + 1) * 512
                        for jc in range(4):
                            bk = (jc + ts) % 2
                            hp, gp = ph[bk], pg[bk]
                            hkey, gkey = ("ph%d" % bk, u), ("pg%d" % bk, u)
                            r.mmgroup([
                                (lambda e, dc=dc, jc=jc, hp=hp, buf=buf, ns=ns, ne_=ne_: e.matmul(
                                    hp[:], w0b[:, buf, dc, jc * 128:(jc + 1) * 128],
                                    xnT[:, dc, ns:ne_], start=(dc == 0), stop=(dc == 7)))
                                for dc in range(8)],
                                reads=[("w0b", u, buf), KX], writes=[hkey])
                            if pend:
                                emit_td_unit(pend.pop(0))
                            r.mmgroup([
                                (lambda e, dc=dc, jc=jc, gp=gp, buf=buf, ns=ns, ne_=ne_: e.matmul(
                                    gp[:], w1b[:, buf, dc, jc * 128:(jc + 1) * 128],
                                    xnT[:, dc, ns:ne_], start=(dc == 0), stop=(dc == 7)))
                                for dc in range(8)],
                                reads=[("w1b", u, buf), KX], writes=[gkey])
                            sp = jc % 2
                            jj = j * 4 + jc
                            r.op("scalar", lambda e, gp=gp, sp=sp, ei=ei, jj=jj: e.activation(
                                s_sb[:, sp, :], gp[:], AF.Silu, bias=b1_pp[:, ei:ei + 1, jj:jj + 1]),
                                 reads=[gkey, ("b1pp", u)], writes=[("ssb", u, sp)])
                            r.op("vector", lambda e, hp=hp, sp=sp, ei=ei, jj=jj, ts=ts, jc=jc: e.scalar_tensor_tensor(
                                pTb[:, ts % 2, jc, :], hp[:], b0_pp[:, ei:ei + 1, jj:jj + 1],
                                s_sb[:, sp, :], ALU.add, ALU.mult),
                                 reads=[hkey, ("b0pp", u), ("ssb", u, sp)],
                                 writes=[("pT", u, ts % 2, jc)])
                        for td in range(4):
                            pend.append((ei, ts, td, buf))
            for unit in pend:
                emit_td_unit(unit)
            pend = []

            for td in range(16):
                p = td % 2
                xkey = ("xt", u, p)
                xtile = xt[:, p, :]
                r.dma("sync", xtile, xin[(t0 + td) * 128:(t0 + td + 1) * 128, :],
                      writes=[xkey], sem="d_xt%d%s" % (p, su))
                r.op("vector", lambda e, xtile=xtile, td=td, p=p: e.tensor_add(
                    xo[:, p, :], xtile, y_acc[:, td, :]),
                     reads=[xkey, ("ya", u, td)], writes=[("xo", u, p)])
                r.dma("sync", xout[(t0 + td) * 128:(t0 + td + 1) * 128, :], xo[:, p, :],
                      reads=[("xo", u, p)], sem="d_so%d%s" % (p, su))
            r.barrier()
        r.emit(r.block)


def _decl_l0ffn_weights(nc, pref=""):
    W = {
        "Wg": nc.dram_tensor(pref + "Wg", [D, NE], F32, kind="ExternalInput").ap(),
        "W0": nc.dram_tensor(pref + "W0", [NE, 8, 128, 8, 512], F32, kind="ExternalInput").ap(),
        "W1": nc.dram_tensor(pref + "W1", [NE, 8, 128, 8, 512], F32, kind="ExternalInput").ap(),
        "W2": nc.dram_tensor(pref + "W2", [NE, 8, 128, 4, 1024], F32, kind="ExternalInput").ap(),
        "b2": nc.dram_tensor(pref + "b2", [NE, D], F32, kind="ExternalInput").ap(),
        "bg_bc": nc.dram_tensor(pref + "bg_bc", [128, 8], F32, kind="ExternalInput").ap(),
        "b0_pp": nc.dram_tensor(pref + "b0_pp", [128, 8, 32], F32, kind="ExternalInput").ap(),
        "b1_pp": nc.dram_tensor(pref + "b1_pp", [128, 8, 32], F32, kind="ExternalInput").ap(),
    }
    return W


def _host_l0ffn_weights(inp, pref=""):
    b0 = np.ascontiguousarray(inp["L0_ffn_b0"].reshape(8, 32, 128).transpose(2, 0, 1))
    b1 = np.ascontiguousarray(inp["L0_ffn_b1"].reshape(8, 32, 128).transpose(2, 0, 1))
    w0h = np.ascontiguousarray(
        inp["L0_ffn_W0"].reshape(8, 8, 128, 8, 512).transpose(0, 3, 2, 1, 4))
    w1h = np.ascontiguousarray(
        inp["L0_ffn_W1"].reshape(8, 8, 128, 8, 512).transpose(0, 3, 2, 1, 4))
    w2h = np.ascontiguousarray(
        inp["L0_ffn_W2"].reshape(8, 8, 4, 128, 1024).transpose(0, 1, 3, 2, 4))
    return {
        pref + "Wg": inp["L0_ffn_Wg"],
        pref + "W0": w0h,
        pref + "W1": w1h,
        pref + "W2": w2h,
        pref + "b2": inp["L0_ffn_b2"],
        pref + "bg_bc": np.ascontiguousarray(np.broadcast_to(inp["L0_ffn_bg"], (128, 8))),
        pref + "b0_pp": b0,
        pref + "b1_pp": b1,
    }


def _globals_setup(nc, es, r):
    g = {
        "idsb": es.enter_context(nc.sbuf_tensor("idsb", [128, 128], BF16)),
        "idsf": es.enter_context(nc.sbuf_tensor("idsf", [128, 128], F32)),
    }
    idsb_d = nc.dram_tensor("idsb_d", [128, 128], BF16, kind="ExternalInput").ap()
    idsf_d = nc.dram_tensor("idsf_d", [128, 128], F32, kind="ExternalInput").ap()
    r.dma("sync", g["idsb"][:], idsb_d, writes=["idsb"])
    r.dma("sync", g["idsf"][:], idsf_d, writes=["idsf"])
    return g


def _host_globals():
    return {
        "idsb_d": np.eye(128, dtype=np.float32).astype(BF),
        "idsf_d": np.eye(128, dtype=np.float32),
    }


def build_stageA(nexperts=NE):
    nc = bass.Bass("TRN2", target_bir_lowering=False)
    x = nc.dram_tensor("x", [T, D], F32, kind="ExternalInput").ap()
    out = nc.dram_tensor("out", [T, D], F32, kind="ExternalOutput").ap()
    W = _decl_l0ffn_weights(nc)
    with ExitStack() as es:
        r = Rec(nc, es)
        r.block = es.enter_context(nc.Block())
        g = _globals_setup(nc, es, r)
        emit_l0ffn(r, nc, g, x, out, W, "A", nexperts=nexperts)
        r.barrier()
        r.emit(r.block)
    return nc




def emit_l0attn(r, nc, g, xin, xout, W, u):
    with ExitStack() as es:
        sb = lambda n, s, d: es.enter_context(nc.sbuf_tensor(n + u, s, d))
        ps = lambda n, s, d: es.enter_context(nc.psum_tensor(n + u, s, d))

        xn = sb("xn", [128, 32, 1024], BF16)
        kT = sb("kT", [128, 8, 4096], BF16)
        pS = [ps("pS%d" % i, [128, 512], F32) for i in range(4)]
        pt_b = ps("ptb", [128, 128], BF16)
        py = ps("py", [128, 1024], F32)

        with ExitStack() as es1:
            sb1 = lambda n, s, d: es1.enter_context(nc.sbuf_tensor(n + u, s, d))
            wqk = sb1("wqk", [128, 8, 2048], BF16)
            bqk = sb1("bqk", [128, 16], F32)
            r.dma("sync", bqk[:], W["bqk_pp"], writes=[("bqk", u)])
            xnTg = sb1("xnTg", [128, 2, 8, 512], BF16)
            qstg = sb1("qstg", [128, 1, 8, 512], BF16)
            xt = sb1("xt", [128, 2, 1024], F32)
            sqb = sb1("sq", [128, 1, 1024], F32)
            xnt = sb1("xnt", [128, 2, 1024], BF16)
            stt = sb1("stt", [128, 2, 8], F32)

            r.dma("gpsimd", wqk[:], W["wqkT"].rearrange("(c p) n -> p c n", p=128),
                  writes=[("wqk", u)])
            for tg in range(8):
                gp = tg % 2
                for tt4 in range(4):
                    tt = tg * 4 + tt4
                    p = tt % 2
                    xkey = ("xt", u, p)
                    xtile = xt[:, p, :]
                    r.dma("sync", xtile, xin[tt * 128:(tt + 1) * 128, :], writes=[xkey],
                          sem="d_xt%d%s" % (p, u))
                    layernorm_tile(r, u, xtile, stt[:, p, :], sqb[:, 0, :], ("stt", u, p), ("sq", u, 0), xkey)
                    r.op("vector", lambda e, p=p: e.tensor_scalar(
                        xnt[:, p, :], xt[:, p, :], stt[:, p, 7:8], stt[:, p, 6:7], ALU.add, ALU.mult),
                         reads=[xkey, ("stt", u, p)], writes=[("xnt", u, p)])
                    r.op("vector", lambda e, p=p, tt=tt: e.tensor_copy(xn[:, tt, :], xnt[:, p, :]),
                         reads=[("xnt", u, p)], writes=[("xn", u)])
                    for dc in range(8):
                        r.op("tensor", lambda e, p=p, dc=dc: e.transpose(
                            pt_b[:], xnt[:, p, dc * 128:(dc + 1) * 128], g["idsb"][:]),
                             reads=[("xnt", u, p), "idsb"], writes=[("ptb", u)])
                        r.op("vector", lambda e, dc=dc, tt4=tt4, gp=gp: e.tensor_copy(
                            xnTg[:, gp, dc, tt4 * 128:(tt4 + 1) * 128], pt_b[:]),
                             reads=[("ptb", u)], writes=[("xnTg", u, gp)])
                for m in range(16):
                    bank = m % 2
                    pk = ("pS%d" % bank, u)
                    r.mmgroup([
                        (lambda e, dc=dc, m=m, gp=gp, bank=bank: e.matmul(
                            pS[bank][:], wqk[:, dc, m * 128:(m + 1) * 128],
                            xnTg[:, gp, dc, :], start=(dc == 0), stop=(dc == 7)))
                        for dc in range(8)],
                        reads=[("wqk", u), ("xnTg", u, gp)], writes=[pk])
                    if m < 8:
                        r.op("vector", lambda e, m=m, bank=bank: e.tensor_scalar(
                            qstg[:, 0, m, :], pS[bank][:], bqk[:, m:m + 1], None, ALU.add),
                             reads=[pk, ("bqk", u)], writes=[("qstg", u)])
                    else:
                        r.op("vector", lambda e, m=m, gp=gp, tg=tg, bank=bank: e.tensor_scalar(
                            kT[:, m - 8, tg * 512:(tg + 1) * 512], pS[bank][:],
                            bqk[:, m:m + 1], None, ALU.add),
                             reads=[pk, ("bqk", u)], writes=[("kT", u)])
                r.dma("sync", W["qTs"][:, :, tg * 512:(tg + 1) * 512].rearrange("c p n -> p c n"),
                      qstg[:, 0, :, :], reads=[("qstg", u)], sem="d_qs%s" % u)
            r.barrier()
            r.emit(r.block)

        with ExitStack() as es2:
            sb2 = lambda n, s, d: es2.enter_context(nc.sbuf_tensor(n + u, s, d))
            qTt = sb2("qTt", [128, 2, 8, 128], BF16)
            Eb = sb2("Eb", [128, 4096], BF16)
            tri = sb2("tri", [128, 4, 512], BF16)
            r.dma("sync", tri[:], W["tri4"].rearrange("r p n -> p r n"), writes=[("tri", u)])
            avg = sb2("avg", [128, 4096], F32)
            gum = sb2("gum", [128, 4096], F32)
            dwb = sb2("dwb", [128, 4096], BF16)
            dwT = sb2("dwT", [128, 2, 128], BF16)
            zz = sb2("zz", [128, 8], F32)
            zcv = sb2("zcv", [128, 2, 8], F32)
            scb = sb2("scb", [128, 2], F32)
            xt2 = sb2("xt2", [128, 2, 1024], F32)
            xo2 = sb2("xo2", [128, 2, 1024], F32)

            r.op("vector", lambda e: e.memset(scb[:, 0:1], 0.125), writes=[("scb", u)])
            r.op("vector", lambda e: e.memset(scb[:, 1:2], 0.5), writes=[("scb", u)])

            for qt in range(32):
                nk = qt // 4 + 1
                span = nk * 512
                qp = qt % 2
                r.dma("sync", qTt[:, qp, :, :],
                      W["qTs"][:, :, qt * 128:(qt + 1) * 128].rearrange("c p n -> p c n"),
                      writes=[("qTt", u, qp)], sem="d_qt%d%s" % (qp, u))
                r.dma("sync", gum[:, 0:span], W["gum"][qt * 128:(qt + 1) * 128, 0:span],
                      writes=[("gum", u)], sem="d_gum%s" % u)
                for hp in range(8):
                    for h2 in range(2):
                        h = 2 * hp + h2
                        ek = ("Eb", u)
                        zk = ("zz", u, h2)
                        for kt in range(nk):
                            bank = 2 * (kt % 2) + h2
                            pk = ("pS%d" % bank, u)
                            base = 64 * h2
                            r.mmgroup([(lambda e, h=h, kt=kt, bank=bank, base=base, qp=qp: e.matmul(
                                pS[bank][:], qTt[base:base + 64, qp, h // 2, :],
                                kT[base:base + 64, h // 2, kt * 512:(kt + 1) * 512],
                                start=True, stop=True))],
                                reads=[("qTt", u, qp), ("kT", u)], writes=[pk])
                            if kt == nk - 1:
                                r.op("vector", lambda e, bank=bank, qt=qt: e.tensor_tensor(
                                    pS[bank][:], pS[bank][:], tri[:, qt % 4, :], ALU.add),
                                     reads=[("tri", u), pk], writes=[pk])
                            r.op("scalar", lambda e, bank=bank, kt=kt, h2=h2: e.activation(
                                Eb[:, kt * 512:(kt + 1) * 512], pS[bank][:], AF.Exp,
                                scale=scb[:, 0:1], accum_out=zcv[:, h2, kt:kt + 1]),
                                 reads=[pk, ("scb", u)], writes=[ek, ("zzc", u, h2)])
                        r.op("vector", lambda e, h2=h2, nk=nk: e.reduce_sum(
                            zz[:, h2:h2 + 1], zcv[:, h2, 0:nk], axis=AX),
                             reads=[("zzc", u, h2)], writes=[zk])
                        r.op("vector", lambda e, h2=h2: e.tensor_scalar(
                            zz[:, 2 + h2:3 + h2], zz[:, h2:h2 + 1], 16.0, 1e-30, ALU.mult, ALU.add),
                             reads=[zk], writes=[zk])
                        r.op("vector", lambda e, h2=h2: e.reciprocal(
                            zz[:, 2 + h2:3 + h2], zz[:, 2 + h2:3 + h2]),
                             reads=[zk], writes=[zk])
                        if hp == 0 and h2 == 0:
                            r.op("vector", lambda e, h2=h2, span=span: e.tensor_scalar(
                                avg[:, 0:span], Eb[:, 0:span], zz[:, 2 + h2:3 + h2], None, ALU.mult),
                                 reads=[ek, zk], writes=[("avg", u)])
                        else:
                            r.op("vector", lambda e, h2=h2, span=span: e.scalar_tensor_tensor(
                                avg[:, 0:span], Eb[:, 0:span], zz[:, 2 + h2:3 + h2],
                                avg[:, 0:span], ALU.mult, ALU.add),
                                 reads=[ek, zk, ("avg", u)], writes=[("avg", u)])
                r.op("vector", lambda e, span=span: e.tensor_scalar(
                    avg[:, 0:span], avg[:, 0:span], -40.0, 40.0, ALU.max, ALU.min),
                     reads=[("avg", u)], writes=[("avg", u)])
                r.op("vector", lambda e, span=span: e.tensor_add(
                    avg[:, 0:span], avg[:, 0:span], gum[:, 0:span]),
                     reads=[("avg", u), ("gum", u)], writes=[("avg", u)])
                r.op("vector", lambda e, span=span, qt=qt: e.tensor_tensor(
                    avg[:, span - 512:span], avg[:, span - 512:span], tri[:, qt % 4, :], ALU.add),
                     reads=[("avg", u), ("tri", u)], writes=[("avg", u)])
                r.op("scalar", lambda e, span=span: e.activation(
                    dwb[:, 0:span], avg[:, 0:span], AF.Exp, scale=scb[:, 1:2],
                    accum_out=zz[:, 6:7]),
                     reads=[("avg", u), ("scb", u)], writes=[("dwb", u), ("zz2", u)])
                r.op("vector", lambda e: e.tensor_scalar(
                    zz[:, 7:8], zz[:, 6:7], 1.0, 1e-30, ALU.mult, ALU.add),
                     reads=[("zz2", u)], writes=[("zz2", u)])
                r.op("vector", lambda e: e.reciprocal(zz[:, 7:8], zz[:, 7:8]),
                     reads=[("zz2", u)], writes=[("zz2", u)])
                for ktile in range(nk * 4):
                    dp = ktile % 2
                    r.op("tensor", lambda e, ktile=ktile: e.transpose(
                        pt_b[:], dwb[:, ktile * 128:(ktile + 1) * 128], g["idsb"][:]),
                         reads=[("dwb", u), "idsb"], writes=[("ptb", u)])
                    r.op("vector", lambda e, dp=dp: e.tensor_copy(dwT[:, dp, :], pt_b[:]),
                         reads=[("ptb", u)], writes=[("dwT", u, dp)])
                    for half in range(2):
                        r.mmgroup([(lambda e, ktile=ktile, half=half, dp=dp, nk=nk: e.matmul(
                            py[:, half * 512:(half + 1) * 512], dwT[:, dp, :],
                            xn[:, ktile, half * 512:(half + 1) * 512],
                            start=(ktile == 0), stop=(ktile == nk * 4 - 1)))],
                            reads=[("dwT", u, dp), ("xn", u)], writes=[("py", u)])
                p2 = qt % 2
                xkey = ("xt2", u, p2)
                r.dma("sync", xt2[:, p2, :], xin[qt * 128:(qt + 1) * 128, :], writes=[xkey],
                      sem="d_x2%d%s" % (p2, u))
                r.op("vector", lambda e, p2=p2: e.scalar_tensor_tensor(
                    xo2[:, p2, :], py[:], zz[:, 7:8], xt2[:, p2, :], ALU.mult, ALU.add),
                     reads=[("py", u), ("zz2", u), xkey], writes=[("xo2", u, p2)])
                r.dma("sync", xout[qt * 128:(qt + 1) * 128, :], xo2[:, p2, :],
                      reads=[("xo2", u, p2)], sem="d_o2%d%s" % (p2, u))
            r.barrier()
        r.emit(r.block)


def _decl_l0attn_weights(nc, pref=""):
    return {
        "wqkT": nc.dram_tensor(pref + "wqkT", [D, 2 * D], F32, kind="ExternalInput").ap(),
        "bqk_pp": nc.dram_tensor(pref + "bqk_pp", [128, 16], F32, kind="ExternalInput").ap(),
        "tri4": nc.dram_tensor(pref + "tri4", [4, 128, 512], BF16, kind="ExternalInput").ap(),
        "gum": nc.dram_tensor(pref + "gum", [T, T], F32, kind="ExternalInput").ap(),
        "qTs": nc.dram_tensor(pref + "qTs", [8, 128, T], BF16, kind="Internal").ap(),
    }


def _host_l0attn_weights(inp, pref=""):
    w = inp["L0_attn_in_proj_weight"]
    b = inp["L0_attn_in_proj_bias"]
    wqkT = np.ascontiguousarray(w[:2 * D].T)
    bqk = np.ascontiguousarray(b[:2 * D].reshape(16, 128).T)
    tri4 = np.empty((4, 128, 512), np.float32)
    for rr in range(4):
        qr = np.arange(128)[:, None]
        kc = np.arange(512)[None, :]
        tri4[rr] = np.where(kc < rr * 128 + qr, 0.0, NEG).astype(np.float32)
    return {pref + "wqkT": wqkT, pref + "bqk_pp": bqk, pref + "tri4": tri4.astype(BF)}


def _host_gumbel():
    def _rotl(x, d):
        return (x << np.uint32(d)) | (x >> np.uint32(32 - d))

    n = CORES * T * T
    minval, maxval = np.float32(1e-06), np.float32(1.0 - 1e-06)
    idx = np.arange(n, dtype=np.uint64)
    x0 = (idx >> np.uint64(32)).astype(np.uint32)
    x1 = (idx & np.uint64(0xFFFFFFFF)).astype(np.uint32)
    del idx
    ks0, ks1 = np.uint32(0), np.uint32(42)
    ks2 = ks0 ^ ks1 ^ np.uint32(0x1BD11BDA)
    rotations = [(13, 15, 26, 6), (17, 29, 16, 24)]
    x0 = x0 + ks0
    x1 = x1 + ks1
    ks = [ks1, ks2, ks0]
    for i in range(5):
        for rr in rotations[i % 2]:
            x0 += x1
            x1 = _rotl(x1, rr)
            x1 = x0 ^ x1
        x0 += ks[i % 3]
        x1 += ks[(i + 1) % 3] + np.uint32(i + 1)
    bits = x0 ^ x1
    del x0, x1
    fl = ((bits >> np.uint32(9)) | np.uint32(0x3F800000)).view(np.float32) - np.float32(1.0)
    del bits
    u = np.maximum(minval, fl * (maxval - minval) + minval)
    del fl
    gm = -np.log(-np.log(u))
    return gm.reshape(CORES, T, T)


def build_stageB():
    nc = bass.Bass("TRN2", target_bir_lowering=False)
    x = nc.dram_tensor("x", [T, D], F32, kind="ExternalInput").ap()
    out = nc.dram_tensor("out", [T, D], F32, kind="ExternalOutput").ap()
    W = _decl_l0attn_weights(nc)
    with ExitStack() as es:
        r = Rec(nc, es)
        r.block = es.enter_context(nc.Block())
        g = _globals_setup(nc, es, r)
        emit_l0attn(r, nc, g, x, out, W, "B")
        r.barrier()
        r.emit(r.block)
    return nc




def bcast16(ap2d, inner):
    return bass.AP(ap2d.tensor, ap2d.offset, [ap2d.ap[0], [0, 16], [1, inner]])


def emit_l1ffn(r, nc, g, xin, xout, W, u, su=None):
    su = su or u
    with ExitStack() as es:
        sb = lambda n, s, d: es.enter_context(nc.sbuf_tensor(n + u, s, d))
        scb = sb("scb", [128, 2], F32)
        r.op("vector", lambda e: e.memset(scb[:, 0:1], 0.125), writes=[("scb", u)])
        esA = es.enter_context(ExitStack())
        kT = esA.enter_context(nc.sbuf_tensor("kT" + u, [128, 8, 4096], BF16))
        Vaug = esA.enter_context(nc.sbuf_tensor("Vaug" + u, [128, 32, 1040], BF16))
        Vv = Vaug[:].rearrange("p t (h c) -> p t h c", c=65)
        r.op("vector", lambda e: e.memset(Vv[:, :, :, 64:65], 1.0), writes=[("Vones", u)])

        with ExitStack() as e1:
            sb1 = lambda n, s, d: e1.enter_context(nc.sbuf_tensor(n + u, s, d))
            ps1 = lambda n, s, d: e1.enter_context(nc.psum_tensor(n + u, s, d))
            pt_b = ps1("ptb", [128, 128], BF16)
            xt = sb1("xt", [128, 2, 1024], F32)
            sqb = sb1("sq", [128, 1024], F32)
            xnt = sb1("xnt", [128, 2, 1024], BF16)
            stt = sb1("stt", [128, 2, 8], F32)
            xnTg = sb1("xnTg", [128, 8, 512], BF16)
            for tg in range(8):
                for tt4 in range(4):
                    tt = tg * 4 + tt4
                    p = tt % 2
                    xkey = ("xt", u, p)
                    xtile = xt[:, p, :]
                    r.dma("sync", xtile, xin[tt * 128:(tt + 1) * 128, :], writes=[xkey],
                          sem="d_xt%d%s" % (p, su))
                    layernorm_tile(r, u, xtile, stt[:, p, :], sqb[:], ("stt", u, p), ("sq", u), xkey)
                    r.op("vector", lambda e, p=p: e.tensor_scalar(
                        xnt[:, p, :], xt[:, p, :], stt[:, p, 7:8], stt[:, p, 6:7], ALU.add, ALU.mult),
                         reads=[xkey, ("stt", u, p)], writes=[("xnt", u, p)])
                    r.dma("sync", W["xns"][tt * 128:(tt + 1) * 128, :], xnt[:, p, :],
                          reads=[("xnt", u, p)], sem="d_xns%d%s" % (p, su))
                    for dc in range(8):
                        r.op("tensor", lambda e, p=p, dc=dc: e.transpose(
                            pt_b[:], xnt[:, p, dc * 128:(dc + 1) * 128], g["idsb"][:]),
                             reads=[("xnt", u, p), "idsb"], writes=[("ptb", u)])
                        r.op("vector", lambda e, dc=dc, tt4=tt4: e.tensor_copy(
                            xnTg[:, dc, tt4 * 128:(tt4 + 1) * 128], pt_b[:]),
                             reads=[("ptb", u)], writes=[("xnTg", u)])
                r.dma("sync", W["xnTs"][:, :, tg * 512:(tg + 1) * 512].rearrange("c p n -> p c n"),
                      xnTg[:], reads=[("xnTg", u)], sem="d_xnT%s" % su)
            r.barrier()
            r.emit(r.block)

        with ExitStack() as e1:
            sb1 = lambda n, s, d: e1.enter_context(nc.sbuf_tensor(n + u, s, d))
            ps1 = lambda n, s, d: e1.enter_context(nc.psum_tensor(n + u, s, d))
            pt_b = ps1("ptbb", [128, 128], BF16)
            pP = [ps1("pP%d" % i, [128, 512], F32) for i in range(2)]
            wpr = sb1("wpr", [128, 2, 8, 1024], BF16)
            xnTg = sb1("xnTg2", [128, 2, 8, 512], BF16)
            bias_bc = sb1("bbc", [128, 1024], F32)
            tabs = sb1("tabs", [128, 2, 2, 64], F32)
            work = sb1("work", [128, 1024], F32)
            tmp1 = sb1("tmp1", [128, 512], F32)
            tmp2 = sb1("tmp2", [128, 512], F32)
            rro = sb1("rro", [128, 1024], BF16)
            qstg = sb1("qstg", [128, 8, 512], BF16)

            for ip, pr in enumerate(["q", "k", "v"]):
                wb_ = ip % 2
                r.dma("gpsimd", wpr[:, wb_, :, :],
                      W["qkvwT"][:, ip * 1024:(ip + 1) * 1024].rearrange("(c p) n -> p c n", p=128),
                      writes=[("wpr", u, wb_)], sem="d_wp%d%s" % (wb_, su))
                r.dma("sync", bias_bc[:], W["bqkv_bc"][:, ip * 1024:(ip + 1) * 1024],
                      writes=[("bbc", u)], sem="d_bbc%s" % su)
                for tg in range(8):
                    gp = tg % 2
                    r.dma("sync", xnTg[:, gp, :, :],
                          W["xnTs"][:, :, tg * 512:(tg + 1) * 512].rearrange("c p n -> p c n"),
                          writes=[("xnTg2", u, gp)], sem="d_xg%d%s" % (gp, su))
                    for tt4 in range(4):
                        tt = tg * 4 + tt4
                        if pr != "v":
                            r.dma("sync", tabs[:, tt % 2, 0, :], W["cosF"][:, tt, :],
                                  writes=[("tabs", u, tt % 2)], sem="d_tc%d%s" % (tt % 2, su))
                            r.dma("sync", tabs[:, tt % 2, 1, :], W["sinF"][:, tt, :],
                                  writes=[("tabs", u, tt % 2)], sem="d_ts%d%s" % (tt % 2, su))
                        for half in range(2):
                            bk = (tt4 * 2 + half) % 2
                            pk = ("pP%d" % bk, u)
                            r.mmgroup([
                                (lambda e, dc=dc, tt4=tt4, half=half, gp=gp, wb_=wb_, bk=bk: e.matmul(
                                    pP[bk][:], xnTg[:, gp, dc, tt4 * 128:(tt4 + 1) * 128],
                                    wpr[:, wb_, dc, half * 512:(half + 1) * 512],
                                    start=(dc == 0), stop=(dc == 7)))
                                for dc in range(8)],
                                reads=[("xnTg2", u, gp), ("wpr", u, wb_)], writes=[pk])
                            if pr == "v":
                                pv = pP[bk][:].rearrange("p (h c) -> p h c", c=64)
                                bv = bias_bc[:, half * 512:(half + 1) * 512].rearrange(
                                    "p (h c) -> p h c", c=64)
                                r.op("vector", lambda e, tt=tt, half=half, pv=pv, bv=bv: e.tensor_tensor(
                                    Vv[:, tt, half * 8:(half + 1) * 8, 0:64], pv, bv, ALU.add),
                                     reads=[pk, ("bbc", u)], writes=[("V", u)])
                            else:
                                r.op("vector", lambda e, half=half, bk=bk: e.tensor_tensor(
                                    work[:, half * 512:(half + 1) * 512], pP[bk][:],
                                    bias_bc[:, half * 512:(half + 1) * 512], ALU.add),
                                     reads=[pk, ("bbc", u)], writes=[("work", u)])
                        if pr != "v":
                            tp = tt % 2
                            wv = work[:].rearrange("p (h c) -> p h c", c=64)
                            rv = rro[:].rearrange("p (h c) -> p h c", c=64)
                            t1v = tmp1[:].rearrange("p (h c) -> p h c", c=32)
                            t2v = tmp2[:].rearrange("p (h c) -> p h c", c=32)
                            c1 = bcast16(tabs[:, tp, 0, 0:32], 32)
                            c2 = bcast16(tabs[:, tp, 0, 32:64], 32)
                            s1 = bcast16(tabs[:, tp, 1, 0:32], 32)
                            s2 = bcast16(tabs[:, tp, 1, 32:64], 32)
                            tk = ("tabs", u, tp)
                            r.op("vector", lambda e, c1=c1, wv=wv, t1v=t1v: e.tensor_tensor(
                                t1v, wv[:, :, 0:32], c1, ALU.mult),
                                 reads=[("work", u), tk], writes=[("tmp1", u)])
                            r.op("vector", lambda e, s1=s1, wv=wv, t2v=t2v: e.tensor_tensor(
                                t2v, wv[:, :, 32:64], s1, ALU.mult),
                                 reads=[("work", u), tk], writes=[("tmp2", u)])
                            r.op("vector", lambda e, rv=rv, t1v=t1v, t2v=t2v: e.tensor_tensor(
                                rv[:, :, 0:32], t1v, t2v, ALU.subtract),
                                 reads=[("tmp1", u), ("tmp2", u)], writes=[("rro", u)])
                            r.op("vector", lambda e, c2=c2, wv=wv, t1v=t1v: e.tensor_tensor(
                                t1v, wv[:, :, 32:64], c2, ALU.mult),
                                 reads=[("work", u), tk], writes=[("tmp1", u)])
                            r.op("vector", lambda e, s2=s2, wv=wv, t2v=t2v: e.tensor_tensor(
                                t2v, wv[:, :, 0:32], s2, ALU.mult),
                                 reads=[("work", u), tk], writes=[("tmp2", u)])
                            r.op("vector", lambda e, rv=rv, t1v=t1v, t2v=t2v: e.tensor_tensor(
                                rv[:, :, 32:64], t1v, t2v, ALU.add),
                                 reads=[("tmp1", u), ("tmp2", u)], writes=[("rro", u)])
                            for dc in range(8):
                                r.op("tensor", lambda e, dc=dc: e.transpose(
                                    pt_b[:], rro[:, dc * 128:(dc + 1) * 128], g["idsb"][:]),
                                     reads=[("rro", u), "idsb"], writes=[("ptb", u)])
                                if pr == "q":
                                    r.op("vector", lambda e, dc=dc, tt4=tt4: e.tensor_copy(
                                        qstg[:, dc, tt4 * 128:(tt4 + 1) * 128], pt_b[:]),
                                         reads=[("ptb", u)], writes=[("qstg", u)])
                                else:
                                    r.op("vector", lambda e, dc=dc, tt=tt: e.tensor_copy(
                                        kT[:, dc, tt * 128:(tt + 1) * 128], pt_b[:]),
                                         reads=[("ptb", u)], writes=[("kT", u)])
                    if pr == "q":
                        r.dma("sync", W["qTs"][:, :, tg * 512:(tg + 1) * 512].rearrange("c p n -> p c n"),
                              qstg[:], reads=[("qstg", u)], sem="d_qs%s" % su)
            r.barrier()
            r.emit(r.block)

        with ExitStack() as e2:
            sb2 = lambda n, s, d: e2.enter_context(nc.sbuf_tensor(n + u, s, d))
            ps2 = lambda n, s, d: e2.enter_context(nc.psum_tensor(n + u, s, d))
            pS = [ps2("pS%d" % i, [128, 1024], F32) for i in range(2)]
            pOT = [ps2("pOT%d" % i, [128, 512], F32) for i in range(1)]
            pt_f = ps2("ptf", [128, 128], F32)
            pt_b2 = ps2("ptb2", [128, 128], BF16)
            qTg = sb2("qTg", [128, 1, 8, 512], BF16)
            ET = sb2("ET", [128, 2, 1024], BF16)
            oT = sb2("oT", [128, 2, 512], F32)
            rz = sb2("rz", [128, 2], F32)
            ao = sb2("ao", [128, 4, 1024], BF16)
            aostg = sb2("aostg", [128, 8, 512], BF16)

            for qg in range(8):
                gp = 0
                r.dma("sync", qTg[:, 0, :, :],
                      W["qTs"][:, :, qg * 512:(qg + 1) * 512].rearrange("c p n -> p c n"),
                      writes=[("qTg", u, gp)], sem="d_qg%s" % u)
                for h in range(16):
                    hp = h // 2
                    h2 = h % 2
                    base = 64 * h2
                    ok = ("pOT0", u)
                    for kt2 in range(16):
                        bank = kt2 % 2
                        for sub in range(2):
                            kt = 2 * kt2 + sub
                            r.mmgroup([(lambda e, hp=hp, kt=kt, bank=bank, base=base, sub=sub: e.matmul(
                                pS[bank][:, sub * 512:(sub + 1) * 512],
                                kT[base:base + 64, hp, kt * 128:(kt + 1) * 128],
                                qTg[base:base + 64, 0, hp, :], start=True, stop=True))],
                                reads=[("qTg", u, gp), ("kT", u)], writes=[("pS", u, bank, sub)])
                        ekey = ("ET", u, bank)
                        r.op("scalar", lambda e, bank=bank: e.activation(
                            ET[:, bank, :], pS[bank][:], AF.Exp, scale=scb[:, 0:1]),
                             reads=[("pS", u, bank, 0), ("pS", u, bank, 1), ("scb", u)],
                             writes=[ekey])
                        if kt2 > 0:
                            pb = 1 - bank
                            for sub in range(2):
                                kt = 2 * (kt2 - 1) + sub
                                r.mmgroup([(lambda e, h=h, kt=kt, pb=pb, sub=sub: e.matmul(
                                    pOT[0][0:65, :], Vaug[:, kt, h * 65:(h + 1) * 65],
                                    ET[:, pb, sub * 512:(sub + 1) * 512],
                                    start=(kt == 0), stop=False))],
                                    reads=[("ET", u, pb), ("V", u), ("Vones", u)], writes=[ok])
                    for sub in range(2):
                        kt = 30 + sub
                        r.mmgroup([(lambda e, h=h, kt=kt, sub=sub: e.matmul(
                            pOT[0][0:65, :], Vaug[:, kt, h * 65:(h + 1) * 65],
                            ET[:, 1, sub * 512:(sub + 1) * 512],
                            start=False, stop=(sub == 1)))],
                            reads=[("ET", u, 1), ("V", u), ("Vones", u)], writes=[ok])
                    r.op("vector", lambda e, h2=h2: e.tensor_copy(oT[0:65, h2, :], pOT[0][0:65, :]),
                         reads=[ok], writes=[("oT", u, h2)])
                    for qs in range(4):
                        r.op("tensor", lambda e, h2=h2, qs=qs: e.transpose(
                            pt_f[:, 0:65], oT[0:65, h2, qs * 128:(qs + 1) * 128],
                            g["idsf"][0:65, 0:65]),
                             reads=[("oT", u, h2), "idsf"], writes=[("ptf", u)])
                        r.op("vector", lambda e, h2=h2: e.reciprocal(
                            rz[:, h2:h2 + 1], pt_f[:, 64:65]),
                             reads=[("ptf", u)], writes=[("rz", u, h2)])
                        r.op("vector", lambda e, h2=h2, qs=qs, h=h: e.tensor_scalar(
                            ao[:, qs, h * 64:(h + 1) * 64], pt_f[:, 0:64],
                            rz[:, h2:h2 + 1], None, ALU.mult),
                             reads=[("ptf", u), ("rz", u, h2)], writes=[("ao", u)])
                for qs in range(4):
                    for dc in range(8):
                        r.op("tensor", lambda e, qs=qs, dc=dc: e.transpose(
                            pt_b2[:], ao[:, qs, dc * 128:(dc + 1) * 128], g["idsb"][:]),
                             reads=[("ao", u), "idsb"], writes=[("ptb2", u)])
                        r.op("vector", lambda e, qs=qs, dc=dc: e.tensor_copy(
                            aostg[:, dc, qs * 128:(qs + 1) * 128], pt_b2[:]),
                             reads=[("ptb2", u)], writes=[("aostg", u)])
                r.dma("sync", W["aoTs"][:, :, qg * 512:(qg + 1) * 512].rearrange("c p n -> p c n"),
                      aostg[:], reads=[("aostg", u)], sem="d_ao%s" % su)
            r.barrier()
            r.emit(r.block)
        esA.close()

        with ExitStack() as e3:
            sb3 = lambda n, s, d: e3.enter_context(nc.sbuf_tensor(n + u, s, d))
            ps3 = lambda n, s, d: e3.enter_context(nc.psum_tensor(n + u, s, d))
            pG = ps3("pG", [128, 512], F32)
            pyE2 = [ps3("pyE%d" % i, [128, 1024], F32) for i in range(2)]
            pyE = pyE2[0]
            pyF = ps3("pyF", [128, 1024], F32)
            pt_b3 = ps3("ptb3", [128, 128], BF16)
            aoTg = sb3("aoTg", [128, 8, 2048], BF16)
            moe = sb3("moe", [128, 16, 1024], F32)
            ewb = sb3("ewb", [128, 2, 8, 1024], BF16)
            fwb = sb3("fwb", [128, 8, 1024], BF16)
            gwb = sb3("gwb", [128, 8, 8], BF16)
            gbias = sb3("gbias", [128, 8], F32)
            fbias = sb3("fbias", [128, 1024], F32)
            eb_sb = sb3("ebsb", [8, 1024], F32)
            cm = sb3("cm", [128, 16, 8], F32)
            cmT = sb3("cmT", [8, 2048], F32)
            gs = sb3("gs", [128, 48], F32)
            mb3 = sb3("mb3", [128, 1024], BF16)
            moeT = sb3("moeT", [128, 8, 128], BF16)
            xnb3 = sb3("xnb3", [128, 2, 1024], BF16)
            xt3 = sb3("xt3", [128, 2, 1024], F32)
            xo3 = sb3("xo3", [128, 2, 1024], F32)

            r.dma("gpsimd", gwb[:], W["gwT"].rearrange("(c p) n -> p c n", p=128),
                  writes=[("gwb", u)], sem="d_gw%s" % su)
            r.dma("gpsimd", fwb[:], W["fwT"].rearrange("(c p) n -> p c n", p=128),
                  writes=[("fwb", u)], sem="d_fw%s" % su)
            r.dma("sync", gbias[:], W["gb_bc"], writes=[("gbias", u)], sem="d_gb%s" % su)
            r.dma("sync", fbias[:], W["fb_bc"], writes=[("fbias", u)], sem="d_fb%s" % su)
            r.dma("sync", eb_sb[:], W["eb"], writes=[("ebsb", u)], sem="d_eb%s" % su)

            for G in range(2):
                t0 = G * 16
                r.dma("sync", aoTg[:],
                      W["aoTs"][:, :, G * 2048:(G + 1) * 2048].rearrange("c p n -> p c n"),
                      writes=[("aoTg", u)], sem="d_aog%s" % su)
                for tt in range(16):
                    r.mmgroup([
                        (lambda e, dc=dc, tt=tt: e.matmul(
                            pG[:, 0:8], aoTg[:, dc, tt * 128:(tt + 1) * 128], gwb[:, dc, :],
                            start=(dc == 0), stop=(dc == 7)))
                        for dc in range(8)],
                        reads=[("aoTg", u), ("gwb", u)], writes=[("pG", u)])
                    gk = ("gs", u)
                    r.op("vector", lambda e: e.tensor_add(gs[:, 0:8], pG[:, 0:8], gbias[:]),
                         reads=[("pG", u), ("gbias", u)], writes=[gk])
                    r.op("scalar", lambda e: e.activation(
                        gs[:, 8:16], gs[:, 0:8], AF.Exp, accum_out=gs[:, 16:17]),
                         reads=[gk], writes=[gk])
                    r.op("vector", lambda e: e.reciprocal(gs[:, 17:18], gs[:, 16:17]),
                         reads=[gk], writes=[gk])
                    r.op("vector", lambda e: e.tensor_scalar(
                        gs[:, 24:32], gs[:, 8:16], gs[:, 17:18], None, ALU.mult),
                         reads=[gk], writes=[gk])
                    r.op("vector", lambda e: e.reduce_max(gs[:, 18:19], gs[:, 24:32], axis=AX),
                         reads=[gk], writes=[gk])
                    r.op("vector", lambda e: e.tensor_scalar(
                        gs[:, 32:40], gs[:, 24:32], gs[:, 18:19], None, ALU.is_equal),
                         reads=[gk], writes=[gk])
                    r.op("vector", lambda e: e.scalar_tensor_tensor(
                        gs[:, 40:48], gs[:, 32:40], -1e30, gs[:, 24:32], ALU.mult, ALU.add),
                         reads=[gk], writes=[gk])
                    r.op("vector", lambda e: e.reduce_max(gs[:, 19:20], gs[:, 40:48], axis=AX),
                         reads=[gk], writes=[gk])
                    r.op("vector", lambda e: e.tensor_scalar(
                        gs[:, 32:40], gs[:, 24:32], gs[:, 19:20], None, ALU.is_ge),
                         reads=[gk], writes=[gk])
                    r.op("vector", lambda e, tt=tt: e.tensor_tensor(
                        cm[:, tt, :], gs[:, 24:32], gs[:, 32:40], ALU.mult),
                         reads=[gk], writes=[("cm", u)])
                    r.op("tensor", lambda e, tt=tt: e.transpose(
                        pG[0:8, 0:128], cm[:, tt, :], g["idsf"][:]),
                         reads=[("cm", u), "idsf"], writes=[("pG", u)])
                    r.op("vector", lambda e, tt=tt: e.tensor_copy(
                        cmT[:, tt * 128:(tt + 1) * 128], pG[0:8, 0:128]),
                         reads=[("pG", u)], writes=[("cmT", u)])
                for tt in range(16):
                    pp = tt % 2
                    pE = pyE2[pp]
                    for half in range(2):
                        r.mmgroup([(lambda e, tt=tt, half=half, pE=pE: e.matmul(
                            pE[:, half * 512:(half + 1) * 512], cmT[:, tt * 128:(tt + 1) * 128],
                            eb_sb[:, half * 512:(half + 1) * 512], start=True, stop=True))],
                            reads=[("cmT", u), ("ebsb", u)], writes=[("pyE", u, pp)])
                    r.op("vector", lambda e, tt=tt, pE=pE: e.tensor_copy(moe[:, tt, :], pE[:]),
                         reads=[("pyE", u, tt % 2)], writes=[("moe", u, tt)])
                for ei in range(NE):
                    wb_ = ei % 2
                    r.dma("gpsimd", ewb[:, wb_, :, :],
                          W["ewT"][ei].rearrange("(c p) n -> p c n", p=128),
                          writes=[("ewb", u, wb_)], sem="d_ew%d%s" % (wb_, su))
                    for tt in range(16):
                        pp = tt % 2
                        pE = pyE2[pp]
                        for half in range(2):
                            r.mmgroup([
                                (lambda e, dc=dc, tt=tt, half=half, wb_=wb_, pE=pE: e.matmul(
                                    pE[:, half * 512:(half + 1) * 512],
                                    aoTg[:, dc, tt * 128:(tt + 1) * 128],
                                    ewb[:, wb_, dc, half * 512:(half + 1) * 512],
                                    start=(dc == 0), stop=(dc == 7)))
                                for dc in range(8)],
                                reads=[("aoTg", u), ("ewb", u, wb_)], writes=[("pyE", u, pp)])
                        r.op("vector", lambda e, tt=tt, ei=ei, pE=pE: e.scalar_tensor_tensor(
                            moe[:, tt, :], pE[:], cm[:, tt:tt + 1, ei:ei + 1], moe[:, tt, :],
                            ALU.mult, ALU.add),
                             reads=[("pyE", u, tt % 2), ("cm", u)], writes=[("moe", u, tt)])
                for tt in range(16):
                    p = tt % 2
                    ta = t0 + tt
                    r.op("vector", lambda e, tt=tt: e.tensor_copy(mb3[:], moe[:, tt, :]),
                         reads=[("moe", u, tt)], writes=[("mb3", u)])
                    for dc in range(8):
                        r.op("tensor", lambda e, dc=dc: e.transpose(
                            pt_b3[:], mb3[:, dc * 128:(dc + 1) * 128], g["idsb"][:]),
                             reads=[("mb3", u), "idsb"], writes=[("ptb3", u)])
                        r.op("vector", lambda e, dc=dc: e.tensor_copy(moeT[:, dc, :], pt_b3[:]),
                             reads=[("ptb3", u)], writes=[("moeT", u)])
                    for half in range(2):
                        r.mmgroup([
                            (lambda e, dc=dc, half=half: e.matmul(
                                pyF[:, half * 512:(half + 1) * 512], moeT[:, dc, :],
                                fwb[:, dc, half * 512:(half + 1) * 512],
                                start=(dc == 0), stop=(dc == 7)))
                            for dc in range(8)],
                            reads=[("moeT", u), ("fwb", u)], writes=[("pyF", u)])
                    r.dma("sync", xnb3[:, p, :], W["xns"][ta * 128:(ta + 1) * 128, :],
                          writes=[("xnb3", u, p)], sem="d_xn3%d%s" % (p, su))
                    r.dma("sync", xt3[:, p, :], xin[ta * 128:(ta + 1) * 128, :],
                          writes=[("xt3", u, p)], sem="d_xt3%d%s" % (p, su))
                    r.op("vector", lambda e, p=p: e.tensor_add(xo3[:, p, :], pyF[:], fbias[:]),
                         reads=[("pyF", u), ("fbias", u)], writes=[("xo3", u, p)])
                    r.op("vector", lambda e, p=p: e.tensor_add(
                        xo3[:, p, :], xo3[:, p, :], xnb3[:, p, :]),
                         reads=[("xo3", u, p), ("xnb3", u, p)], writes=[("xo3", u, p)])
                    r.op("vector", lambda e, p=p: e.scalar_tensor_tensor(
                        xo3[:, p, :], xo3[:, p, :], 0.5, xt3[:, p, :], ALU.mult, ALU.add),
                         reads=[("xo3", u, p), ("xt3", u, p)], writes=[("xo3", u, p)])
                    r.dma("sync", xout[ta * 128:(ta + 1) * 128, :], xo3[:, p, :],
                          reads=[("xo3", u, p)], sem="d_of%d%s" % (p, su))
            r.barrier()
        r.emit(r.block)


def _decl_l1ffn_weights(nc, pref=""):
    return {
        "qkvwT": nc.dram_tensor(pref + "qkvwT", [D, 3 * D], F32, kind="ExternalInput").ap(),
        "bqkv_bc": nc.dram_tensor(pref + "bqkv_bc", [128, 3 * D], F32, kind="ExternalInput").ap(),
        "cosF": nc.dram_tensor(pref + "cosF", [128, 32, 64], F32, kind="ExternalInput").ap(),
        "sinF": nc.dram_tensor(pref + "sinF", [128, 32, 64], F32, kind="ExternalInput").ap(),
        "gwT": nc.dram_tensor(pref + "gwT", [D, 8], F32, kind="ExternalInput").ap(),
        "gb_bc": nc.dram_tensor(pref + "gb_bc", [128, 8], F32, kind="ExternalInput").ap(),
        "ewT": nc.dram_tensor(pref + "ewT", [NE, D, D], F32, kind="ExternalInput").ap(),
        "eb": nc.dram_tensor(pref + "eb", [NE, D], F32, kind="ExternalInput").ap(),
        "fwT": nc.dram_tensor(pref + "fwT", [D, D], F32, kind="ExternalInput").ap(),
        "fb_bc": nc.dram_tensor(pref + "fb_bc", [128, D], F32, kind="ExternalInput").ap(),
        "xns": nc.dram_tensor(pref + "xns", [T, D], BF16, kind="Internal").ap(),
        "xnTs": nc.dram_tensor(pref + "xnTs", [8, 128, T], BF16, kind="Internal").ap(),
        "qTs": nc.dram_tensor(pref + "qTs_f", [8, 128, T], BF16, kind="Internal").ap(),
        "aoTs": nc.dram_tensor(pref + "aoTs", [8, 128, T], BF16, kind="Internal").ap(),
    }


def _host_l1ffn_weights(inp, pref=""):
    qkvwT = np.ascontiguousarray(np.concatenate(
        [inp["L1_ffn_q_w"].T, inp["L1_ffn_k_w"].T, inp["L1_ffn_v_w"].T], axis=1))
    bqkv = np.concatenate([inp["L1_ffn_q_b"], inp["L1_ffn_k_b"], inp["L1_ffn_v_b"]])
    bqkv_bc = np.ascontiguousarray(np.broadcast_to(bqkv, (128, 3 * D)))
    inv = 1.0 / 10000.0 ** (np.arange(0, HD, 2, dtype=np.float32) / HD)
    freqs = np.outer(np.arange(T, dtype=np.float32), inv)
    emb = np.concatenate([freqs, freqs], axis=-1)
    cos = np.cos(emb).astype(np.float32).reshape(32, 128, 64).transpose(1, 0, 2)
    sin = np.sin(emb).astype(np.float32).reshape(32, 128, 64).transpose(1, 0, 2)
    return {
        pref + "qkvwT": qkvwT,
        pref + "bqkv_bc": bqkv_bc,
        pref + "cosF": np.ascontiguousarray(cos),
        pref + "sinF": np.ascontiguousarray(sin),
        pref + "gwT": np.ascontiguousarray(inp["L1_ffn_gate_w"].T),
        pref + "gb_bc": np.ascontiguousarray(np.broadcast_to(inp["L1_ffn_gate_b"], (128, 8))),
        pref + "ewT": np.ascontiguousarray(inp["L1_ffn_expert_w"].transpose(0, 2, 1)),
        pref + "eb": inp["L1_ffn_expert_b"],
        pref + "fwT": np.ascontiguousarray(inp["L1_ffn_ffn_w"].T),
        pref + "fb_bc": np.ascontiguousarray(np.broadcast_to(inp["L1_ffn_ffn_b"], (128, D))),
    }


def build_stageD():
    nc = bass.Bass("TRN2", target_bir_lowering=False)
    x = nc.dram_tensor("x", [T, D], F32, kind="ExternalInput").ap()
    out = nc.dram_tensor("out", [T, D], F32, kind="ExternalOutput").ap()
    W = _decl_l1ffn_weights(nc)
    with ExitStack() as es:
        r = Rec(nc, es)
        r.block = es.enter_context(nc.Block())
        g = _globals_setup(nc, es, r)
        emit_l1ffn(r, nc, g, x, out, W, "D")
        r.barrier()
        r.emit(r.block)
    return nc




def bcast16_g(ap2d, inner):
    return bass.AP(ap2d.tensor, ap2d.offset, [ap2d.ap[0], [0, 16], [1, inner]])


def emit_l1attn(r, nc, g, xin, xout, W, u):
    with ExitStack() as es:
        sb = lambda n, s, d: es.enter_context(nc.sbuf_tensor(n + u, s, d))
        scb = sb("scb", [128, 2], F32)
        r.op("vector", lambda e: e.memset(scb[:, 0:1], 0.125), writes=[("scb", u)])
        r.op("vector", lambda e: e.memset(scb[:, 1:2], -20.0), writes=[("scb", u)])
        kT = sb("kT", [128, 8, 4096], BF16)
        Vaug = sb("Vaug", [128, 32, 1040], BF16)
        Vv = Vaug[:].rearrange("p t (h c) -> p t h c", c=65)
        r.op("vector", lambda e: e.memset(Vv[:, :, :, 64:65], 1.0), writes=[("Vones", u)])

        with ExitStack() as e1:
            sb1 = lambda n, s, d: e1.enter_context(nc.sbuf_tensor(n + u, s, d))
            ps1 = lambda n, s, d: e1.enter_context(nc.psum_tensor(n + u, s, d))
            pt_b = ps1("ptb", [128, 128], BF16)
            xt = sb1("xt", [128, 2, 1024], F32)
            sqb = sb1("sq", [128, 1024], F32)
            xnt = sb1("xnt", [128, 2, 1024], BF16)
            stt = sb1("stt", [128, 2, 8], F32)
            xnTg = sb1("xnTg", [128, 8, 512], BF16)
            zrow = sb1("zrow", [1, 1024], BF16)
            r.op("vector", lambda e: e.memset(zrow[:], 0.0), writes=[("zrow", u)])
            r.dma("sync", W["vs"][0:1, :], zrow[:], reads=[("zrow", u)], sem="d_zr%s" % u)
            r.dma("sync", W["vs"][T + 1:T + 2, :], zrow[:], reads=[("zrow", u)], sem="d_zr%s" % u)
            for tg in range(8):
                for tt4 in range(4):
                    tt = tg * 4 + tt4
                    p = tt % 2
                    xkey = ("xt", u, p)
                    xtile = xt[:, p, :]
                    r.dma("sync", xtile, xin[tt * 128:(tt + 1) * 128, :], writes=[xkey],
                          sem="d_xt%d%s" % (p, u))
                    layernorm_tile(r, u, xtile, stt[:, p, :], sqb[:], ("stt", u, p), ("sq", u), xkey)
                    r.op("vector", lambda e, p=p: e.tensor_scalar(
                        xnt[:, p, :], xt[:, p, :], stt[:, p, 7:8], stt[:, p, 6:7], ALU.add, ALU.mult),
                         reads=[xkey, ("stt", u, p)], writes=[("xnt", u, p)])
                    for dc in range(8):
                        r.op("tensor", lambda e, p=p, dc=dc: e.transpose(
                            pt_b[:], xnt[:, p, dc * 128:(dc + 1) * 128], g["idsb"][:]),
                             reads=[("xnt", u, p), "idsb"], writes=[("ptb", u)])
                        r.op("vector", lambda e, dc=dc, tt4=tt4: e.tensor_copy(
                            xnTg[:, dc, tt4 * 128:(tt4 + 1) * 128], pt_b[:]),
                             reads=[("ptb", u)], writes=[("xnTg", u)])
                r.dma("sync", W["xnTs"][:, :, tg * 512:(tg + 1) * 512].rearrange("c p n -> p c n"),
                      xnTg[:], reads=[("xnTg", u)], sem="d_xnT%s" % u)
            r.barrier()
            r.emit(r.block)

        with ExitStack() as e1:
            sb1 = lambda n, s, d: e1.enter_context(nc.sbuf_tensor(n + u, s, d))
            ps1 = lambda n, s, d: e1.enter_context(nc.psum_tensor(n + u, s, d))
            pt_b = ps1("ptbb", [128, 128], BF16)
            pP = [ps1("pP%d" % i, [128, 512], F32) for i in range(2)]
            wpr = sb1("wpr", [128, 8, 1024], BF16)
            xnTg = sb1("xnTg2", [128, 2, 8, 512], BF16)
            bias_bc = sb1("bbc", [128, 1024], F32)
            tabs = sb1("tabs", [128, 2, 2, 32], F32)
            work = sb1("work", [128, 1024], F32)
            elu1 = sb1("elu1", [128, 1024], F32)
            tmp1 = sb1("tmp1", [128, 512], F32)
            tmp2 = sb1("tmp2", [128, 512], F32)
            rro = sb1("rro", [128, 1024], BF16)
            qstg = sb1("qstg", [128, 8, 512], BF16)
            vtile = sb1("vtile", [128, 2, 1024], BF16)

            for ip, pr in enumerate(["q", "k", "v"]):
                r.dma("gpsimd", wpr[:],
                      W["qkvwT"][:, ip * 1024:(ip + 1) * 1024].rearrange("(c p) n -> p c n", p=128),
                      writes=[("wpr", u)], sem="d_wp%s" % u)
                r.dma("sync", bias_bc[:], W["bqkv_bc"][:, ip * 1024:(ip + 1) * 1024],
                      writes=[("bbc", u)], sem="d_bbc%s" % u)
                for tg in range(8):
                    gp = tg % 2
                    r.dma("sync", xnTg[:, gp, :, :],
                          W["xnTs"][:, :, tg * 512:(tg + 1) * 512].rearrange("c p n -> p c n"),
                          writes=[("xnTg2", u, gp)], sem="d_xg%d%s" % (gp, u))
                    for tt4 in range(4):
                        tt = tg * 4 + tt4
                        if pr != "v":
                            r.dma("sync", tabs[:, tt % 2, 0, :], W["cosA"][:, tt, :],
                                  writes=[("tabs", u, tt % 2)], sem="d_tc%d%s" % (tt % 2, u))
                            r.dma("sync", tabs[:, tt % 2, 1, :], W["sinA"][:, tt, :],
                                  writes=[("tabs", u, tt % 2)], sem="d_ts%d%s" % (tt % 2, u))
                        for half in range(2):
                            bk = (tt4 * 2 + half) % 2
                            pk = ("pP%d" % bk, u)
                            r.mmgroup([
                                (lambda e, dc=dc, tt4=tt4, half=half, gp=gp, bk=bk: e.matmul(
                                    pP[bk][:], xnTg[:, gp, dc, tt4 * 128:(tt4 + 1) * 128],
                                    wpr[:, dc, half * 512:(half + 1) * 512],
                                    start=(dc == 0), stop=(dc == 7)))
                                for dc in range(8)],
                                reads=[("xnTg2", u, gp), ("wpr", u)], writes=[pk])
                            if pr == "v":
                                pv = pP[bk][:].rearrange("p (h c) -> p h c", c=64)
                                bv = bias_bc[:, half * 512:(half + 1) * 512].rearrange(
                                    "p (h c) -> p h c", c=64)
                                r.op("vector", lambda e, tt=tt, half=half, pv=pv, bv=bv: e.tensor_tensor(
                                    Vv[:, tt, half * 8:(half + 1) * 8, 0:64], pv, bv, ALU.add),
                                     reads=[pk, ("bbc", u)], writes=[("V", u)])
                                r.op("vector", lambda e, tt=tt, half=half, bk=bk, p=tt % 2: e.tensor_tensor(
                                    vtile[:, p, half * 512:(half + 1) * 512], pP[bk][:],
                                    bias_bc[:, half * 512:(half + 1) * 512], ALU.add),
                                     reads=[pk, ("bbc", u)], writes=[("vtile", u, tt % 2)])
                            else:
                                r.op("vector", lambda e, half=half, bk=bk: e.tensor_tensor(
                                    work[:, half * 512:(half + 1) * 512], pP[bk][:],
                                    bias_bc[:, half * 512:(half + 1) * 512], ALU.add),
                                     reads=[pk, ("bbc", u)], writes=[("work", u)])
                        if pr == "v":
                            p = tt % 2
                            r.dma("sync", W["vs"][1 + tt * 128:1 + (tt + 1) * 128, :],
                                  vtile[:, p, :], reads=[("vtile", u, p)],
                                  sem="d_vs%d%s" % (p, u))
                        else:
                            r.op("vector", lambda e: e.tensor_scalar_min(elu1[:], work[:], 0.0),
                                 reads=[("work", u)], writes=[("elu1", u)])
                            r.op("scalar", lambda e: e.activation(elu1[:], elu1[:], AF.Exp),
                                 reads=[("elu1", u)], writes=[("elu1", u)])
                            r.op("vector", lambda e: e.tensor_scalar_max(work[:], work[:], 0.0),
                                 reads=[("work", u)], writes=[("work", u)])
                            r.op("vector", lambda e: e.tensor_add(work[:], work[:], elu1[:]),
                                 reads=[("work", u), ("elu1", u)], writes=[("work", u)])
                            tp = tt % 2
                            wv = work[:].rearrange("p (h c two) -> p h c two", two=2, c=32)
                            rv = rro[:].rearrange("p (h c two) -> p h c two", two=2, c=32)
                            t1v = tmp1[:].rearrange("p (h c) -> p h c", c=32)
                            t2v = tmp2[:].rearrange("p (h c) -> p h c", c=32)
                            cb = bcast16_g(tabs[:, tp, 0, :], 32)
                            sb_ = bcast16_g(tabs[:, tp, 1, :], 32)
                            tk = ("tabs", u, tp)
                            r.op("vector", lambda e, cb=cb, wv=wv, t1v=t1v: e.tensor_tensor(
                                t1v, wv[:, :, :, 0], cb, ALU.mult),
                                 reads=[("work", u), tk], writes=[("tmp1", u)])
                            r.op("vector", lambda e, sb_=sb_, wv=wv, t2v=t2v: e.tensor_tensor(
                                t2v, wv[:, :, :, 1], sb_, ALU.mult),
                                 reads=[("work", u), tk], writes=[("tmp2", u)])
                            r.op("vector", lambda e, rv=rv, t1v=t1v, t2v=t2v: e.tensor_tensor(
                                rv[:, :, :, 0], t1v, t2v, ALU.subtract),
                                 reads=[("tmp1", u), ("tmp2", u)], writes=[("rro", u)])
                            r.op("vector", lambda e, sb_=sb_, wv=wv, t1v=t1v: e.tensor_tensor(
                                t1v, wv[:, :, :, 0], sb_, ALU.mult),
                                 reads=[("work", u), tk], writes=[("tmp1", u)])
                            r.op("vector", lambda e, cb=cb, wv=wv, t2v=t2v: e.tensor_tensor(
                                t2v, wv[:, :, :, 1], cb, ALU.mult),
                                 reads=[("work", u), tk], writes=[("tmp2", u)])
                            r.op("vector", lambda e, rv=rv, t1v=t1v, t2v=t2v: e.tensor_tensor(
                                rv[:, :, :, 1], t1v, t2v, ALU.add),
                                 reads=[("tmp1", u), ("tmp2", u)], writes=[("rro", u)])
                            for dc in range(8):
                                r.op("tensor", lambda e, dc=dc: e.transpose(
                                    pt_b[:], rro[:, dc * 128:(dc + 1) * 128], g["idsb"][:]),
                                     reads=[("rro", u), "idsb"], writes=[("ptb", u)])
                                if pr == "q":
                                    r.op("vector", lambda e, dc=dc, tt4=tt4: e.tensor_copy(
                                        qstg[:, dc, tt4 * 128:(tt4 + 1) * 128], pt_b[:]),
                                         reads=[("ptb", u)], writes=[("qstg", u)])
                                else:
                                    r.op("vector", lambda e, dc=dc, tt=tt: e.tensor_copy(
                                        kT[:, dc, tt * 128:(tt + 1) * 128], pt_b[:]),
                                         reads=[("ptb", u)], writes=[("kT", u)])
                    if pr == "q":
                        r.dma("sync", W["qTs"][:, :, tg * 512:(tg + 1) * 512].rearrange("c p n -> p c n"),
                              qstg[:], reads=[("qstg", u)], sem="d_qs%s" % u)
            r.barrier()
            r.emit(r.block)

        with ExitStack() as e2:
            sb2 = lambda n, s, d: e2.enter_context(nc.sbuf_tensor(n + u, s, d))
            ps2 = lambda n, s, d: e2.enter_context(nc.psum_tensor(n + u, s, d))
            pS = [ps2("pS%d" % i, [128, 1024], F32) for i in range(2)]
            pOT = [ps2("pOT%d" % i, [128, 512], F32) for i in range(1)]
            pt_f = ps2("ptf", [128, 128], F32)
            qTg = sb2("qTg", [128, 1, 8, 512], BF16)
            ET = sb2("ET", [128, 2, 1024], BF16)
            oT = sb2("oT", [128, 2, 512], F32)
            rz = sb2("rz", [128, 2], F32)
            ao = sb2("ao", [128, 4, 1024], BF16)
            vle = sb2("vle", [128, 2, 3, 1024], BF16)
            wl = sb2("wl", [128, 3, 1024], BF16)
            bl = sb2("bl", [128, 1024], BF16)
            lep = sb2("lep", [128, 2, 1024], F32)
            xt2 = sb2("xt2", [128, 2, 1024], F32)
            xo2 = sb2("xo2", [128, 2, 1024], F32)

            r.dma("sync", wl[:], W["wl_bc"], writes=[("wl", u)], sem="d_wl%s" % u)
            r.dma("sync", bl[:], W["bl_bc"], writes=[("bl", u)], sem="d_bl%s" % u)

            for qg in range(8):
                gp = 0
                r.dma("sync", qTg[:, 0, :, :],
                      W["qTs"][:, :, qg * 512:(qg + 1) * 512].rearrange("c p n -> p c n"),
                      writes=[("qTg", u, gp)], sem="d_qg%s" % u)
                for h in range(16):
                    hp = h // 2
                    h2 = h % 2
                    base = 64 * h2
                    ok = ("pOT0", u)
                    for kt2 in range(16):
                        bank = kt2 % 2
                        for sub in range(2):
                            kt = 2 * kt2 + sub
                            r.mmgroup([(lambda e, hp=hp, kt=kt, bank=bank, base=base, sub=sub: e.matmul(
                                pS[bank][:, sub * 512:(sub + 1) * 512],
                                kT[base:base + 64, hp, kt * 128:(kt + 1) * 128],
                                qTg[base:base + 64, 0, hp, :], start=True, stop=True))],
                                reads=[("qTg", u, gp), ("kT", u)], writes=[("pS", u, bank, sub)])
                        ekey = ("ET", u, bank)
                        r.op("scalar", lambda e, bank=bank: e.activation(
                            ET[:, bank, :], pS[bank][:], AF.Exp, scale=scb[:, 0:1],
                            bias=scb[:, 1:2]),
                             reads=[("pS", u, bank, 0), ("pS", u, bank, 1), ("scb", u)],
                             writes=[ekey])
                        if kt2 > 0:
                            pb = 1 - bank
                            for sub in range(2):
                                kt = 2 * (kt2 - 1) + sub
                                r.mmgroup([(lambda e, h=h, kt=kt, pb=pb, sub=sub: e.matmul(
                                    pOT[0][0:65, :], Vaug[:, kt, h * 65:(h + 1) * 65],
                                    ET[:, pb, sub * 512:(sub + 1) * 512],
                                    start=(kt == 0), stop=False))],
                                    reads=[("ET", u, pb), ("V", u), ("Vones", u)], writes=[ok])
                    for sub in range(2):
                        kt = 30 + sub
                        r.mmgroup([(lambda e, h=h, kt=kt, sub=sub: e.matmul(
                            pOT[0][0:65, :], Vaug[:, kt, h * 65:(h + 1) * 65],
                            ET[:, 1, sub * 512:(sub + 1) * 512],
                            start=False, stop=(sub == 1)))],
                            reads=[("ET", u, 1), ("V", u), ("Vones", u)], writes=[ok])
                    r.op("vector", lambda e, h2=h2: e.tensor_copy(oT[0:65, h2, :], pOT[0][0:65, :]),
                         reads=[ok], writes=[("oT", u, h2)])
                    for qs in range(4):
                        r.op("tensor", lambda e, h2=h2, qs=qs: e.transpose(
                            pt_f[:, 0:65], oT[0:65, h2, qs * 128:(qs + 1) * 128],
                            g["idsf"][0:65, 0:65]),
                             reads=[("oT", u, h2), "idsf"], writes=[("ptf", u)])
                        r.op("vector", lambda e, h2=h2: e.reciprocal(
                            rz[:, h2:h2 + 1], pt_f[:, 64:65]),
                             reads=[("ptf", u)], writes=[("rz", u, h2)])
                        r.op("vector", lambda e, h2=h2, qs=qs, h=h: e.tensor_scalar(
                            ao[:, qs, h * 64:(h + 1) * 64], pt_f[:, 0:64],
                            rz[:, h2:h2 + 1], None, ALU.mult),
                             reads=[("ptf", u), ("rz", u, h2)], writes=[("ao", u)])
                for qs in range(4):
                    tt = qg * 4 + qs
                    p = tt % 2
                    lk = ("vle", u, p)
                    r.dma("sync", vle[:, p, 0, :], W["vs"][tt * 128:(tt + 1) * 128, :],
                          writes=[lk], sem="d_vm%d%s" % (p, u))
                    r.dma("sync", vle[:, p, 1, :], W["vs"][1 + tt * 128:1 + (tt + 1) * 128, :],
                          writes=[lk], sem="d_v0%d%s" % (p, u))
                    r.dma("sync", vle[:, p, 2, :], W["vs"][2 + tt * 128:2 + (tt + 1) * 128, :],
                          writes=[lk], sem="d_vp%d%s" % (p, u))
                    r.dma("sync", xt2[:, p, :], xin[tt * 128:(tt + 1) * 128, :],
                          writes=[("xt2", u, p)], sem="d_x2%d%s" % (p, u))
                    r.op("vector", lambda e, p=p: e.tensor_tensor(
                        lep[:, p, :], vle[:, p, 0, :], wl[:, 0, :], ALU.mult),
                         reads=[lk, ("wl", u)], writes=[("lep", u, p)])
                    r.op("vector", lambda e, p=p: e.tensor_tensor(
                        xo2[:, p, :], vle[:, p, 1, :], wl[:, 1, :], ALU.mult),
                         reads=[lk, ("wl", u)], writes=[("xo2", u, p)])
                    r.op("vector", lambda e, p=p: e.tensor_add(
                        lep[:, p, :], lep[:, p, :], xo2[:, p, :]),
                         reads=[("lep", u, p), ("xo2", u, p)], writes=[("lep", u, p)])
                    r.op("vector", lambda e, p=p: e.tensor_tensor(
                        xo2[:, p, :], vle[:, p, 2, :], wl[:, 2, :], ALU.mult),
                         reads=[lk, ("wl", u)], writes=[("xo2", u, p)])
                    r.op("vector", lambda e, p=p: e.tensor_add(
                        lep[:, p, :], lep[:, p, :], xo2[:, p, :]),
                         reads=[("lep", u, p), ("xo2", u, p)], writes=[("lep", u, p)])
                    r.op("vector", lambda e, p=p: e.tensor_add(
                        lep[:, p, :], lep[:, p, :], bl[:]),
                         reads=[("lep", u, p), ("bl", u)], writes=[("lep", u, p)])
                    r.op("vector", lambda e, p=p, qs=qs: e.tensor_add(
                        lep[:, p, :], lep[:, p, :], ao[:, qs, :]),
                         reads=[("lep", u, p), ("ao", u)], writes=[("lep", u, p)])
                    r.op("vector", lambda e, p=p: e.tensor_add(
                        xo2[:, p, :], lep[:, p, :], xt2[:, p, :]),
                         reads=[("lep", u, p), ("xt2", u, p)], writes=[("xo2", u, p)])
                    r.dma("sync", xout[tt * 128:(tt + 1) * 128, :], xo2[:, p, :],
                          reads=[("xo2", u, p)], sem="d_oe%d%s" % (p, u))
            r.barrier()
            r.emit(r.block)


def _decl_l1attn_weights(nc, pref=""):
    return {
        "qkvwT": nc.dram_tensor(pref + "aqkvwT", [D, 3 * D], F32, kind="ExternalInput").ap(),
        "bqkv_bc": nc.dram_tensor(pref + "abqkv_bc", [128, 3 * D], F32, kind="ExternalInput").ap(),
        "cosA": nc.dram_tensor(pref + "cosA", [128, 32, 32], F32, kind="ExternalInput").ap(),
        "sinA": nc.dram_tensor(pref + "sinA", [128, 32, 32], F32, kind="ExternalInput").ap(),
        "wl_bc": nc.dram_tensor(pref + "wl_bc", [128, 3, D], BF16, kind="ExternalInput").ap(),
        "bl_bc": nc.dram_tensor(pref + "bl_bc", [128, D], BF16, kind="ExternalInput").ap(),
        "xnTs": nc.dram_tensor(pref + "xnTs_e", [8, 128, T], BF16, kind="Internal").ap(),
        "qTs": nc.dram_tensor(pref + "qTs_e", [8, 128, T], BF16, kind="Internal").ap(),
        "vs": nc.dram_tensor(pref + "vs_e", [T + 2, D], BF16, kind="Internal").ap(),
    }


def _host_l1attn_weights(inp, pref=""):
    w = inp["L1_attn_w_qkv"]
    b = inp["L1_attn_b_qkv"]
    qkvwT = np.ascontiguousarray(w.T)
    bqkv_bc = np.ascontiguousarray(np.broadcast_to(b, (128, 3 * D)))
    pos = np.arange(T, dtype=np.float32)[:, None]
    div = np.exp(np.arange(0, HD, 2, dtype=np.float32) * (-np.log(10000.0) / HD))
    ang = (pos * div).astype(np.float32)
    cos = np.cos(ang).astype(np.float32).reshape(32, 128, 32).transpose(1, 0, 2)
    sin = np.sin(ang).astype(np.float32).reshape(32, 128, 32).transpose(1, 0, 2)
    wle = inp["L1_attn_w_lepe"].reshape(D, 3)
    wl_bc = np.ascontiguousarray(np.broadcast_to(wle.T[:, None, :], (3, 128, D)).transpose(1, 0, 2))
    bl_bc = np.ascontiguousarray(np.broadcast_to(inp["L1_attn_b_lepe"], (128, D)))
    return {
        pref + "aqkvwT": qkvwT,
        pref + "abqkv_bc": bqkv_bc,
        pref + "cosA": np.ascontiguousarray(cos),
        pref + "sinA": np.ascontiguousarray(sin),
        pref + "wl_bc": wl_bc.astype(BF),
        pref + "bl_bc": bl_bc.astype(BF),
    }


def build_stageE():
    nc = bass.Bass("TRN2", target_bir_lowering=False)
    x = nc.dram_tensor("x", [T, D], F32, kind="ExternalInput").ap()
    out = nc.dram_tensor("out", [T, D], F32, kind="ExternalOutput").ap()
    W = _decl_l1attn_weights(nc)
    with ExitStack() as es:
        r = Rec(nc, es)
        r.block = es.enter_context(nc.Block())
        g = _globals_setup(nc, es, r)
        emit_l1attn(r, nc, g, x, out, W, "E")
        r.barrier()
        r.emit(r.block)
    return nc


def build_fused():
    nc = bass.Bass("TRN2", target_bir_lowering=False)
    x = nc.dram_tensor("x", [T, D], F32, kind="ExternalInput").ap()
    out = nc.dram_tensor("out", [T, D], F32, kind="ExternalOutput").ap()
    xb0 = nc.dram_tensor("xb0", [T, D], F32, kind="Internal").ap()
    xb1 = nc.dram_tensor("xb1", [T, D], F32, kind="Internal").ap()
    WA = _decl_l0ffn_weights(nc)
    WB = _decl_l0attn_weights(nc)
    WD = _decl_l1ffn_weights(nc)
    WE = _decl_l1attn_weights(nc)
    with ExitStack() as es:
        r = Rec(nc, es)
        r.block = es.enter_context(nc.Block())
        g = _globals_setup(nc, es, r)
        emit_l0ffn(r, nc, g, x, xb0, WA, "A")
        emit_l0attn(r, nc, g, xb0, xb1, WB, "B")
        emit_l0ffn(r, nc, g, xb1, xb0, WA, "C", su="A")
        emit_l1ffn(r, nc, g, xb0, xb1, WD, "D")
        emit_l1attn(r, nc, g, xb1, xb0, WE, "E")
        emit_l1ffn(r, nc, g, xb0, out, WD, "F", su="D")
        r.barrier()
        r.emit(r.block)
    return nc


def kernel(**inputs):
    inputs = {k: np.asarray(v) for k, v in inputs.items()}
    wm = _host_globals()
    wm.update(_host_l0ffn_weights(inputs))
    wm.update(_host_l0attn_weights(inputs))
    wm.update(_host_l1ffn_weights(inputs))
    wm.update(_host_l1attn_weights(inputs))
    gum = _host_gumbel()
    nc = build_fused()
    in_maps = []
    for c in range(CORES):
        im = dict(wm)
        im["x"] = np.ascontiguousarray(inputs["L0_attn_hidden_states"][c])
        im["gum"] = gum[c]
        in_maps.append(im)
    res = run_bass_kernel_spmd(nc, in_maps, core_ids=list(range(CORES)))
    return np.stack([np.asarray(res.results[c]["out"]) for c in range(CORES)]).astype(np.float32)

# version 11
# speedup vs baseline: 1.0797x; 1.0797x over previous
import numpy as np
import ml_dtypes
from contextlib import ExitStack

import concourse.bass as bass
import concourse.mybir as mybir
from concourse.bass_utils import run_bass_kernel_spmd

F32 = mybir.dt.float32
BF16 = mybir.dt.bfloat16
AF = mybir.ActivationFunctionType
ALU = mybir.AluOpType
AX = mybir.AxisListType.X
BF = ml_dtypes.bfloat16

CORES = 8
T = 4096
D = 1024
NH, HD = 16, 64
NE = 8
DHID = 4096
NEG = -1.0e9

ENGS = ("tensor", "vector", "scalar", "gpsimd", "sync")


class Rec:

    def __init__(self, nc, es, block=None):
        self.nc = nc
        self.es = es
        self.block = block
        self.sems = {}
        self.cnt = {}
        self.ev = {e: [] for e in ENGS}
        self.waited = {e: {} for e in ENGS}
        self.lw = {}
        self.lr = {}
        for e in ENGS:
            self._mksem("q_" + e)

    def _mksem(self, name):
        self.sems[name] = self.es.enter_context(self.nc.semaphore(name))
        self.cnt[name] = 0

    def sem(self, name):
        if name not in self.sems:
            self._mksem(name)
        return name

    def _wait(self, eng, tok):
        s, v = tok
        if self.waited[eng].get(s, 0) >= v:
            return
        self.waited[eng][s] = v
        h = self.sems[s]
        self.ev[eng].append(lambda e, h=h, v=v: e.wait_ge(h, v))

    def _deps(self, eng, reads, writes):
        toks = []
        for b in reads:
            t = self.lw.get(b)
            if t is not None:
                toks.append(t)
        for b in writes:
            t = self.lw.get(b)
            if t is not None:
                toks.append(t)
            toks.extend(self.lr.get(b, {}).items())
        for t in sorted(set(toks)):
            self._wait(eng, t)

    def _register(self, tok, reads, writes):
        if tok is None:
            return
        s, v = tok
        for b in reads:
            d = self.lr.setdefault(b, {})
            d[s] = max(d.get(s, 0), v)
        for b in writes:
            self.lw[b] = tok
            self.lr[b] = {}

    def op(self, eng, fn, reads=(), writes=(), sig=True):
        self._deps(eng, reads, writes)
        if sig:
            s = "q_" + eng
            self.cnt[s] += 1
            v = self.cnt[s]
            h = self.sems[s]
            self.ev[eng].append(lambda e, fn=fn, h=h: fn(e).then_inc(h, 1))
            tok = (s, v)
        else:
            self.ev[eng].append(lambda e, fn=fn: fn(e))
            tok = None
        self._register(tok, reads, writes)
        return tok

    def mmgroup(self, fns, reads=(), writes=()):
        self._deps("tensor", reads, writes)
        for fn in fns[:-1]:
            self.ev["tensor"].append(lambda e, fn=fn: fn(e))
        s = "q_tensor"
        self.cnt[s] += 1
        v = self.cnt[s]
        h = self.sems[s]
        fnl = fns[-1]
        self.ev["tensor"].append(lambda e, fn=fnl, h=h: fn(e).then_inc(h, 1))
        tok = (s, v)
        self._register(tok, reads, writes)
        return tok

    def dma(self, q, out, in_, reads=(), writes=(), sem=None, **kw):
        if sem is None:
            key = writes[0] if writes else (reads[0] if reads else "misc")
            sem = "d_" + str(key)
        s = self.sem(sem)
        self._deps(q, reads, writes)
        self.cnt[s] += 16
        v = self.cnt[s]
        h = self.sems[s]
        self.ev[q].append(
            lambda e, out=out, in_=in_, h=h, kw=kw: e.dma_start(out, in_, **kw).then_inc(h, 16))
        tok = (s, v)
        self._register(tok, reads, writes)
        return tok

    def barrier(self):
        toks = [(s, c) for s, c in self.cnt.items() if c > 0]
        for e in ENGS:
            for t in toks:
                self._wait(e, t)

    def emit(self, block):
        for e in ENGS:
            if self.ev[e]:
                evs = self.ev[e]
                self.ev[e] = []

                def section(eng, evs=evs):
                    for f in evs:
                        f(eng)

                getattr(block, e)(section)


def layernorm_tile(r, u, xtile, stt, sqrow, sk, sq_key, xt_key):
    r.op("vector", lambda e: e.reduce_sum(stt[:, 0:1], xtile, axis=AX), reads=[xt_key], writes=[sk])
    r.op("scalar", lambda e: e.activation(sqrow, xtile, AF.Square), reads=[xt_key], writes=[sq_key])
    r.op("vector", lambda e: e.reduce_sum(stt[:, 1:2], sqrow, axis=AX), reads=[sq_key], writes=[sk])
    r.op("vector", lambda e: e.tensor_scalar_mul(stt[:, 2:3], stt[:, 0:1], 1.0 / D), reads=[sk], writes=[sk])
    r.op("vector", lambda e: e.tensor_scalar_mul(stt[:, 3:4], stt[:, 1:2], 1.0 / D), reads=[sk], writes=[sk])
    r.op("vector", lambda e: e.tensor_tensor(stt[:, 4:5], stt[:, 2:3], stt[:, 2:3], ALU.mult), reads=[sk], writes=[sk])
    r.op("vector", lambda e: e.tensor_sub(stt[:, 4:5], stt[:, 3:4], stt[:, 4:5]), reads=[sk], writes=[sk])
    r.op("vector", lambda e: e.tensor_scalar_add(stt[:, 4:5], stt[:, 4:5], 1e-5), reads=[sk], writes=[sk])
    r.op("scalar", lambda e: e.activation(stt[:, 5:6], stt[:, 4:5], AF.Sqrt), reads=[sk], writes=[sk])
    r.op("vector", lambda e: e.reciprocal(stt[:, 6:7], stt[:, 5:6]), reads=[sk], writes=[sk])
    r.op("vector", lambda e: e.tensor_scalar_mul(stt[:, 7:8], stt[:, 2:3], -1.0), reads=[sk], writes=[sk])


def emit_l0ffn(r, nc, g, xin, xout, W, u, nexperts=NE, su=None):
    su = su or u
    with ExitStack() as es:
        sb = lambda n, s, d: es.enter_context(nc.sbuf_tensor(n + u, s, d))
        ps = lambda n, s, d: es.enter_context(nc.psum_tensor(n + u, s, d))

        xnT = sb("xnT", [128, 8, 2048], BF16)
        y_acc = sb("yacc", [128, 16, 1024], F32)
        c_g = sb("cg", [128, 16, 8], F32)
        cT = sb("cT", [8, 2048], F32)
        w0b = sb("w0b", [128, 2, 8, 512], BF16)
        w1b = sb("w1b", [128, 2, 8, 512], BF16)
        w2b = sb("w2b", [128, 2, 4, 1024], BF16)
        wg_b = sb("wgb", [128, 8, 8], BF16)
        bg_bc = sb("bgbc", [128, 8], F32)
        b0_pp = sb("b0pp", [128, 8, 32], F32)
        b1_pp = sb("b1pp", [128, 8, 32], F32)
        b2_sb = sb("b2sb", [8, 1024], F32)
        xt = sb("xt", [128, 2, 1024], F32)
        sqb = sb("sq", [128, 2, 1024], F32)
        xnt = sb("xnt", [128, 2, 1024], BF16)
        stt = sb("stt", [128, 2, 8], F32)
        s_sb = sb("ssb", [128, 2, 512], F32)
        pTb = sb("pTb", [128, 2, 4, 512], BF16)
        xo = sb("xo", [128, 2, 1024], F32)
        gt = sb("gt", [128, 2, 8], F32)

        pt_b = ps("ptb", [128, 128], BF16)
        ph = [ps("ph%d" % i, [128, 512], F32) for i in range(2)]
        pg = [ps("pg%d" % i, [128, 512], F32) for i in range(2)]
        py = ps("py", [128, 1024], F32)

        r.dma("gpsimd", wg_b[:], W["Wg"].rearrange("(c p) n -> p c n", p=128),
              writes=[("wgb", u)], sem="d_cwg" + su)
        r.dma("sync", bg_bc[:], W["bg_bc"], writes=[("bgbc", u)], sem="d_cbg" + su)
        r.dma("sync", b0_pp[:], W["b0_pp"], writes=[("b0pp", u)], sem="d_cb0" + su)
        r.dma("sync", b1_pp[:], W["b1_pp"], writes=[("b1pp", u)], sem="d_cb1" + su)
        r.dma("sync", b2_sb[:], W["b2"], writes=[("b2sb", u)], sem="d_cb2" + su)

        for G in range(2):
            t0 = G * 16
            KX = ("xnT", u)
            for tt in range(16):
                p = tt % 2
                xkey = ("xt", u, p)
                xtile = xt[:, p, :]
                r.dma("sync", xtile, xin[(t0 + tt) * 128:(t0 + tt + 1) * 128, :],
                      writes=[xkey], sem="d_xt%d%s" % (p, su))
                layernorm_tile(r, u, xtile, stt[:, p, :], sqb[:, p, :], ("stt", u, p), ("sq", u, p), xkey)
                sk = ("stt", u, p)
                r.op("vector", lambda e, p=p: e.tensor_scalar(
                    xnt[:, p, :], xt[:, p, :], stt[:, p, 7:8], stt[:, p, 6:7], ALU.add, ALU.mult),
                     reads=[xkey, sk], writes=[("xnt", u, p)])
                for dc in range(8):
                    r.op("tensor", lambda e, p=p, dc=dc: e.transpose(
                        pt_b[:], xnt[:, p, dc * 128:(dc + 1) * 128], g["idsb"][:]),
                         reads=[("xnt", u, p), "idsb"], writes=[("ptb", u)])
                    r.op("vector", lambda e, dc=dc, tt=tt: e.tensor_copy(
                        xnT[:, dc, tt * 128:(tt + 1) * 128], pt_b[:]),
                         reads=[("ptb", u)], writes=[KX])

            for tt in range(16):
                p = tt % 2
                gk = ("gt", u, p)
                sk = ("stt", u, p)
                qk = ("sq", u, p)
                r.mmgroup([
                    (lambda e, dc=dc, tt=tt: e.matmul(
                        ph[0][:, 0:8], xnT[:, dc, tt * 128:(tt + 1) * 128], wg_b[:, dc, :],
                        start=(dc == 0), stop=(dc == 7)))
                    for dc in range(8)],
                    reads=[KX, ("wgb", u)], writes=[("ph0", u)])
                r.op("vector", lambda e, p=p: e.tensor_add(gt[:, p, :], ph[0][:, 0:8], bg_bc[:]),
                     reads=[("ph0", u), ("bgbc", u)], writes=[gk])
                r.op("vector", lambda e, p=p: e.reduce_max(stt[:, p, 0:1], gt[:, p, :], axis=AX),
                     reads=[gk], writes=[sk])
                r.op("vector", lambda e, p=p: e.tensor_scalar(
                    sqb[:, p, 0:8], gt[:, p, :], stt[:, p, 0:1], None, ALU.is_equal),
                     reads=[gk, sk], writes=[qk])
                r.op("vector", lambda e, p=p: e.scalar_tensor_tensor(
                    sqb[:, p, 8:16], sqb[:, p, 0:8], -1e30, gt[:, p, :], ALU.mult, ALU.add),
                     reads=[qk, gk], writes=[qk])
                r.op("vector", lambda e, p=p: e.reduce_max(stt[:, p, 1:2], sqb[:, p, 8:16], axis=AX),
                     reads=[qk], writes=[sk])
                r.op("vector", lambda e, p=p: e.tensor_scalar(
                    sqb[:, p, 16:24], sqb[:, p, 8:16], stt[:, p, 1:2], None, ALU.is_equal),
                     reads=[qk, sk], writes=[qk])
                r.op("vector", lambda e, p=p: e.tensor_scalar_mul(stt[:, p, 2:3], stt[:, p, 0:1], -1.0),
                     reads=[sk], writes=[sk])
                r.op("scalar", lambda e, p=p: e.activation(
                    stt[:, p, 3:4], stt[:, p, 1:2], AF.Exp, bias=stt[:, p, 2:3]),
                     reads=[sk], writes=[sk])
                r.op("vector", lambda e, p=p: e.tensor_scalar_add(stt[:, p, 4:5], stt[:, p, 3:4], 1.0),
                     reads=[sk], writes=[sk])
                r.op("vector", lambda e, p=p: e.reciprocal(stt[:, p, 4:5], stt[:, p, 4:5]),
                     reads=[sk], writes=[sk])
                r.op("vector", lambda e, p=p: e.tensor_scalar_mul(stt[:, p, 5:6], stt[:, p, 4:5], 0.5),
                     reads=[sk], writes=[sk])
                r.op("vector", lambda e, p=p: e.tensor_tensor(
                    stt[:, p, 6:7], stt[:, p, 3:4], stt[:, p, 5:6], ALU.mult),
                     reads=[sk], writes=[sk])
                r.op("vector", lambda e, p=p: e.tensor_scalar(
                    sqb[:, p, 24:32], sqb[:, p, 0:8], stt[:, p, 5:6], None, ALU.mult),
                     reads=[qk, sk], writes=[qk])
                r.op("vector", lambda e, p=p, tt=tt: e.scalar_tensor_tensor(
                    c_g[:, tt, :], sqb[:, p, 16:24], stt[:, p, 6:7], sqb[:, p, 24:32], ALU.mult, ALU.add),
                     reads=[qk, sk], writes=[("cg", u)])
                r.op("tensor", lambda e, tt=tt: e.transpose(
                    pg[0][0:8, 0:128], c_g[:, tt, :], g["idsf"][:]),
                     reads=[("cg", u), "idsf"], writes=[("pg0", u)])
                r.op("vector", lambda e, tt=tt: e.tensor_copy(
                    cT[:, tt * 128:(tt + 1) * 128], pg[0][0:8, 0:128]),
                     reads=[("pg0", u)], writes=[("cTb", u)])

            for td in range(16):
                for half in range(2):
                    r.mmgroup([lambda e, td=td, half=half: e.matmul(
                        py[:, half * 512:(half + 1) * 512], cT[:, td * 128:(td + 1) * 128],
                        b2_sb[:, half * 512:(half + 1) * 512], start=True, stop=True)],
                        reads=[("cTb", u), ("b2sb", u)], writes=[("py", u)])
                r.op("vector", lambda e, td=td: e.tensor_copy(y_acc[:, td, :], py[:]),
                     reads=[("py", u)], writes=[("ya", u, td)])

            def emit_td_unit(unit):
                ei_, ts_, td_, buf_ = unit
                gtd = ts_ * 4 + td_
                for half in range(2):
                    r.mmgroup([
                        (lambda e, jc=jc, td=td_, half=half, ts=ts_, buf=buf_: e.matmul(
                            py[:, half * 512:(half + 1) * 512],
                            pTb[:, ts % 2, jc, td * 128:(td + 1) * 128],
                            w2b[:, buf, jc, half * 512:(half + 1) * 512],
                            start=(jc == 0), stop=(jc == 3)))
                        for jc in range(4)],
                        reads=[("pT", u, ts_ % 2, 0), ("pT", u, ts_ % 2, 1),
                               ("pT", u, ts_ % 2, 2), ("pT", u, ts_ % 2, 3),
                               ("w2b", u, buf_)],
                        writes=[("py", u)])
                r.op("vector", lambda e, gtd=gtd, ei=ei_: e.scalar_tensor_tensor(
                    y_acc[:, gtd, :], py[:], c_g[:, gtd:gtd + 1, ei:ei + 1],
                    y_acc[:, gtd, :], ALU.mult, ALU.add),
                     reads=[("py", u), ("cg", u)], writes=[("ya", u, gtd)])

            pend = []
            for ei in range(nexperts):
                for j in range(8):
                    buf = (ei * 8 + j) % 2
                    js, je = j * 512, (j + 1) * 512
                    r.dma("gpsimd", w0b[:, buf, :, :], W["W0"][ei, j],
                          writes=[("w0b", u, buf)], sem="d_w0%d%s" % (buf, su))
                    r.dma("gpsimd", w1b[:, buf, :, :], W["W1"][ei, j],
                          writes=[("w1b", u, buf)], sem="d_w1%d%s" % (buf, su))
                    r.dma("gpsimd", w2b[:, buf, :, :], W["W2"][ei, j],
                          writes=[("w2b", u, buf)], sem="d_w2%d%s" % (buf, su))
                    for ts in range(4):
                        ns, ne_ = ts * 512, (ts + 1) * 512
                        for jc in range(4):
                            bk = (jc + ts) % 2
                            hp, gp = ph[bk], pg[bk]
                            hkey, gkey = ("ph%d" % bk, u), ("pg%d" % bk, u)
                            r.mmgroup([
                                (lambda e, dc=dc, jc=jc, hp=hp, buf=buf, ns=ns, ne_=ne_: e.matmul(
                                    hp[:], w0b[:, buf, dc, jc * 128:(jc + 1) * 128],
                                    xnT[:, dc, ns:ne_], start=(dc == 0), stop=(dc == 7)))
                                for dc in range(8)],
                                reads=[("w0b", u, buf), KX], writes=[hkey])
                            if pend:
                                emit_td_unit(pend.pop(0))
                            r.mmgroup([
                                (lambda e, dc=dc, jc=jc, gp=gp, buf=buf, ns=ns, ne_=ne_: e.matmul(
                                    gp[:], w1b[:, buf, dc, jc * 128:(jc + 1) * 128],
                                    xnT[:, dc, ns:ne_], start=(dc == 0), stop=(dc == 7)))
                                for dc in range(8)],
                                reads=[("w1b", u, buf), KX], writes=[gkey])
                            sp = jc % 2
                            jj = j * 4 + jc
                            r.op("scalar", lambda e, gp=gp, sp=sp, ei=ei, jj=jj: e.activation(
                                s_sb[:, sp, :], gp[:], AF.Silu, bias=b1_pp[:, ei:ei + 1, jj:jj + 1]),
                                 reads=[gkey, ("b1pp", u)], writes=[("ssb", u, sp)])
                            r.op("vector", lambda e, hp=hp, sp=sp, ei=ei, jj=jj, ts=ts, jc=jc: e.scalar_tensor_tensor(
                                pTb[:, ts % 2, jc, :], hp[:], b0_pp[:, ei:ei + 1, jj:jj + 1],
                                s_sb[:, sp, :], ALU.add, ALU.mult),
                                 reads=[hkey, ("b0pp", u), ("ssb", u, sp)],
                                 writes=[("pT", u, ts % 2, jc)])
                        for td in range(4):
                            pend.append((ei, ts, td, buf))
            for unit in pend:
                emit_td_unit(unit)
            pend = []

            for td in range(16):
                p = td % 2
                xkey = ("xt", u, p)
                xtile = xt[:, p, :]
                r.dma("sync", xtile, xin[(t0 + td) * 128:(t0 + td + 1) * 128, :],
                      writes=[xkey], sem="d_xt%d%s" % (p, su))
                r.op("vector", lambda e, xtile=xtile, td=td, p=p: e.tensor_add(
                    xo[:, p, :], xtile, y_acc[:, td, :]),
                     reads=[xkey, ("ya", u, td)], writes=[("xo", u, p)])
                r.dma("sync", xout[(t0 + td) * 128:(t0 + td + 1) * 128, :], xo[:, p, :],
                      reads=[("xo", u, p)], sem="d_so%d%s" % (p, su))
            r.barrier()
        r.emit(r.block)


def _decl_l0ffn_weights(nc, pref=""):
    W = {
        "Wg": nc.dram_tensor(pref + "Wg", [D, NE], F32, kind="ExternalInput").ap(),
        "W0": nc.dram_tensor(pref + "W0", [NE, 8, 128, 8, 512], F32, kind="ExternalInput").ap(),
        "W1": nc.dram_tensor(pref + "W1", [NE, 8, 128, 8, 512], F32, kind="ExternalInput").ap(),
        "W2": nc.dram_tensor(pref + "W2", [NE, 8, 128, 4, 1024], F32, kind="ExternalInput").ap(),
        "b2": nc.dram_tensor(pref + "b2", [NE, D], F32, kind="ExternalInput").ap(),
        "bg_bc": nc.dram_tensor(pref + "bg_bc", [128, 8], F32, kind="ExternalInput").ap(),
        "b0_pp": nc.dram_tensor(pref + "b0_pp", [128, 8, 32], F32, kind="ExternalInput").ap(),
        "b1_pp": nc.dram_tensor(pref + "b1_pp", [128, 8, 32], F32, kind="ExternalInput").ap(),
    }
    return W


def _host_l0ffn_weights(inp, pref=""):
    b0 = np.ascontiguousarray(inp["L0_ffn_b0"].reshape(8, 32, 128).transpose(2, 0, 1))
    b1 = np.ascontiguousarray(inp["L0_ffn_b1"].reshape(8, 32, 128).transpose(2, 0, 1))
    w0h = np.ascontiguousarray(
        inp["L0_ffn_W0"].reshape(8, 8, 128, 8, 512).transpose(0, 3, 2, 1, 4))
    w1h = np.ascontiguousarray(
        inp["L0_ffn_W1"].reshape(8, 8, 128, 8, 512).transpose(0, 3, 2, 1, 4))
    w2h = np.ascontiguousarray(
        inp["L0_ffn_W2"].reshape(8, 8, 4, 128, 1024).transpose(0, 1, 3, 2, 4))
    return {
        pref + "Wg": inp["L0_ffn_Wg"],
        pref + "W0": w0h,
        pref + "W1": w1h,
        pref + "W2": w2h,
        pref + "b2": inp["L0_ffn_b2"],
        pref + "bg_bc": np.ascontiguousarray(np.broadcast_to(inp["L0_ffn_bg"], (128, 8))),
        pref + "b0_pp": b0,
        pref + "b1_pp": b1,
    }


def _globals_setup(nc, es, r):
    g = {
        "idsb": es.enter_context(nc.sbuf_tensor("idsb", [128, 128], BF16)),
        "idsf": es.enter_context(nc.sbuf_tensor("idsf", [128, 128], F32)),
    }
    idsb_d = nc.dram_tensor("idsb_d", [128, 128], BF16, kind="ExternalInput").ap()
    idsf_d = nc.dram_tensor("idsf_d", [128, 128], F32, kind="ExternalInput").ap()
    r.dma("sync", g["idsb"][:], idsb_d, writes=["idsb"])
    r.dma("sync", g["idsf"][:], idsf_d, writes=["idsf"])
    return g


def _host_globals():
    return {
        "idsb_d": np.eye(128, dtype=np.float32).astype(BF),
        "idsf_d": np.eye(128, dtype=np.float32),
    }


def build_stageA(nexperts=NE):
    nc = bass.Bass("TRN2", target_bir_lowering=False)
    x = nc.dram_tensor("x", [T, D], F32, kind="ExternalInput").ap()
    out = nc.dram_tensor("out", [T, D], F32, kind="ExternalOutput").ap()
    W = _decl_l0ffn_weights(nc)
    with ExitStack() as es:
        r = Rec(nc, es)
        r.block = es.enter_context(nc.Block())
        g = _globals_setup(nc, es, r)
        emit_l0ffn(r, nc, g, x, out, W, "A", nexperts=nexperts)
        r.barrier()
        r.emit(r.block)
    return nc




def emit_l0attn(r, nc, g, xin, xout, W, u):
    with ExitStack() as es:
        sb = lambda n, s, d: es.enter_context(nc.sbuf_tensor(n + u, s, d))
        ps = lambda n, s, d: es.enter_context(nc.psum_tensor(n + u, s, d))

        xn = sb("xn", [128, 32, 1024], BF16)
        kT = sb("kT", [128, 8, 4096], BF16)
        pS = [ps("pS%d" % i, [128, 512], F32) for i in range(4)]
        pt_b = ps("ptb", [128, 128], BF16)
        py = ps("py", [128, 1024], F32)

        with ExitStack() as es1:
            sb1 = lambda n, s, d: es1.enter_context(nc.sbuf_tensor(n + u, s, d))
            wqk = sb1("wqk", [128, 8, 2048], BF16)
            bqk = sb1("bqk", [128, 16], F32)
            r.dma("sync", bqk[:], W["bqk_pp"], writes=[("bqk", u)])
            xnTg = sb1("xnTg", [128, 2, 8, 512], BF16)
            qstg = sb1("qstg", [128, 1, 8, 512], BF16)
            xt = sb1("xt", [128, 2, 1024], F32)
            sqb = sb1("sq", [128, 1, 1024], F32)
            xnt = sb1("xnt", [128, 2, 1024], BF16)
            stt = sb1("stt", [128, 2, 8], F32)

            r.dma("gpsimd", wqk[:], W["wqkT"].rearrange("(c p) n -> p c n", p=128),
                  writes=[("wqk", u)])
            for tg in range(8):
                gp = tg % 2
                for tt4 in range(4):
                    tt = tg * 4 + tt4
                    p = tt % 2
                    xkey = ("xt", u, p)
                    xtile = xt[:, p, :]
                    r.dma("sync", xtile, xin[tt * 128:(tt + 1) * 128, :], writes=[xkey],
                          sem="d_xt%d%s" % (p, u))
                    layernorm_tile(r, u, xtile, stt[:, p, :], sqb[:, 0, :], ("stt", u, p), ("sq", u, 0), xkey)
                    r.op("vector", lambda e, p=p: e.tensor_scalar(
                        xnt[:, p, :], xt[:, p, :], stt[:, p, 7:8], stt[:, p, 6:7], ALU.add, ALU.mult),
                         reads=[xkey, ("stt", u, p)], writes=[("xnt", u, p)])
                    r.op("vector", lambda e, p=p, tt=tt: e.tensor_copy(xn[:, tt, :], xnt[:, p, :]),
                         reads=[("xnt", u, p)], writes=[("xn", u)])
                    for dc in range(8):
                        r.op("tensor", lambda e, p=p, dc=dc: e.transpose(
                            pt_b[:], xnt[:, p, dc * 128:(dc + 1) * 128], g["idsb"][:]),
                             reads=[("xnt", u, p), "idsb"], writes=[("ptb", u)])
                        r.op("vector", lambda e, dc=dc, tt4=tt4, gp=gp: e.tensor_copy(
                            xnTg[:, gp, dc, tt4 * 128:(tt4 + 1) * 128], pt_b[:]),
                             reads=[("ptb", u)], writes=[("xnTg", u, gp)])
                for m in range(16):
                    bank = m % 2
                    pk = ("pS%d" % bank, u)
                    r.mmgroup([
                        (lambda e, dc=dc, m=m, gp=gp, bank=bank: e.matmul(
                            pS[bank][:], wqk[:, dc, m * 128:(m + 1) * 128],
                            xnTg[:, gp, dc, :], start=(dc == 0), stop=(dc == 7)))
                        for dc in range(8)],
                        reads=[("wqk", u), ("xnTg", u, gp)], writes=[pk])
                    if m < 8:
                        r.op("vector", lambda e, m=m, bank=bank: e.tensor_scalar(
                            qstg[:, 0, m, :], pS[bank][:], bqk[:, m:m + 1], None, ALU.add),
                             reads=[pk, ("bqk", u)], writes=[("qstg", u)])
                    else:
                        r.op("vector", lambda e, m=m, gp=gp, tg=tg, bank=bank: e.tensor_scalar(
                            kT[:, m - 8, tg * 512:(tg + 1) * 512], pS[bank][:],
                            bqk[:, m:m + 1], None, ALU.add),
                             reads=[pk, ("bqk", u)], writes=[("kT", u)])
                r.dma("sync", W["qTs"][:, :, tg * 512:(tg + 1) * 512].rearrange("c p n -> p c n"),
                      qstg[:, 0, :, :], reads=[("qstg", u)], sem="d_qs%s" % u)
            r.barrier()
            r.emit(r.block)

        with ExitStack() as es2:
            sb2 = lambda n, s, d: es2.enter_context(nc.sbuf_tensor(n + u, s, d))
            qTt = sb2("qTt", [128, 2, 8, 128], BF16)
            Eb = sb2("Eb", [128, 2, 4096], BF16)
            tri = sb2("tri", [128, 4, 512], BF16)
            r.dma("sync", tri[:], W["tri4"].rearrange("r p n -> p r n"), writes=[("tri", u)])
            avg = sb2("avg", [128, 4096], F32)
            gum = sb2("gum", [128, 4096], F32)
            dwb = sb2("dwb", [128, 4096], BF16)
            dwT = sb2("dwT", [128, 2, 128], BF16)
            zz = sb2("zz", [128, 8], F32)
            zcv = sb2("zcv", [128, 2, 8], F32)
            scb = sb2("scb", [128, 2], F32)
            xt2 = sb2("xt2", [128, 1, 1024], F32)
            xo2 = sb2("xo2", [128, 1, 1024], F32)

            r.op("vector", lambda e: e.memset(scb[:, 0:1], 0.125), writes=[("scb", u)])
            r.op("vector", lambda e: e.memset(scb[:, 1:2], 0.5), writes=[("scb", u)])

            for qt in range(32):
                nk = qt // 4 + 1
                span = nk * 512
                qp = qt % 2
                r.dma("sync", qTt[:, qp, :, :],
                      W["qTs"][:, :, qt * 128:(qt + 1) * 128].rearrange("c p n -> p c n"),
                      writes=[("qTt", u, qp)], sem="d_qt%d%s" % (qp, u))
                r.dma("sync", gum[:, 0:span], W["gum"][qt * 128:(qt + 1) * 128, 0:span],
                      writes=[("gum", u)], sem="d_gum%s" % u)
                for hp in range(8):
                    for h2 in range(2):
                        h = 2 * hp + h2
                        ek = ("Eb", u, h2)
                        zk = ("zz", u, h2)
                        for kt in range(nk):
                            bank = 2 * (kt % 2) + h2
                            pk = ("pS%d" % bank, u)
                            base = 64 * h2
                            r.mmgroup([(lambda e, h=h, kt=kt, bank=bank, base=base, qp=qp: e.matmul(
                                pS[bank][:], qTt[base:base + 64, qp, h // 2, :],
                                kT[base:base + 64, h // 2, kt * 512:(kt + 1) * 512],
                                start=True, stop=True))],
                                reads=[("qTt", u, qp), ("kT", u)], writes=[pk])
                            if kt == nk - 1:
                                r.op("vector", lambda e, bank=bank, qt=qt: e.tensor_tensor(
                                    pS[bank][:], pS[bank][:], tri[:, qt % 4, :], ALU.add),
                                     reads=[("tri", u), pk], writes=[pk])
                            r.op("scalar", lambda e, bank=bank, kt=kt, h2=h2: e.activation(
                                Eb[:, h2, kt * 512:(kt + 1) * 512], pS[bank][:], AF.Exp,
                                scale=scb[:, 0:1], accum_out=zcv[:, h2, kt:kt + 1]),
                                 reads=[pk, ("scb", u)], writes=[ek, ("zzc", u, h2)])
                        r.op("vector", lambda e, h2=h2, nk=nk: e.reduce_sum(
                            zz[:, h2:h2 + 1], zcv[:, h2, 0:nk], axis=AX),
                             reads=[("zzc", u, h2)], writes=[zk])
                        r.op("vector", lambda e, h2=h2: e.tensor_scalar(
                            zz[:, 2 + h2:3 + h2], zz[:, h2:h2 + 1], 16.0, 1e-30, ALU.mult, ALU.add),
                             reads=[zk], writes=[zk])
                        r.op("vector", lambda e, h2=h2: e.reciprocal(
                            zz[:, 2 + h2:3 + h2], zz[:, 2 + h2:3 + h2]),
                             reads=[zk], writes=[zk])
                        if hp == 0 and h2 == 0:
                            r.op("vector", lambda e, h2=h2, span=span: e.tensor_scalar(
                                avg[:, 0:span], Eb[:, h2, 0:span], zz[:, 2 + h2:3 + h2], None, ALU.mult),
                                 reads=[ek, zk], writes=[("avg", u)])
                        else:
                            r.op("vector", lambda e, h2=h2, span=span: e.scalar_tensor_tensor(
                                avg[:, 0:span], Eb[:, h2, 0:span], zz[:, 2 + h2:3 + h2],
                                avg[:, 0:span], ALU.mult, ALU.add),
                                 reads=[ek, zk, ("avg", u)], writes=[("avg", u)])
                r.op("vector", lambda e, span=span: e.tensor_scalar(
                    avg[:, 0:span], avg[:, 0:span], -40.0, 40.0, ALU.max, ALU.min),
                     reads=[("avg", u)], writes=[("avg", u)])
                r.op("vector", lambda e, span=span: e.tensor_add(
                    avg[:, 0:span], avg[:, 0:span], gum[:, 0:span]),
                     reads=[("avg", u), ("gum", u)], writes=[("avg", u)])
                r.op("vector", lambda e, span=span, qt=qt: e.tensor_tensor(
                    avg[:, span - 512:span], avg[:, span - 512:span], tri[:, qt % 4, :], ALU.add),
                     reads=[("avg", u), ("tri", u)], writes=[("avg", u)])
                r.op("scalar", lambda e, span=span: e.activation(
                    dwb[:, 0:span], avg[:, 0:span], AF.Exp, scale=scb[:, 1:2],
                    accum_out=zz[:, 6:7]),
                     reads=[("avg", u), ("scb", u)], writes=[("dwb", u), ("zz2", u)])
                r.op("vector", lambda e: e.tensor_scalar(
                    zz[:, 7:8], zz[:, 6:7], 1.0, 1e-30, ALU.mult, ALU.add),
                     reads=[("zz2", u)], writes=[("zz2", u)])
                r.op("vector", lambda e: e.reciprocal(zz[:, 7:8], zz[:, 7:8]),
                     reads=[("zz2", u)], writes=[("zz2", u)])
                for ktile in range(nk * 4):
                    dp = ktile % 2
                    r.op("tensor", lambda e, ktile=ktile: e.transpose(
                        pt_b[:], dwb[:, ktile * 128:(ktile + 1) * 128], g["idsb"][:]),
                         reads=[("dwb", u), "idsb"], writes=[("ptb", u)])
                    r.op("vector", lambda e, dp=dp: e.tensor_copy(dwT[:, dp, :], pt_b[:]),
                         reads=[("ptb", u)], writes=[("dwT", u, dp)])
                    for half in range(2):
                        r.mmgroup([(lambda e, ktile=ktile, half=half, dp=dp, nk=nk: e.matmul(
                            py[:, half * 512:(half + 1) * 512], dwT[:, dp, :],
                            xn[:, ktile, half * 512:(half + 1) * 512],
                            start=(ktile == 0), stop=(ktile == nk * 4 - 1)))],
                            reads=[("dwT", u, dp), ("xn", u)], writes=[("py", u)])
                p2 = 0
                xkey = ("xt2", u, p2)
                r.dma("sync", xt2[:, p2, :], xin[qt * 128:(qt + 1) * 128, :], writes=[xkey],
                      sem="d_x2%s" % u)
                r.op("vector", lambda e, p2=p2: e.scalar_tensor_tensor(
                    xo2[:, p2, :], py[:], zz[:, 7:8], xt2[:, p2, :], ALU.mult, ALU.add),
                     reads=[("py", u), ("zz2", u), xkey], writes=[("xo2", u, p2)])
                r.dma("sync", xout[qt * 128:(qt + 1) * 128, :], xo2[:, p2, :],
                      reads=[("xo2", u, p2)], sem="d_o2%s" % u)
            r.barrier()
        r.emit(r.block)


def _decl_l0attn_weights(nc, pref=""):
    return {
        "wqkT": nc.dram_tensor(pref + "wqkT", [D, 2 * D], F32, kind="ExternalInput").ap(),
        "bqk_pp": nc.dram_tensor(pref + "bqk_pp", [128, 16], F32, kind="ExternalInput").ap(),
        "tri4": nc.dram_tensor(pref + "tri4", [4, 128, 512], BF16, kind="ExternalInput").ap(),
        "gum": nc.dram_tensor(pref + "gum", [T, T], F32, kind="ExternalInput").ap(),
        "qTs": nc.dram_tensor(pref + "qTs", [8, 128, T], BF16, kind="Internal").ap(),
    }


def _host_l0attn_weights(inp, pref=""):
    w = inp["L0_attn_in_proj_weight"]
    b = inp["L0_attn_in_proj_bias"]
    wqkT = np.ascontiguousarray(w[:2 * D].T)
    bqk = np.ascontiguousarray(b[:2 * D].reshape(16, 128).T)
    tri4 = np.empty((4, 128, 512), np.float32)
    for rr in range(4):
        qr = np.arange(128)[:, None]
        kc = np.arange(512)[None, :]
        tri4[rr] = np.where(kc < rr * 128 + qr, 0.0, NEG).astype(np.float32)
    return {pref + "wqkT": wqkT, pref + "bqk_pp": bqk, pref + "tri4": tri4.astype(BF)}


def _host_gumbel():
    def _rotl(x, d):
        return (x << np.uint32(d)) | (x >> np.uint32(32 - d))

    n = CORES * T * T
    minval, maxval = np.float32(1e-06), np.float32(1.0 - 1e-06)
    idx = np.arange(n, dtype=np.uint64)
    x0 = (idx >> np.uint64(32)).astype(np.uint32)
    x1 = (idx & np.uint64(0xFFFFFFFF)).astype(np.uint32)
    del idx
    ks0, ks1 = np.uint32(0), np.uint32(42)
    ks2 = ks0 ^ ks1 ^ np.uint32(0x1BD11BDA)
    rotations = [(13, 15, 26, 6), (17, 29, 16, 24)]
    x0 = x0 + ks0
    x1 = x1 + ks1
    ks = [ks1, ks2, ks0]
    for i in range(5):
        for rr in rotations[i % 2]:
            x0 += x1
            x1 = _rotl(x1, rr)
            x1 = x0 ^ x1
        x0 += ks[i % 3]
        x1 += ks[(i + 1) % 3] + np.uint32(i + 1)
    bits = x0 ^ x1
    del x0, x1
    fl = ((bits >> np.uint32(9)) | np.uint32(0x3F800000)).view(np.float32) - np.float32(1.0)
    del bits
    u = np.maximum(minval, fl * (maxval - minval) + minval)
    del fl
    gm = -np.log(-np.log(u))
    return gm.reshape(CORES, T, T)


def build_stageB():
    nc = bass.Bass("TRN2", target_bir_lowering=False)
    x = nc.dram_tensor("x", [T, D], F32, kind="ExternalInput").ap()
    out = nc.dram_tensor("out", [T, D], F32, kind="ExternalOutput").ap()
    W = _decl_l0attn_weights(nc)
    with ExitStack() as es:
        r = Rec(nc, es)
        r.block = es.enter_context(nc.Block())
        g = _globals_setup(nc, es, r)
        emit_l0attn(r, nc, g, x, out, W, "B")
        r.barrier()
        r.emit(r.block)
    return nc




def bcast16(ap2d, inner):
    return bass.AP(ap2d.tensor, ap2d.offset, [ap2d.ap[0], [0, 16], [1, inner]])


def emit_l1ffn(r, nc, g, xin, xout, W, u, su=None):
    su = su or u
    with ExitStack() as es:
        sb = lambda n, s, d: es.enter_context(nc.sbuf_tensor(n + u, s, d))
        scb = sb("scb", [128, 2], F32)
        r.op("vector", lambda e: e.memset(scb[:, 0:1], 0.125), writes=[("scb", u)])
        esA = es.enter_context(ExitStack())
        kT = esA.enter_context(nc.sbuf_tensor("kT" + u, [128, 8, 4096], BF16))
        Vaug = esA.enter_context(nc.sbuf_tensor("Vaug" + u, [128, 32, 1040], BF16))
        Vv = Vaug[:].rearrange("p t (h c) -> p t h c", c=65)
        r.op("vector", lambda e: e.memset(Vv[:, :, :, 64:65], 1.0), writes=[("Vones", u)])

        with ExitStack() as e1:
            sb1 = lambda n, s, d: e1.enter_context(nc.sbuf_tensor(n + u, s, d))
            ps1 = lambda n, s, d: e1.enter_context(nc.psum_tensor(n + u, s, d))
            pt_b = ps1("ptb", [128, 128], BF16)
            xt = sb1("xt", [128, 2, 1024], F32)
            sqb = sb1("sq", [128, 1024], F32)
            xnt = sb1("xnt", [128, 2, 1024], BF16)
            stt = sb1("stt", [128, 2, 8], F32)
            xnTg = sb1("xnTg", [128, 8, 512], BF16)
            for tg in range(8):
                for tt4 in range(4):
                    tt = tg * 4 + tt4
                    p = tt % 2
                    xkey = ("xt", u, p)
                    xtile = xt[:, p, :]
                    r.dma("sync", xtile, xin[tt * 128:(tt + 1) * 128, :], writes=[xkey],
                          sem="d_xt%d%s" % (p, su))
                    layernorm_tile(r, u, xtile, stt[:, p, :], sqb[:], ("stt", u, p), ("sq", u), xkey)
                    r.op("vector", lambda e, p=p: e.tensor_scalar(
                        xnt[:, p, :], xt[:, p, :], stt[:, p, 7:8], stt[:, p, 6:7], ALU.add, ALU.mult),
                         reads=[xkey, ("stt", u, p)], writes=[("xnt", u, p)])
                    r.dma("sync", W["xns"][tt * 128:(tt + 1) * 128, :], xnt[:, p, :],
                          reads=[("xnt", u, p)], sem="d_xns%d%s" % (p, su))
                    for dc in range(8):
                        r.op("tensor", lambda e, p=p, dc=dc: e.transpose(
                            pt_b[:], xnt[:, p, dc * 128:(dc + 1) * 128], g["idsb"][:]),
                             reads=[("xnt", u, p), "idsb"], writes=[("ptb", u)])
                        r.op("vector", lambda e, dc=dc, tt4=tt4: e.tensor_copy(
                            xnTg[:, dc, tt4 * 128:(tt4 + 1) * 128], pt_b[:]),
                             reads=[("ptb", u)], writes=[("xnTg", u)])
                r.dma("sync", W["xnTs"][:, :, tg * 512:(tg + 1) * 512].rearrange("c p n -> p c n"),
                      xnTg[:], reads=[("xnTg", u)], sem="d_xnT%s" % su)
            r.barrier()
            r.emit(r.block)

        with ExitStack() as e1:
            sb1 = lambda n, s, d: e1.enter_context(nc.sbuf_tensor(n + u, s, d))
            ps1 = lambda n, s, d: e1.enter_context(nc.psum_tensor(n + u, s, d))
            pt_b = ps1("ptbb", [128, 128], BF16)
            pP = [ps1("pP%d" % i, [128, 512], F32) for i in range(2)]
            wpr = sb1("wpr", [128, 2, 8, 1024], BF16)
            xnTg = sb1("xnTg2", [128, 2, 8, 512], BF16)
            bias_bc = sb1("bbc", [128, 1024], F32)
            tabs = sb1("tabs", [128, 2, 2, 64], F32)
            work = sb1("work", [128, 1024], F32)
            tmp1 = sb1("tmp1", [128, 512], F32)
            tmp2 = sb1("tmp2", [128, 512], F32)
            rro = sb1("rro", [128, 1024], BF16)
            qstg = sb1("qstg", [128, 8, 512], BF16)

            for ip, pr in enumerate(["q", "k", "v"]):
                wb_ = ip % 2
                r.dma("gpsimd", wpr[:, wb_, :, :],
                      W["qkvwT"][:, ip * 1024:(ip + 1) * 1024].rearrange("(c p) n -> p c n", p=128),
                      writes=[("wpr", u, wb_)], sem="d_wp%d%s" % (wb_, su))
                r.dma("sync", bias_bc[:], W["bqkv_bc"][:, ip * 1024:(ip + 1) * 1024],
                      writes=[("bbc", u)], sem="d_bbc%s" % su)
                for tg in range(8):
                    gp = tg % 2
                    r.dma("sync", xnTg[:, gp, :, :],
                          W["xnTs"][:, :, tg * 512:(tg + 1) * 512].rearrange("c p n -> p c n"),
                          writes=[("xnTg2", u, gp)], sem="d_xg%d%s" % (gp, su))
                    for tt4 in range(4):
                        tt = tg * 4 + tt4
                        if pr != "v":
                            r.dma("sync", tabs[:, tt % 2, 0, :], W["cosF"][:, tt, :],
                                  writes=[("tabs", u, tt % 2)], sem="d_tc%d%s" % (tt % 2, su))
                            r.dma("sync", tabs[:, tt % 2, 1, :], W["sinF"][:, tt, :],
                                  writes=[("tabs", u, tt % 2)], sem="d_ts%d%s" % (tt % 2, su))
                        for half in range(2):
                            bk = (tt4 * 2 + half) % 2
                            pk = ("pP%d" % bk, u)
                            r.mmgroup([
                                (lambda e, dc=dc, tt4=tt4, half=half, gp=gp, wb_=wb_, bk=bk: e.matmul(
                                    pP[bk][:], xnTg[:, gp, dc, tt4 * 128:(tt4 + 1) * 128],
                                    wpr[:, wb_, dc, half * 512:(half + 1) * 512],
                                    start=(dc == 0), stop=(dc == 7)))
                                for dc in range(8)],
                                reads=[("xnTg2", u, gp), ("wpr", u, wb_)], writes=[pk])
                            if pr == "v":
                                pv = pP[bk][:].rearrange("p (h c) -> p h c", c=64)
                                bv = bias_bc[:, half * 512:(half + 1) * 512].rearrange(
                                    "p (h c) -> p h c", c=64)
                                r.op("vector", lambda e, tt=tt, half=half, pv=pv, bv=bv: e.tensor_tensor(
                                    Vv[:, tt, half * 8:(half + 1) * 8, 0:64], pv, bv, ALU.add),
                                     reads=[pk, ("bbc", u)], writes=[("V", u)])
                            else:
                                r.op("vector", lambda e, half=half, bk=bk: e.tensor_tensor(
                                    work[:, half * 512:(half + 1) * 512], pP[bk][:],
                                    bias_bc[:, half * 512:(half + 1) * 512], ALU.add),
                                     reads=[pk, ("bbc", u)], writes=[("work", u)])
                        if pr != "v":
                            tp = tt % 2
                            wv = work[:].rearrange("p (h c) -> p h c", c=64)
                            rv = rro[:].rearrange("p (h c) -> p h c", c=64)
                            t1v = tmp1[:].rearrange("p (h c) -> p h c", c=32)
                            t2v = tmp2[:].rearrange("p (h c) -> p h c", c=32)
                            c1 = bcast16(tabs[:, tp, 0, 0:32], 32)
                            c2 = bcast16(tabs[:, tp, 0, 32:64], 32)
                            s1 = bcast16(tabs[:, tp, 1, 0:32], 32)
                            s2 = bcast16(tabs[:, tp, 1, 32:64], 32)
                            tk = ("tabs", u, tp)
                            r.op("vector", lambda e, c1=c1, wv=wv, t1v=t1v: e.tensor_tensor(
                                t1v, wv[:, :, 0:32], c1, ALU.mult),
                                 reads=[("work", u), tk], writes=[("tmp1", u)])
                            r.op("vector", lambda e, s1=s1, wv=wv, t2v=t2v: e.tensor_tensor(
                                t2v, wv[:, :, 32:64], s1, ALU.mult),
                                 reads=[("work", u), tk], writes=[("tmp2", u)])
                            r.op("vector", lambda e, rv=rv, t1v=t1v, t2v=t2v: e.tensor_tensor(
                                rv[:, :, 0:32], t1v, t2v, ALU.subtract),
                                 reads=[("tmp1", u), ("tmp2", u)], writes=[("rro", u)])
                            r.op("vector", lambda e, c2=c2, wv=wv, t1v=t1v: e.tensor_tensor(
                                t1v, wv[:, :, 32:64], c2, ALU.mult),
                                 reads=[("work", u), tk], writes=[("tmp1", u)])
                            r.op("vector", lambda e, s2=s2, wv=wv, t2v=t2v: e.tensor_tensor(
                                t2v, wv[:, :, 0:32], s2, ALU.mult),
                                 reads=[("work", u), tk], writes=[("tmp2", u)])
                            r.op("vector", lambda e, rv=rv, t1v=t1v, t2v=t2v: e.tensor_tensor(
                                rv[:, :, 32:64], t1v, t2v, ALU.add),
                                 reads=[("tmp1", u), ("tmp2", u)], writes=[("rro", u)])
                            for dc in range(8):
                                r.op("tensor", lambda e, dc=dc: e.transpose(
                                    pt_b[:], rro[:, dc * 128:(dc + 1) * 128], g["idsb"][:]),
                                     reads=[("rro", u), "idsb"], writes=[("ptb", u)])
                                if pr == "q":
                                    r.op("vector", lambda e, dc=dc, tt4=tt4: e.tensor_copy(
                                        qstg[:, dc, tt4 * 128:(tt4 + 1) * 128], pt_b[:]),
                                         reads=[("ptb", u)], writes=[("qstg", u)])
                                else:
                                    r.op("vector", lambda e, dc=dc, tt=tt: e.tensor_copy(
                                        kT[:, dc, tt * 128:(tt + 1) * 128], pt_b[:]),
                                         reads=[("ptb", u)], writes=[("kT", u)])
                    if pr == "q":
                        r.dma("sync", W["qTs"][:, :, tg * 512:(tg + 1) * 512].rearrange("c p n -> p c n"),
                              qstg[:], reads=[("qstg", u)], sem="d_qs%s" % su)
            r.barrier()
            r.emit(r.block)

        with ExitStack() as e2:
            sb2 = lambda n, s, d: e2.enter_context(nc.sbuf_tensor(n + u, s, d))
            ps2 = lambda n, s, d: e2.enter_context(nc.psum_tensor(n + u, s, d))
            pS = [ps2("pS%d" % i, [128, 1024], F32) for i in range(2)]
            pOT = [ps2("pOT%d" % i, [128, 512], F32) for i in range(1)]
            pt_f = ps2("ptf", [128, 128], F32)
            pt_b2 = ps2("ptb2", [128, 128], BF16)
            qTg = sb2("qTg", [128, 1, 8, 512], BF16)
            ET = sb2("ET", [128, 2, 1024], BF16)
            oT = sb2("oT", [128, 2, 512], F32)
            rz = sb2("rz", [128, 2], F32)
            ao = sb2("ao", [128, 4, 1024], BF16)
            aostg = sb2("aostg", [128, 8, 512], BF16)

            for qg in range(8):
                gp = 0
                r.dma("sync", qTg[:, 0, :, :],
                      W["qTs"][:, :, qg * 512:(qg + 1) * 512].rearrange("c p n -> p c n"),
                      writes=[("qTg", u, gp)], sem="d_qg%s" % u)
                for h in range(16):
                    hp = h // 2
                    h2 = h % 2
                    base = 64 * h2
                    ok = ("pOT0", u)
                    for kt2 in range(16):
                        bank = kt2 % 2
                        for sub in range(2):
                            kt = 2 * kt2 + sub
                            r.mmgroup([(lambda e, hp=hp, kt=kt, bank=bank, base=base, sub=sub: e.matmul(
                                pS[bank][:, sub * 512:(sub + 1) * 512],
                                kT[base:base + 64, hp, kt * 128:(kt + 1) * 128],
                                qTg[base:base + 64, 0, hp, :], start=True, stop=True))],
                                reads=[("qTg", u, gp), ("kT", u)], writes=[("pS", u, bank, sub)])
                        ekey = ("ET", u, bank)
                        r.op("scalar", lambda e, bank=bank: e.activation(
                            ET[:, bank, :], pS[bank][:], AF.Exp, scale=scb[:, 0:1]),
                             reads=[("pS", u, bank, 0), ("pS", u, bank, 1), ("scb", u)],
                             writes=[ekey])
                        if kt2 > 0:
                            pb = 1 - bank
                            for sub in range(2):
                                kt = 2 * (kt2 - 1) + sub
                                r.mmgroup([(lambda e, h=h, kt=kt, pb=pb, sub=sub: e.matmul(
                                    pOT[0][0:65, :], Vaug[:, kt, h * 65:(h + 1) * 65],
                                    ET[:, pb, sub * 512:(sub + 1) * 512],
                                    start=(kt == 0), stop=False))],
                                    reads=[("ET", u, pb), ("V", u), ("Vones", u)], writes=[ok])
                    for sub in range(2):
                        kt = 30 + sub
                        r.mmgroup([(lambda e, h=h, kt=kt, sub=sub: e.matmul(
                            pOT[0][0:65, :], Vaug[:, kt, h * 65:(h + 1) * 65],
                            ET[:, 1, sub * 512:(sub + 1) * 512],
                            start=False, stop=(sub == 1)))],
                            reads=[("ET", u, 1), ("V", u), ("Vones", u)], writes=[ok])
                    r.op("vector", lambda e, h2=h2: e.tensor_copy(oT[0:65, h2, :], pOT[0][0:65, :]),
                         reads=[ok], writes=[("oT", u, h2)])
                    for qs in range(4):
                        r.op("tensor", lambda e, h2=h2, qs=qs: e.transpose(
                            pt_f[:, 0:65], oT[0:65, h2, qs * 128:(qs + 1) * 128],
                            g["idsf"][0:65, 0:65]),
                             reads=[("oT", u, h2), "idsf"], writes=[("ptf", u)])
                        r.op("vector", lambda e, h2=h2: e.reciprocal(
                            rz[:, h2:h2 + 1], pt_f[:, 64:65]),
                             reads=[("ptf", u)], writes=[("rz", u, h2)])
                        r.op("vector", lambda e, h2=h2, qs=qs, h=h: e.tensor_scalar(
                            ao[:, qs, h * 64:(h + 1) * 64], pt_f[:, 0:64],
                            rz[:, h2:h2 + 1], None, ALU.mult),
                             reads=[("ptf", u), ("rz", u, h2)], writes=[("ao", u)])
                for qs in range(4):
                    for dc in range(8):
                        r.op("tensor", lambda e, qs=qs, dc=dc: e.transpose(
                            pt_b2[:], ao[:, qs, dc * 128:(dc + 1) * 128], g["idsb"][:]),
                             reads=[("ao", u), "idsb"], writes=[("ptb2", u)])
                        r.op("vector", lambda e, qs=qs, dc=dc: e.tensor_copy(
                            aostg[:, dc, qs * 128:(qs + 1) * 128], pt_b2[:]),
                             reads=[("ptb2", u)], writes=[("aostg", u)])
                r.dma("sync", W["aoTs"][:, :, qg * 512:(qg + 1) * 512].rearrange("c p n -> p c n"),
                      aostg[:], reads=[("aostg", u)], sem="d_ao%s" % su)
            r.barrier()
            r.emit(r.block)
        esA.close()

        with ExitStack() as e3:
            sb3 = lambda n, s, d: e3.enter_context(nc.sbuf_tensor(n + u, s, d))
            ps3 = lambda n, s, d: e3.enter_context(nc.psum_tensor(n + u, s, d))
            pG = ps3("pG", [128, 512], F32)
            pyE2 = [ps3("pyE%d" % i, [128, 1024], F32) for i in range(2)]
            pyE = pyE2[0]
            pyF = ps3("pyF", [128, 1024], F32)
            pt_b3 = ps3("ptb3", [128, 128], BF16)
            aoTg = sb3("aoTg", [128, 8, 2048], BF16)
            moe = sb3("moe", [128, 16, 1024], F32)
            ewb = sb3("ewb", [128, 2, 8, 1024], BF16)
            fwb = sb3("fwb", [128, 8, 1024], BF16)
            gwb = sb3("gwb", [128, 8, 8], BF16)
            gbias = sb3("gbias", [128, 8], F32)
            fbias = sb3("fbias", [128, 1024], F32)
            eb_sb = sb3("ebsb", [8, 1024], F32)
            cm = sb3("cm", [128, 16, 8], F32)
            cmT = sb3("cmT", [8, 2048], F32)
            gs = sb3("gs", [128, 48], F32)
            mb3 = sb3("mb3", [128, 1024], BF16)
            moeT = sb3("moeT", [128, 8, 128], BF16)
            xnb3 = sb3("xnb3", [128, 2, 1024], BF16)
            xt3 = sb3("xt3", [128, 2, 1024], F32)
            xo3 = sb3("xo3", [128, 2, 1024], F32)

            r.dma("gpsimd", gwb[:], W["gwT"].rearrange("(c p) n -> p c n", p=128),
                  writes=[("gwb", u)], sem="d_gw%s" % su)
            r.dma("gpsimd", fwb[:], W["fwT"].rearrange("(c p) n -> p c n", p=128),
                  writes=[("fwb", u)], sem="d_fw%s" % su)
            r.dma("sync", gbias[:], W["gb_bc"], writes=[("gbias", u)], sem="d_gb%s" % su)
            r.dma("sync", fbias[:], W["fb_bc"], writes=[("fbias", u)], sem="d_fb%s" % su)
            r.dma("sync", eb_sb[:], W["eb"], writes=[("ebsb", u)], sem="d_eb%s" % su)

            for G in range(2):
                t0 = G * 16
                r.dma("sync", aoTg[:],
                      W["aoTs"][:, :, G * 2048:(G + 1) * 2048].rearrange("c p n -> p c n"),
                      writes=[("aoTg", u)], sem="d_aog%s" % su)
                for tt in range(16):
                    r.mmgroup([
                        (lambda e, dc=dc, tt=tt: e.matmul(
                            pG[:, 0:8], aoTg[:, dc, tt * 128:(tt + 1) * 128], gwb[:, dc, :],
                            start=(dc == 0), stop=(dc == 7)))
                        for dc in range(8)],
                        reads=[("aoTg", u), ("gwb", u)], writes=[("pG", u)])
                    gk = ("gs", u)
                    r.op("vector", lambda e: e.tensor_add(gs[:, 0:8], pG[:, 0:8], gbias[:]),
                         reads=[("pG", u), ("gbias", u)], writes=[gk])
                    r.op("scalar", lambda e: e.activation(
                        gs[:, 8:16], gs[:, 0:8], AF.Exp, accum_out=gs[:, 16:17]),
                         reads=[gk], writes=[gk])
                    r.op("vector", lambda e: e.reciprocal(gs[:, 17:18], gs[:, 16:17]),
                         reads=[gk], writes=[gk])
                    r.op("vector", lambda e: e.tensor_scalar(
                        gs[:, 24:32], gs[:, 8:16], gs[:, 17:18], None, ALU.mult),
                         reads=[gk], writes=[gk])
                    r.op("vector", lambda e: e.reduce_max(gs[:, 18:19], gs[:, 24:32], axis=AX),
                         reads=[gk], writes=[gk])
                    r.op("vector", lambda e: e.tensor_scalar(
                        gs[:, 32:40], gs[:, 24:32], gs[:, 18:19], None, ALU.is_equal),
                         reads=[gk], writes=[gk])
                    r.op("vector", lambda e: e.scalar_tensor_tensor(
                        gs[:, 40:48], gs[:, 32:40], -1e30, gs[:, 24:32], ALU.mult, ALU.add),
                         reads=[gk], writes=[gk])
                    r.op("vector", lambda e: e.reduce_max(gs[:, 19:20], gs[:, 40:48], axis=AX),
                         reads=[gk], writes=[gk])
                    r.op("vector", lambda e: e.tensor_scalar(
                        gs[:, 32:40], gs[:, 24:32], gs[:, 19:20], None, ALU.is_ge),
                         reads=[gk], writes=[gk])
                    r.op("vector", lambda e, tt=tt: e.tensor_tensor(
                        cm[:, tt, :], gs[:, 24:32], gs[:, 32:40], ALU.mult),
                         reads=[gk], writes=[("cm", u)])
                    r.op("tensor", lambda e, tt=tt: e.transpose(
                        pG[0:8, 0:128], cm[:, tt, :], g["idsf"][:]),
                         reads=[("cm", u), "idsf"], writes=[("pG", u)])
                    r.op("vector", lambda e, tt=tt: e.tensor_copy(
                        cmT[:, tt * 128:(tt + 1) * 128], pG[0:8, 0:128]),
                         reads=[("pG", u)], writes=[("cmT", u)])
                for tt in range(16):
                    pp = tt % 2
                    pE = pyE2[pp]
                    for half in range(2):
                        r.mmgroup([(lambda e, tt=tt, half=half, pE=pE: e.matmul(
                            pE[:, half * 512:(half + 1) * 512], cmT[:, tt * 128:(tt + 1) * 128],
                            eb_sb[:, half * 512:(half + 1) * 512], start=True, stop=True))],
                            reads=[("cmT", u), ("ebsb", u)], writes=[("pyE", u, pp)])
                    r.op("vector", lambda e, tt=tt, pE=pE: e.tensor_copy(moe[:, tt, :], pE[:]),
                         reads=[("pyE", u, tt % 2)], writes=[("moe", u, tt)])
                for ei in range(NE):
                    wb_ = ei % 2
                    r.dma("gpsimd", ewb[:, wb_, :, :],
                          W["ewT"][ei].rearrange("(c p) n -> p c n", p=128),
                          writes=[("ewb", u, wb_)], sem="d_ew%d%s" % (wb_, su))
                    for tt in range(16):
                        pp = tt % 2
                        pE = pyE2[pp]
                        for half in range(2):
                            r.mmgroup([
                                (lambda e, dc=dc, tt=tt, half=half, wb_=wb_, pE=pE: e.matmul(
                                    pE[:, half * 512:(half + 1) * 512],
                                    aoTg[:, dc, tt * 128:(tt + 1) * 128],
                                    ewb[:, wb_, dc, half * 512:(half + 1) * 512],
                                    start=(dc == 0), stop=(dc == 7)))
                                for dc in range(8)],
                                reads=[("aoTg", u), ("ewb", u, wb_)], writes=[("pyE", u, pp)])
                        r.op("vector", lambda e, tt=tt, ei=ei, pE=pE: e.scalar_tensor_tensor(
                            moe[:, tt, :], pE[:], cm[:, tt:tt + 1, ei:ei + 1], moe[:, tt, :],
                            ALU.mult, ALU.add),
                             reads=[("pyE", u, tt % 2), ("cm", u)], writes=[("moe", u, tt)])
                for tt in range(16):
                    p = tt % 2
                    ta = t0 + tt
                    r.op("vector", lambda e, tt=tt: e.tensor_copy(mb3[:], moe[:, tt, :]),
                         reads=[("moe", u, tt)], writes=[("mb3", u)])
                    for dc in range(8):
                        r.op("tensor", lambda e, dc=dc: e.transpose(
                            pt_b3[:], mb3[:, dc * 128:(dc + 1) * 128], g["idsb"][:]),
                             reads=[("mb3", u), "idsb"], writes=[("ptb3", u)])
                        r.op("vector", lambda e, dc=dc: e.tensor_copy(moeT[:, dc, :], pt_b3[:]),
                             reads=[("ptb3", u)], writes=[("moeT", u)])
                    for half in range(2):
                        r.mmgroup([
                            (lambda e, dc=dc, half=half: e.matmul(
                                pyF[:, half * 512:(half + 1) * 512], moeT[:, dc, :],
                                fwb[:, dc, half * 512:(half + 1) * 512],
                                start=(dc == 0), stop=(dc == 7)))
                            for dc in range(8)],
                            reads=[("moeT", u), ("fwb", u)], writes=[("pyF", u)])
                    r.dma("sync", xnb3[:, p, :], W["xns"][ta * 128:(ta + 1) * 128, :],
                          writes=[("xnb3", u, p)], sem="d_xn3%d%s" % (p, su))
                    r.dma("sync", xt3[:, p, :], xin[ta * 128:(ta + 1) * 128, :],
                          writes=[("xt3", u, p)], sem="d_xt3%d%s" % (p, su))
                    r.op("vector", lambda e, p=p: e.tensor_add(xo3[:, p, :], pyF[:], fbias[:]),
                         reads=[("pyF", u), ("fbias", u)], writes=[("xo3", u, p)])
                    r.op("vector", lambda e, p=p: e.tensor_add(
                        xo3[:, p, :], xo3[:, p, :], xnb3[:, p, :]),
                         reads=[("xo3", u, p), ("xnb3", u, p)], writes=[("xo3", u, p)])
                    r.op("vector", lambda e, p=p: e.scalar_tensor_tensor(
                        xo3[:, p, :], xo3[:, p, :], 0.5, xt3[:, p, :], ALU.mult, ALU.add),
                         reads=[("xo3", u, p), ("xt3", u, p)], writes=[("xo3", u, p)])
                    r.dma("sync", xout[ta * 128:(ta + 1) * 128, :], xo3[:, p, :],
                          reads=[("xo3", u, p)], sem="d_of%d%s" % (p, su))
            r.barrier()
        r.emit(r.block)


def _decl_l1ffn_weights(nc, pref=""):
    return {
        "qkvwT": nc.dram_tensor(pref + "qkvwT", [D, 3 * D], F32, kind="ExternalInput").ap(),
        "bqkv_bc": nc.dram_tensor(pref + "bqkv_bc", [128, 3 * D], F32, kind="ExternalInput").ap(),
        "cosF": nc.dram_tensor(pref + "cosF", [128, 32, 64], F32, kind="ExternalInput").ap(),
        "sinF": nc.dram_tensor(pref + "sinF", [128, 32, 64], F32, kind="ExternalInput").ap(),
        "gwT": nc.dram_tensor(pref + "gwT", [D, 8], F32, kind="ExternalInput").ap(),
        "gb_bc": nc.dram_tensor(pref + "gb_bc", [128, 8], F32, kind="ExternalInput").ap(),
        "ewT": nc.dram_tensor(pref + "ewT", [NE, D, D], F32, kind="ExternalInput").ap(),
        "eb": nc.dram_tensor(pref + "eb", [NE, D], F32, kind="ExternalInput").ap(),
        "fwT": nc.dram_tensor(pref + "fwT", [D, D], F32, kind="ExternalInput").ap(),
        "fb_bc": nc.dram_tensor(pref + "fb_bc", [128, D], F32, kind="ExternalInput").ap(),
        "xns": nc.dram_tensor(pref + "xns", [T, D], BF16, kind="Internal").ap(),
        "xnTs": nc.dram_tensor(pref + "xnTs", [8, 128, T], BF16, kind="Internal").ap(),
        "qTs": nc.dram_tensor(pref + "qTs_f", [8, 128, T], BF16, kind="Internal").ap(),
        "aoTs": nc.dram_tensor(pref + "aoTs", [8, 128, T], BF16, kind="Internal").ap(),
    }


def _host_l1ffn_weights(inp, pref=""):
    qkvwT = np.ascontiguousarray(np.concatenate(
        [inp["L1_ffn_q_w"].T, inp["L1_ffn_k_w"].T, inp["L1_ffn_v_w"].T], axis=1))
    bqkv = np.concatenate([inp["L1_ffn_q_b"], inp["L1_ffn_k_b"], inp["L1_ffn_v_b"]])
    bqkv_bc = np.ascontiguousarray(np.broadcast_to(bqkv, (128, 3 * D)))
    inv = 1.0 / 10000.0 ** (np.arange(0, HD, 2, dtype=np.float32) / HD)
    freqs = np.outer(np.arange(T, dtype=np.float32), inv)
    emb = np.concatenate([freqs, freqs], axis=-1)
    cos = np.cos(emb).astype(np.float32).reshape(32, 128, 64).transpose(1, 0, 2)
    sin = np.sin(emb).astype(np.float32).reshape(32, 128, 64).transpose(1, 0, 2)
    return {
        pref + "qkvwT": qkvwT,
        pref + "bqkv_bc": bqkv_bc,
        pref + "cosF": np.ascontiguousarray(cos),
        pref + "sinF": np.ascontiguousarray(sin),
        pref + "gwT": np.ascontiguousarray(inp["L1_ffn_gate_w"].T),
        pref + "gb_bc": np.ascontiguousarray(np.broadcast_to(inp["L1_ffn_gate_b"], (128, 8))),
        pref + "ewT": np.ascontiguousarray(inp["L1_ffn_expert_w"].transpose(0, 2, 1)),
        pref + "eb": inp["L1_ffn_expert_b"],
        pref + "fwT": np.ascontiguousarray(inp["L1_ffn_ffn_w"].T),
        pref + "fb_bc": np.ascontiguousarray(np.broadcast_to(inp["L1_ffn_ffn_b"], (128, D))),
    }


def build_stageD():
    nc = bass.Bass("TRN2", target_bir_lowering=False)
    x = nc.dram_tensor("x", [T, D], F32, kind="ExternalInput").ap()
    out = nc.dram_tensor("out", [T, D], F32, kind="ExternalOutput").ap()
    W = _decl_l1ffn_weights(nc)
    with ExitStack() as es:
        r = Rec(nc, es)
        r.block = es.enter_context(nc.Block())
        g = _globals_setup(nc, es, r)
        emit_l1ffn(r, nc, g, x, out, W, "D")
        r.barrier()
        r.emit(r.block)
    return nc




def bcast16_g(ap2d, inner):
    return bass.AP(ap2d.tensor, ap2d.offset, [ap2d.ap[0], [0, 16], [1, inner]])


def emit_l1attn(r, nc, g, xin, xout, W, u):
    with ExitStack() as es:
        sb = lambda n, s, d: es.enter_context(nc.sbuf_tensor(n + u, s, d))
        scb = sb("scb", [128, 2], F32)
        r.op("vector", lambda e: e.memset(scb[:, 0:1], 0.125), writes=[("scb", u)])
        r.op("vector", lambda e: e.memset(scb[:, 1:2], -20.0), writes=[("scb", u)])
        kT = sb("kT", [128, 8, 4096], BF16)
        Vaug = sb("Vaug", [128, 32, 1040], BF16)
        Vv = Vaug[:].rearrange("p t (h c) -> p t h c", c=65)
        r.op("vector", lambda e: e.memset(Vv[:, :, :, 64:65], 1.0), writes=[("Vones", u)])

        with ExitStack() as e1:
            sb1 = lambda n, s, d: e1.enter_context(nc.sbuf_tensor(n + u, s, d))
            ps1 = lambda n, s, d: e1.enter_context(nc.psum_tensor(n + u, s, d))
            pt_b = ps1("ptb", [128, 128], BF16)
            xt = sb1("xt", [128, 2, 1024], F32)
            sqb = sb1("sq", [128, 1024], F32)
            xnt = sb1("xnt", [128, 2, 1024], BF16)
            stt = sb1("stt", [128, 2, 8], F32)
            xnTg = sb1("xnTg", [128, 8, 512], BF16)
            zrow = sb1("zrow", [1, 1024], BF16)
            r.op("vector", lambda e: e.memset(zrow[:], 0.0), writes=[("zrow", u)])
            r.dma("sync", W["vs"][0:1, :], zrow[:], reads=[("zrow", u)], sem="d_zr%s" % u)
            r.dma("sync", W["vs"][T + 1:T + 2, :], zrow[:], reads=[("zrow", u)], sem="d_zr%s" % u)
            for tg in range(8):
                for tt4 in range(4):
                    tt = tg * 4 + tt4
                    p = tt % 2
                    xkey = ("xt", u, p)
                    xtile = xt[:, p, :]
                    r.dma("sync", xtile, xin[tt * 128:(tt + 1) * 128, :], writes=[xkey],
                          sem="d_xt%d%s" % (p, u))
                    layernorm_tile(r, u, xtile, stt[:, p, :], sqb[:], ("stt", u, p), ("sq", u), xkey)
                    r.op("vector", lambda e, p=p: e.tensor_scalar(
                        xnt[:, p, :], xt[:, p, :], stt[:, p, 7:8], stt[:, p, 6:7], ALU.add, ALU.mult),
                         reads=[xkey, ("stt", u, p)], writes=[("xnt", u, p)])
                    for dc in range(8):
                        r.op("tensor", lambda e, p=p, dc=dc: e.transpose(
                            pt_b[:], xnt[:, p, dc * 128:(dc + 1) * 128], g["idsb"][:]),
                             reads=[("xnt", u, p), "idsb"], writes=[("ptb", u)])
                        r.op("vector", lambda e, dc=dc, tt4=tt4: e.tensor_copy(
                            xnTg[:, dc, tt4 * 128:(tt4 + 1) * 128], pt_b[:]),
                             reads=[("ptb", u)], writes=[("xnTg", u)])
                r.dma("sync", W["xnTs"][:, :, tg * 512:(tg + 1) * 512].rearrange("c p n -> p c n"),
                      xnTg[:], reads=[("xnTg", u)], sem="d_xnT%s" % u)
            r.barrier()
            r.emit(r.block)

        with ExitStack() as e1:
            sb1 = lambda n, s, d: e1.enter_context(nc.sbuf_tensor(n + u, s, d))
            ps1 = lambda n, s, d: e1.enter_context(nc.psum_tensor(n + u, s, d))
            pt_b = ps1("ptbb", [128, 128], BF16)
            pP = [ps1("pP%d" % i, [128, 512], F32) for i in range(2)]
            wpr = sb1("wpr", [128, 8, 1024], BF16)
            xnTg = sb1("xnTg2", [128, 2, 8, 512], BF16)
            bias_bc = sb1("bbc", [128, 1024], F32)
            tabs = sb1("tabs", [128, 2, 2, 32], F32)
            work = sb1("work", [128, 1024], F32)
            elu1 = sb1("elu1", [128, 1024], F32)
            tmp1 = sb1("tmp1", [128, 512], F32)
            tmp2 = sb1("tmp2", [128, 512], F32)
            rro = sb1("rro", [128, 1024], BF16)
            qstg = sb1("qstg", [128, 8, 512], BF16)
            vtile = sb1("vtile", [128, 2, 1024], BF16)

            for ip, pr in enumerate(["q", "k", "v"]):
                r.dma("gpsimd", wpr[:],
                      W["qkvwT"][:, ip * 1024:(ip + 1) * 1024].rearrange("(c p) n -> p c n", p=128),
                      writes=[("wpr", u)], sem="d_wp%s" % u)
                r.dma("sync", bias_bc[:], W["bqkv_bc"][:, ip * 1024:(ip + 1) * 1024],
                      writes=[("bbc", u)], sem="d_bbc%s" % u)
                for tg in range(8):
                    gp = tg % 2
                    r.dma("sync", xnTg[:, gp, :, :],
                          W["xnTs"][:, :, tg * 512:(tg + 1) * 512].rearrange("c p n -> p c n"),
                          writes=[("xnTg2", u, gp)], sem="d_xg%d%s" % (gp, u))
                    for tt4 in range(4):
                        tt = tg * 4 + tt4
                        if pr != "v":
                            r.dma("sync", tabs[:, tt % 2, 0, :], W["cosA"][:, tt, :],
                                  writes=[("tabs", u, tt % 2)], sem="d_tc%d%s" % (tt % 2, u))
                            r.dma("sync", tabs[:, tt % 2, 1, :], W["sinA"][:, tt, :],
                                  writes=[("tabs", u, tt % 2)], sem="d_ts%d%s" % (tt % 2, u))
                        for half in range(2):
                            bk = (tt4 * 2 + half) % 2
                            pk = ("pP%d" % bk, u)
                            r.mmgroup([
                                (lambda e, dc=dc, tt4=tt4, half=half, gp=gp, bk=bk: e.matmul(
                                    pP[bk][:], xnTg[:, gp, dc, tt4 * 128:(tt4 + 1) * 128],
                                    wpr[:, dc, half * 512:(half + 1) * 512],
                                    start=(dc == 0), stop=(dc == 7)))
                                for dc in range(8)],
                                reads=[("xnTg2", u, gp), ("wpr", u)], writes=[pk])
                            if pr == "v":
                                pv = pP[bk][:].rearrange("p (h c) -> p h c", c=64)
                                bv = bias_bc[:, half * 512:(half + 1) * 512].rearrange(
                                    "p (h c) -> p h c", c=64)
                                r.op("vector", lambda e, tt=tt, half=half, pv=pv, bv=bv: e.tensor_tensor(
                                    Vv[:, tt, half * 8:(half + 1) * 8, 0:64], pv, bv, ALU.add),
                                     reads=[pk, ("bbc", u)], writes=[("V", u)])
                                r.op("vector", lambda e, tt=tt, half=half, bk=bk, p=tt % 2: e.tensor_tensor(
                                    vtile[:, p, half * 512:(half + 1) * 512], pP[bk][:],
                                    bias_bc[:, half * 512:(half + 1) * 512], ALU.add),
                                     reads=[pk, ("bbc", u)], writes=[("vtile", u, tt % 2)])
                            else:
                                r.op("vector", lambda e, half=half, bk=bk: e.tensor_tensor(
                                    work[:, half * 512:(half + 1) * 512], pP[bk][:],
                                    bias_bc[:, half * 512:(half + 1) * 512], ALU.add),
                                     reads=[pk, ("bbc", u)], writes=[("work", u)])
                        if pr == "v":
                            p = tt % 2
                            r.dma("sync", W["vs"][1 + tt * 128:1 + (tt + 1) * 128, :],
                                  vtile[:, p, :], reads=[("vtile", u, p)],
                                  sem="d_vs%d%s" % (p, u))
                        else:
                            r.op("vector", lambda e: e.tensor_scalar_min(elu1[:], work[:], 0.0),
                                 reads=[("work", u)], writes=[("elu1", u)])
                            r.op("scalar", lambda e: e.activation(elu1[:], elu1[:], AF.Exp),
                                 reads=[("elu1", u)], writes=[("elu1", u)])
                            r.op("vector", lambda e: e.tensor_scalar_max(work[:], work[:], 0.0),
                                 reads=[("work", u)], writes=[("work", u)])
                            r.op("vector", lambda e: e.tensor_add(work[:], work[:], elu1[:]),
                                 reads=[("work", u), ("elu1", u)], writes=[("work", u)])
                            tp = tt % 2
                            wv = work[:].rearrange("p (h c two) -> p h c two", two=2, c=32)
                            rv = rro[:].rearrange("p (h c two) -> p h c two", two=2, c=32)
                            t1v = tmp1[:].rearrange("p (h c) -> p h c", c=32)
                            t2v = tmp2[:].rearrange("p (h c) -> p h c", c=32)
                            cb = bcast16_g(tabs[:, tp, 0, :], 32)
                            sb_ = bcast16_g(tabs[:, tp, 1, :], 32)
                            tk = ("tabs", u, tp)
                            r.op("vector", lambda e, cb=cb, wv=wv, t1v=t1v: e.tensor_tensor(
                                t1v, wv[:, :, :, 0], cb, ALU.mult),
                                 reads=[("work", u), tk], writes=[("tmp1", u)])
                            r.op("vector", lambda e, sb_=sb_, wv=wv, t2v=t2v: e.tensor_tensor(
                                t2v, wv[:, :, :, 1], sb_, ALU.mult),
                                 reads=[("work", u), tk], writes=[("tmp2", u)])
                            r.op("vector", lambda e, rv=rv, t1v=t1v, t2v=t2v: e.tensor_tensor(
                                rv[:, :, :, 0], t1v, t2v, ALU.subtract),
                                 reads=[("tmp1", u), ("tmp2", u)], writes=[("rro", u)])
                            r.op("vector", lambda e, sb_=sb_, wv=wv, t1v=t1v: e.tensor_tensor(
                                t1v, wv[:, :, :, 0], sb_, ALU.mult),
                                 reads=[("work", u), tk], writes=[("tmp1", u)])
                            r.op("vector", lambda e, cb=cb, wv=wv, t2v=t2v: e.tensor_tensor(
                                t2v, wv[:, :, :, 1], cb, ALU.mult),
                                 reads=[("work", u), tk], writes=[("tmp2", u)])
                            r.op("vector", lambda e, rv=rv, t1v=t1v, t2v=t2v: e.tensor_tensor(
                                rv[:, :, :, 1], t1v, t2v, ALU.add),
                                 reads=[("tmp1", u), ("tmp2", u)], writes=[("rro", u)])
                            for dc in range(8):
                                r.op("tensor", lambda e, dc=dc: e.transpose(
                                    pt_b[:], rro[:, dc * 128:(dc + 1) * 128], g["idsb"][:]),
                                     reads=[("rro", u), "idsb"], writes=[("ptb", u)])
                                if pr == "q":
                                    r.op("vector", lambda e, dc=dc, tt4=tt4: e.tensor_copy(
                                        qstg[:, dc, tt4 * 128:(tt4 + 1) * 128], pt_b[:]),
                                         reads=[("ptb", u)], writes=[("qstg", u)])
                                else:
                                    r.op("vector", lambda e, dc=dc, tt=tt: e.tensor_copy(
                                        kT[:, dc, tt * 128:(tt + 1) * 128], pt_b[:]),
                                         reads=[("ptb", u)], writes=[("kT", u)])
                    if pr == "q":
                        r.dma("sync", W["qTs"][:, :, tg * 512:(tg + 1) * 512].rearrange("c p n -> p c n"),
                              qstg[:], reads=[("qstg", u)], sem="d_qs%s" % u)
            r.barrier()
            r.emit(r.block)

        with ExitStack() as e2:
            sb2 = lambda n, s, d: e2.enter_context(nc.sbuf_tensor(n + u, s, d))
            ps2 = lambda n, s, d: e2.enter_context(nc.psum_tensor(n + u, s, d))
            pS = [ps2("pS%d" % i, [128, 1024], F32) for i in range(2)]
            pOT = [ps2("pOT%d" % i, [128, 512], F32) for i in range(1)]
            pt_f = ps2("ptf", [128, 128], F32)
            qTg = sb2("qTg", [128, 1, 8, 512], BF16)
            ET = sb2("ET", [128, 2, 1024], BF16)
            oT = sb2("oT", [128, 2, 512], F32)
            rz = sb2("rz", [128, 2], F32)
            ao = sb2("ao", [128, 4, 1024], BF16)
            vle = sb2("vle", [128, 2, 3, 1024], BF16)
            wl = sb2("wl", [128, 3, 1024], BF16)
            bl = sb2("bl", [128, 1024], BF16)
            lep = sb2("lep", [128, 2, 1024], F32)
            xt2 = sb2("xt2", [128, 2, 1024], F32)
            xo2 = sb2("xo2", [128, 2, 1024], F32)

            r.dma("sync", wl[:], W["wl_bc"], writes=[("wl", u)], sem="d_wl%s" % u)
            r.dma("sync", bl[:], W["bl_bc"], writes=[("bl", u)], sem="d_bl%s" % u)

            for qg in range(8):
                gp = 0
                r.dma("sync", qTg[:, 0, :, :],
                      W["qTs"][:, :, qg * 512:(qg + 1) * 512].rearrange("c p n -> p c n"),
                      writes=[("qTg", u, gp)], sem="d_qg%s" % u)
                for h in range(16):
                    hp = h // 2
                    h2 = h % 2
                    base = 64 * h2
                    ok = ("pOT0", u)
                    for kt2 in range(16):
                        bank = kt2 % 2
                        for sub in range(2):
                            kt = 2 * kt2 + sub
                            r.mmgroup([(lambda e, hp=hp, kt=kt, bank=bank, base=base, sub=sub: e.matmul(
                                pS[bank][:, sub * 512:(sub + 1) * 512],
                                kT[base:base + 64, hp, kt * 128:(kt + 1) * 128],
                                qTg[base:base + 64, 0, hp, :], start=True, stop=True))],
                                reads=[("qTg", u, gp), ("kT", u)], writes=[("pS", u, bank, sub)])
                        ekey = ("ET", u, bank)
                        r.op("scalar", lambda e, bank=bank: e.activation(
                            ET[:, bank, :], pS[bank][:], AF.Exp, scale=scb[:, 0:1],
                            bias=scb[:, 1:2]),
                             reads=[("pS", u, bank, 0), ("pS", u, bank, 1), ("scb", u)],
                             writes=[ekey])
                        if kt2 > 0:
                            pb = 1 - bank
                            for sub in range(2):
                                kt = 2 * (kt2 - 1) + sub
                                r.mmgroup([(lambda e, h=h, kt=kt, pb=pb, sub=sub: e.matmul(
                                    pOT[0][0:65, :], Vaug[:, kt, h * 65:(h + 1) * 65],
                                    ET[:, pb, sub * 512:(sub + 1) * 512],
                                    start=(kt == 0), stop=False))],
                                    reads=[("ET", u, pb), ("V", u), ("Vones", u)], writes=[ok])
                    for sub in range(2):
                        kt = 30 + sub
                        r.mmgroup([(lambda e, h=h, kt=kt, sub=sub: e.matmul(
                            pOT[0][0:65, :], Vaug[:, kt, h * 65:(h + 1) * 65],
                            ET[:, 1, sub * 512:(sub + 1) * 512],
                            start=False, stop=(sub == 1)))],
                            reads=[("ET", u, 1), ("V", u), ("Vones", u)], writes=[ok])
                    r.op("vector", lambda e, h2=h2: e.tensor_copy(oT[0:65, h2, :], pOT[0][0:65, :]),
                         reads=[ok], writes=[("oT", u, h2)])
                    for qs in range(4):
                        r.op("tensor", lambda e, h2=h2, qs=qs: e.transpose(
                            pt_f[:, 0:65], oT[0:65, h2, qs * 128:(qs + 1) * 128],
                            g["idsf"][0:65, 0:65]),
                             reads=[("oT", u, h2), "idsf"], writes=[("ptf", u)])
                        r.op("vector", lambda e, h2=h2: e.reciprocal(
                            rz[:, h2:h2 + 1], pt_f[:, 64:65]),
                             reads=[("ptf", u)], writes=[("rz", u, h2)])
                        r.op("vector", lambda e, h2=h2, qs=qs, h=h: e.tensor_scalar(
                            ao[:, qs, h * 64:(h + 1) * 64], pt_f[:, 0:64],
                            rz[:, h2:h2 + 1], None, ALU.mult),
                             reads=[("ptf", u), ("rz", u, h2)], writes=[("ao", u)])
                for qs in range(4):
                    tt = qg * 4 + qs
                    p = tt % 2
                    lk = ("vle", u, p)
                    r.dma("sync", vle[:, p, 0, :], W["vs"][tt * 128:(tt + 1) * 128, :],
                          writes=[lk], sem="d_vm%d%s" % (p, u))
                    r.dma("sync", vle[:, p, 1, :], W["vs"][1 + tt * 128:1 + (tt + 1) * 128, :],
                          writes=[lk], sem="d_v0%d%s" % (p, u))
                    r.dma("sync", vle[:, p, 2, :], W["vs"][2 + tt * 128:2 + (tt + 1) * 128, :],
                          writes=[lk], sem="d_vp%d%s" % (p, u))
                    r.dma("sync", xt2[:, p, :], xin[tt * 128:(tt + 1) * 128, :],
                          writes=[("xt2", u, p)], sem="d_x2%d%s" % (p, u))
                    r.op("vector", lambda e, p=p: e.tensor_tensor(
                        lep[:, p, :], vle[:, p, 0, :], wl[:, 0, :], ALU.mult),
                         reads=[lk, ("wl", u)], writes=[("lep", u, p)])
                    r.op("vector", lambda e, p=p: e.tensor_tensor(
                        xo2[:, p, :], vle[:, p, 1, :], wl[:, 1, :], ALU.mult),
                         reads=[lk, ("wl", u)], writes=[("xo2", u, p)])
                    r.op("vector", lambda e, p=p: e.tensor_add(
                        lep[:, p, :], lep[:, p, :], xo2[:, p, :]),
                         reads=[("lep", u, p), ("xo2", u, p)], writes=[("lep", u, p)])
                    r.op("vector", lambda e, p=p: e.tensor_tensor(
                        xo2[:, p, :], vle[:, p, 2, :], wl[:, 2, :], ALU.mult),
                         reads=[lk, ("wl", u)], writes=[("xo2", u, p)])
                    r.op("vector", lambda e, p=p: e.tensor_add(
                        lep[:, p, :], lep[:, p, :], xo2[:, p, :]),
                         reads=[("lep", u, p), ("xo2", u, p)], writes=[("lep", u, p)])
                    r.op("vector", lambda e, p=p: e.tensor_add(
                        lep[:, p, :], lep[:, p, :], bl[:]),
                         reads=[("lep", u, p), ("bl", u)], writes=[("lep", u, p)])
                    r.op("vector", lambda e, p=p, qs=qs: e.tensor_add(
                        lep[:, p, :], lep[:, p, :], ao[:, qs, :]),
                         reads=[("lep", u, p), ("ao", u)], writes=[("lep", u, p)])
                    r.op("vector", lambda e, p=p: e.tensor_add(
                        xo2[:, p, :], lep[:, p, :], xt2[:, p, :]),
                         reads=[("lep", u, p), ("xt2", u, p)], writes=[("xo2", u, p)])
                    r.dma("sync", xout[tt * 128:(tt + 1) * 128, :], xo2[:, p, :],
                          reads=[("xo2", u, p)], sem="d_oe%d%s" % (p, u))
            r.barrier()
            r.emit(r.block)


def _decl_l1attn_weights(nc, pref=""):
    return {
        "qkvwT": nc.dram_tensor(pref + "aqkvwT", [D, 3 * D], F32, kind="ExternalInput").ap(),
        "bqkv_bc": nc.dram_tensor(pref + "abqkv_bc", [128, 3 * D], F32, kind="ExternalInput").ap(),
        "cosA": nc.dram_tensor(pref + "cosA", [128, 32, 32], F32, kind="ExternalInput").ap(),
        "sinA": nc.dram_tensor(pref + "sinA", [128, 32, 32], F32, kind="ExternalInput").ap(),
        "wl_bc": nc.dram_tensor(pref + "wl_bc", [128, 3, D], BF16, kind="ExternalInput").ap(),
        "bl_bc": nc.dram_tensor(pref + "bl_bc", [128, D], BF16, kind="ExternalInput").ap(),
        "xnTs": nc.dram_tensor(pref + "xnTs_e", [8, 128, T], BF16, kind="Internal").ap(),
        "qTs": nc.dram_tensor(pref + "qTs_e", [8, 128, T], BF16, kind="Internal").ap(),
        "vs": nc.dram_tensor(pref + "vs_e", [T + 2, D], BF16, kind="Internal").ap(),
    }


def _host_l1attn_weights(inp, pref=""):
    w = inp["L1_attn_w_qkv"]
    b = inp["L1_attn_b_qkv"]
    qkvwT = np.ascontiguousarray(w.T)
    bqkv_bc = np.ascontiguousarray(np.broadcast_to(b, (128, 3 * D)))
    pos = np.arange(T, dtype=np.float32)[:, None]
    div = np.exp(np.arange(0, HD, 2, dtype=np.float32) * (-np.log(10000.0) / HD))
    ang = (pos * div).astype(np.float32)
    cos = np.cos(ang).astype(np.float32).reshape(32, 128, 32).transpose(1, 0, 2)
    sin = np.sin(ang).astype(np.float32).reshape(32, 128, 32).transpose(1, 0, 2)
    wle = inp["L1_attn_w_lepe"].reshape(D, 3)
    wl_bc = np.ascontiguousarray(np.broadcast_to(wle.T[:, None, :], (3, 128, D)).transpose(1, 0, 2))
    bl_bc = np.ascontiguousarray(np.broadcast_to(inp["L1_attn_b_lepe"], (128, D)))
    return {
        pref + "aqkvwT": qkvwT,
        pref + "abqkv_bc": bqkv_bc,
        pref + "cosA": np.ascontiguousarray(cos),
        pref + "sinA": np.ascontiguousarray(sin),
        pref + "wl_bc": wl_bc.astype(BF),
        pref + "bl_bc": bl_bc.astype(BF),
    }


def build_stageE():
    nc = bass.Bass("TRN2", target_bir_lowering=False)
    x = nc.dram_tensor("x", [T, D], F32, kind="ExternalInput").ap()
    out = nc.dram_tensor("out", [T, D], F32, kind="ExternalOutput").ap()
    W = _decl_l1attn_weights(nc)
    with ExitStack() as es:
        r = Rec(nc, es)
        r.block = es.enter_context(nc.Block())
        g = _globals_setup(nc, es, r)
        emit_l1attn(r, nc, g, x, out, W, "E")
        r.barrier()
        r.emit(r.block)
    return nc


def build_fused():
    nc = bass.Bass("TRN2", target_bir_lowering=False)
    x = nc.dram_tensor("x", [T, D], F32, kind="ExternalInput").ap()
    out = nc.dram_tensor("out", [T, D], F32, kind="ExternalOutput").ap()
    xb0 = nc.dram_tensor("xb0", [T, D], F32, kind="Internal").ap()
    xb1 = nc.dram_tensor("xb1", [T, D], F32, kind="Internal").ap()
    WA = _decl_l0ffn_weights(nc)
    WB = _decl_l0attn_weights(nc)
    WD = _decl_l1ffn_weights(nc)
    WE = _decl_l1attn_weights(nc)
    with ExitStack() as es:
        r = Rec(nc, es)
        r.block = es.enter_context(nc.Block())
        g = _globals_setup(nc, es, r)
        emit_l0ffn(r, nc, g, x, xb0, WA, "A")
        emit_l0attn(r, nc, g, xb0, xb1, WB, "B")
        emit_l0ffn(r, nc, g, xb1, xb0, WA, "C", su="A")
        emit_l1ffn(r, nc, g, xb0, xb1, WD, "D")
        emit_l1attn(r, nc, g, xb1, xb0, WE, "E")
        emit_l1ffn(r, nc, g, xb0, out, WD, "F", su="D")
        r.barrier()
        r.emit(r.block)
    return nc


def kernel(**inputs):
    inputs = {k: np.asarray(v) for k, v in inputs.items()}
    wm = _host_globals()
    wm.update(_host_l0ffn_weights(inputs))
    wm.update(_host_l0attn_weights(inputs))
    wm.update(_host_l1ffn_weights(inputs))
    wm.update(_host_l1attn_weights(inputs))
    gum = _host_gumbel()
    nc = build_fused()
    in_maps = []
    for c in range(CORES):
        im = dict(wm)
        im["x"] = np.ascontiguousarray(inputs["L0_attn_hidden_states"][c])
        im["gum"] = gum[c]
        in_maps.append(im)
    res = run_bass_kernel_spmd(nc, in_maps, core_ids=list(range(CORES)))
    return np.stack([np.asarray(res.results[c]["out"]) for c in range(CORES)]).astype(np.float32)

# version 12 (final)
# speedup vs baseline: 1.1337x; 1.0500x over previous
import numpy as np
import ml_dtypes
from contextlib import ExitStack

import concourse.bass as bass
import concourse.mybir as mybir
from concourse.bass_utils import run_bass_kernel_spmd

F32 = mybir.dt.float32
BF16 = mybir.dt.bfloat16
AF = mybir.ActivationFunctionType
ALU = mybir.AluOpType
AX = mybir.AxisListType.X
BF = ml_dtypes.bfloat16

CORES = 8
T = 4096
D = 1024
NH, HD = 16, 64
NE = 8
DHID = 4096
NEG = -1.0e9

ENGS = ("tensor", "vector", "scalar", "gpsimd", "sync")


class Rec:

    def __init__(self, nc, es, block=None):
        self.nc = nc
        self.es = es
        self.block = block
        self.sems = {}
        self.cnt = {}
        self.ev = {e: [] for e in ENGS}
        self.waited = {e: {} for e in ENGS}
        self.lw = {}
        self.lr = {}
        for e in ENGS:
            self._mksem("q_" + e)

    def _mksem(self, name):
        self.sems[name] = self.es.enter_context(self.nc.semaphore(name))
        self.cnt[name] = 0

    def sem(self, name):
        if name not in self.sems:
            self._mksem(name)
        return name

    def _wait(self, eng, tok):
        s, v = tok
        if self.waited[eng].get(s, 0) >= v:
            return
        self.waited[eng][s] = v
        h = self.sems[s]
        self.ev[eng].append(lambda e, h=h, v=v: e.wait_ge(h, v))

    def _deps(self, eng, reads, writes):
        toks = []
        for b in reads:
            t = self.lw.get(b)
            if t is not None:
                toks.append(t)
        for b in writes:
            t = self.lw.get(b)
            if t is not None:
                toks.append(t)
            toks.extend(self.lr.get(b, {}).items())
        for t in sorted(set(toks)):
            self._wait(eng, t)

    def _register(self, tok, reads, writes):
        if tok is None:
            return
        s, v = tok
        for b in reads:
            d = self.lr.setdefault(b, {})
            d[s] = max(d.get(s, 0), v)
        for b in writes:
            self.lw[b] = tok
            self.lr[b] = {}

    def op(self, eng, fn, reads=(), writes=(), sig=True):
        self._deps(eng, reads, writes)
        if sig:
            s = "q_" + eng
            self.cnt[s] += 1
            v = self.cnt[s]
            h = self.sems[s]
            self.ev[eng].append(lambda e, fn=fn, h=h: fn(e).then_inc(h, 1))
            tok = (s, v)
        else:
            self.ev[eng].append(lambda e, fn=fn: fn(e))
            tok = None
        self._register(tok, reads, writes)
        return tok

    def mmgroup(self, fns, reads=(), writes=()):
        self._deps("tensor", reads, writes)
        for fn in fns[:-1]:
            self.ev["tensor"].append(lambda e, fn=fn: fn(e))
        s = "q_tensor"
        self.cnt[s] += 1
        v = self.cnt[s]
        h = self.sems[s]
        fnl = fns[-1]
        self.ev["tensor"].append(lambda e, fn=fnl, h=h: fn(e).then_inc(h, 1))
        tok = (s, v)
        self._register(tok, reads, writes)
        return tok

    def dma(self, q, out, in_, reads=(), writes=(), sem=None, **kw):
        if sem is None:
            key = writes[0] if writes else (reads[0] if reads else "misc")
            sem = "d_" + str(key)
        s = self.sem(sem)
        self._deps(q, reads, writes)
        self.cnt[s] += 16
        v = self.cnt[s]
        h = self.sems[s]
        self.ev[q].append(
            lambda e, out=out, in_=in_, h=h, kw=kw: e.dma_start(out, in_, **kw).then_inc(h, 16))
        tok = (s, v)
        self._register(tok, reads, writes)
        return tok

    def barrier(self):
        toks = [(s, c) for s, c in self.cnt.items() if c > 0]
        for e in ENGS:
            for t in toks:
                self._wait(e, t)

    def emit(self, block):
        for e in ENGS:
            if self.ev[e]:
                evs = self.ev[e]
                self.ev[e] = []

                def section(eng, evs=evs):
                    for f in evs:
                        f(eng)

                getattr(block, e)(section)


def layernorm_tile(r, u, xtile, stt, sqrow, sk, sq_key, xt_key):
    r.op("vector", lambda e: e.reduce_sum(stt[:, 0:1], xtile, axis=AX), reads=[xt_key], writes=[sk])
    r.op("scalar", lambda e: e.activation(sqrow, xtile, AF.Square), reads=[xt_key], writes=[sq_key])
    r.op("vector", lambda e: e.reduce_sum(stt[:, 1:2], sqrow, axis=AX), reads=[sq_key], writes=[sk])
    r.op("vector", lambda e: e.tensor_scalar_mul(stt[:, 2:3], stt[:, 0:1], 1.0 / D), reads=[sk], writes=[sk])
    r.op("vector", lambda e: e.tensor_scalar_mul(stt[:, 3:4], stt[:, 1:2], 1.0 / D), reads=[sk], writes=[sk])
    r.op("vector", lambda e: e.tensor_tensor(stt[:, 4:5], stt[:, 2:3], stt[:, 2:3], ALU.mult), reads=[sk], writes=[sk])
    r.op("vector", lambda e: e.tensor_sub(stt[:, 4:5], stt[:, 3:4], stt[:, 4:5]), reads=[sk], writes=[sk])
    r.op("vector", lambda e: e.tensor_scalar_add(stt[:, 4:5], stt[:, 4:5], 1e-5), reads=[sk], writes=[sk])
    r.op("scalar", lambda e: e.activation(stt[:, 5:6], stt[:, 4:5], AF.Sqrt), reads=[sk], writes=[sk])
    r.op("vector", lambda e: e.reciprocal(stt[:, 6:7], stt[:, 5:6]), reads=[sk], writes=[sk])
    r.op("vector", lambda e: e.tensor_scalar_mul(stt[:, 7:8], stt[:, 2:3], -1.0), reads=[sk], writes=[sk])


def emit_l0ffn(r, nc, g, xin, xout, W, u, nexperts=NE, su=None):
    su = su or u
    with ExitStack() as es:
        sb = lambda n, s, d: es.enter_context(nc.sbuf_tensor(n + u, s, d))
        ps = lambda n, s, d: es.enter_context(nc.psum_tensor(n + u, s, d))

        xnT = sb("xnT", [128, 8, 2048], BF16)
        y_acc = sb("yacc", [128, 16, 1024], F32)
        c_g = sb("cg", [128, 16, 8], F32)
        cT = sb("cT", [8, 2048], F32)
        w0b = sb("w0b", [128, 2, 8, 512], BF16)
        w1b = sb("w1b", [128, 2, 8, 512], BF16)
        w2b = sb("w2b", [128, 2, 4, 1024], BF16)
        wg_b = sb("wgb", [128, 8, 8], BF16)
        bg_bc = sb("bgbc", [128, 8], F32)
        b0_pp = sb("b0pp", [128, 8, 32], F32)
        b1_pp = sb("b1pp", [128, 8, 32], F32)
        b2_sb = sb("b2sb", [8, 1024], F32)
        xt = sb("xt", [128, 2, 1024], F32)
        sqb = sb("sq", [128, 2, 1024], F32)
        xnt = sb("xnt", [128, 2, 1024], BF16)
        stt = sb("stt", [128, 2, 8], F32)
        s_sb = sb("ssb", [128, 2, 512], F32)
        pTb = sb("pTb", [128, 2, 4, 512], BF16)
        xo = sb("xo", [128, 2, 1024], F32)
        gt = sb("gt", [128, 2, 8], F32)

        pt_b = ps("ptb", [128, 128], BF16)
        ph = [ps("ph%d" % i, [128, 512], F32) for i in range(2)]
        pg = [ps("pg%d" % i, [128, 512], F32) for i in range(2)]
        py = ps("py", [128, 1024], F32)

        r.dma("gpsimd", wg_b[:], W["Wg"].rearrange("(c p) n -> p c n", p=128),
              writes=[("wgb", u)], sem="d_cwg" + su)
        r.dma("sync", bg_bc[:], W["bg_bc"], writes=[("bgbc", u)], sem="d_cbg" + su)
        r.dma("sync", b0_pp[:], W["b0_pp"], writes=[("b0pp", u)], sem="d_cb0" + su)
        r.dma("sync", b1_pp[:], W["b1_pp"], writes=[("b1pp", u)], sem="d_cb1" + su)
        r.dma("sync", b2_sb[:], W["b2"], writes=[("b2sb", u)], sem="d_cb2" + su)

        for G in range(2):
            t0 = G * 16
            KX = ("xnT", u)
            for tt in range(16):
                p = tt % 2
                xkey = ("xt", u, p)
                xtile = xt[:, p, :]
                r.dma("sync", xtile, xin[(t0 + tt) * 128:(t0 + tt + 1) * 128, :],
                      writes=[xkey], sem="d_xt%d%s" % (p, su))
                layernorm_tile(r, u, xtile, stt[:, p, :], sqb[:, p, :], ("stt", u, p), ("sq", u, p), xkey)
                sk = ("stt", u, p)
                r.op("vector", lambda e, p=p: e.tensor_scalar(
                    xnt[:, p, :], xt[:, p, :], stt[:, p, 7:8], stt[:, p, 6:7], ALU.add, ALU.mult),
                     reads=[xkey, sk], writes=[("xnt", u, p)])
                for dc in range(8):
                    r.op("tensor", lambda e, p=p, dc=dc: e.transpose(
                        pt_b[:], xnt[:, p, dc * 128:(dc + 1) * 128], g["idsb"][:]),
                         reads=[("xnt", u, p), "idsb"], writes=[("ptb", u)])
                    r.op("vector", lambda e, dc=dc, tt=tt: e.tensor_copy(
                        xnT[:, dc, tt * 128:(tt + 1) * 128], pt_b[:]),
                         reads=[("ptb", u)], writes=[KX])

            for tt in range(16):
                p = tt % 2
                gk = ("gt", u, p)
                sk = ("stt", u, p)
                qk = ("sq", u, p)
                r.mmgroup([
                    (lambda e, dc=dc, tt=tt: e.matmul(
                        ph[0][:, 0:8], xnT[:, dc, tt * 128:(tt + 1) * 128], wg_b[:, dc, :],
                        start=(dc == 0), stop=(dc == 7)))
                    for dc in range(8)],
                    reads=[KX, ("wgb", u)], writes=[("ph0", u)])
                r.op("vector", lambda e, p=p: e.tensor_add(gt[:, p, :], ph[0][:, 0:8], bg_bc[:]),
                     reads=[("ph0", u), ("bgbc", u)], writes=[gk])
                r.op("vector", lambda e, p=p: e.reduce_max(stt[:, p, 0:1], gt[:, p, :], axis=AX),
                     reads=[gk], writes=[sk])
                r.op("vector", lambda e, p=p: e.tensor_scalar(
                    sqb[:, p, 0:8], gt[:, p, :], stt[:, p, 0:1], None, ALU.is_equal),
                     reads=[gk, sk], writes=[qk])
                r.op("vector", lambda e, p=p: e.scalar_tensor_tensor(
                    sqb[:, p, 8:16], sqb[:, p, 0:8], -1e30, gt[:, p, :], ALU.mult, ALU.add),
                     reads=[qk, gk], writes=[qk])
                r.op("vector", lambda e, p=p: e.reduce_max(stt[:, p, 1:2], sqb[:, p, 8:16], axis=AX),
                     reads=[qk], writes=[sk])
                r.op("vector", lambda e, p=p: e.tensor_scalar(
                    sqb[:, p, 16:24], sqb[:, p, 8:16], stt[:, p, 1:2], None, ALU.is_equal),
                     reads=[qk, sk], writes=[qk])
                r.op("vector", lambda e, p=p: e.tensor_scalar_mul(stt[:, p, 2:3], stt[:, p, 0:1], -1.0),
                     reads=[sk], writes=[sk])
                r.op("scalar", lambda e, p=p: e.activation(
                    stt[:, p, 3:4], stt[:, p, 1:2], AF.Exp, bias=stt[:, p, 2:3]),
                     reads=[sk], writes=[sk])
                r.op("vector", lambda e, p=p: e.tensor_scalar_add(stt[:, p, 4:5], stt[:, p, 3:4], 1.0),
                     reads=[sk], writes=[sk])
                r.op("vector", lambda e, p=p: e.reciprocal(stt[:, p, 4:5], stt[:, p, 4:5]),
                     reads=[sk], writes=[sk])
                r.op("vector", lambda e, p=p: e.tensor_scalar_mul(stt[:, p, 5:6], stt[:, p, 4:5], 0.5),
                     reads=[sk], writes=[sk])
                r.op("vector", lambda e, p=p: e.tensor_tensor(
                    stt[:, p, 6:7], stt[:, p, 3:4], stt[:, p, 5:6], ALU.mult),
                     reads=[sk], writes=[sk])
                r.op("vector", lambda e, p=p: e.tensor_scalar(
                    sqb[:, p, 24:32], sqb[:, p, 0:8], stt[:, p, 5:6], None, ALU.mult),
                     reads=[qk, sk], writes=[qk])
                r.op("vector", lambda e, p=p, tt=tt: e.scalar_tensor_tensor(
                    c_g[:, tt, :], sqb[:, p, 16:24], stt[:, p, 6:7], sqb[:, p, 24:32], ALU.mult, ALU.add),
                     reads=[qk, sk], writes=[("cg", u)])
                r.op("tensor", lambda e, tt=tt: e.transpose(
                    pg[0][0:8, 0:128], c_g[:, tt, :], g["idsf"][:]),
                     reads=[("cg", u), "idsf"], writes=[("pg0", u)])
                r.op("vector", lambda e, tt=tt: e.tensor_copy(
                    cT[:, tt * 128:(tt + 1) * 128], pg[0][0:8, 0:128]),
                     reads=[("pg0", u)], writes=[("cTb", u)])

            for td in range(16):
                for half in range(2):
                    r.mmgroup([lambda e, td=td, half=half: e.matmul(
                        py[:, half * 512:(half + 1) * 512], cT[:, td * 128:(td + 1) * 128],
                        b2_sb[:, half * 512:(half + 1) * 512], start=True, stop=True)],
                        reads=[("cTb", u), ("b2sb", u)], writes=[("py", u)])
                r.op("vector", lambda e, td=td: e.tensor_copy(y_acc[:, td, :], py[:]),
                     reads=[("py", u)], writes=[("ya", u, td)])

            def emit_td_unit(unit):
                ei_, ts_, td_, buf_ = unit
                gtd = ts_ * 4 + td_
                for half in range(2):
                    r.mmgroup([
                        (lambda e, jc=jc, td=td_, half=half, ts=ts_, buf=buf_: e.matmul(
                            py[:, half * 512:(half + 1) * 512],
                            pTb[:, ts % 2, jc, td * 128:(td + 1) * 128],
                            w2b[:, buf, jc, half * 512:(half + 1) * 512],
                            start=(jc == 0), stop=(jc == 3)))
                        for jc in range(4)],
                        reads=[("pT", u, ts_ % 2, 0), ("pT", u, ts_ % 2, 1),
                               ("pT", u, ts_ % 2, 2), ("pT", u, ts_ % 2, 3),
                               ("w2b", u, buf_)],
                        writes=[("py", u)])
                r.op("vector", lambda e, gtd=gtd, ei=ei_: e.scalar_tensor_tensor(
                    y_acc[:, gtd, :], py[:], c_g[:, gtd:gtd + 1, ei:ei + 1],
                    y_acc[:, gtd, :], ALU.mult, ALU.add),
                     reads=[("py", u), ("cg", u)], writes=[("ya", u, gtd)])

            pend = []
            for ei in range(nexperts):
                for j in range(8):
                    buf = (ei * 8 + j) % 2
                    js, je = j * 512, (j + 1) * 512
                    r.dma("gpsimd", w0b[:, buf, :, :], W["W0"][ei, j],
                          writes=[("w0b", u, buf)], sem="d_w0%d%s" % (buf, su))
                    r.dma("gpsimd", w1b[:, buf, :, :], W["W1"][ei, j],
                          writes=[("w1b", u, buf)], sem="d_w1%d%s" % (buf, su))
                    r.dma("gpsimd", w2b[:, buf, :, :], W["W2"][ei, j],
                          writes=[("w2b", u, buf)], sem="d_w2%d%s" % (buf, su))
                    for ts in range(4):
                        ns, ne_ = ts * 512, (ts + 1) * 512
                        for jc in range(4):
                            bk = (jc + ts) % 2
                            hp, gp = ph[bk], pg[bk]
                            hkey, gkey = ("ph%d" % bk, u), ("pg%d" % bk, u)
                            r.mmgroup([
                                (lambda e, dc=dc, jc=jc, hp=hp, buf=buf, ns=ns, ne_=ne_: e.matmul(
                                    hp[:], w0b[:, buf, dc, jc * 128:(jc + 1) * 128],
                                    xnT[:, dc, ns:ne_], start=(dc == 0), stop=(dc == 7)))
                                for dc in range(8)],
                                reads=[("w0b", u, buf), KX], writes=[hkey])
                            if pend:
                                emit_td_unit(pend.pop(0))
                            r.mmgroup([
                                (lambda e, dc=dc, jc=jc, gp=gp, buf=buf, ns=ns, ne_=ne_: e.matmul(
                                    gp[:], w1b[:, buf, dc, jc * 128:(jc + 1) * 128],
                                    xnT[:, dc, ns:ne_], start=(dc == 0), stop=(dc == 7)))
                                for dc in range(8)],
                                reads=[("w1b", u, buf), KX], writes=[gkey])
                            sp = jc % 2
                            jj = j * 4 + jc
                            r.op("scalar", lambda e, gp=gp, sp=sp, ei=ei, jj=jj: e.activation(
                                s_sb[:, sp, :], gp[:], AF.Silu, bias=b1_pp[:, ei:ei + 1, jj:jj + 1]),
                                 reads=[gkey, ("b1pp", u)], writes=[("ssb", u, sp)])
                            r.op("vector", lambda e, hp=hp, sp=sp, ei=ei, jj=jj, ts=ts, jc=jc: e.scalar_tensor_tensor(
                                pTb[:, ts % 2, jc, :], hp[:], b0_pp[:, ei:ei + 1, jj:jj + 1],
                                s_sb[:, sp, :], ALU.add, ALU.mult),
                                 reads=[hkey, ("b0pp", u), ("ssb", u, sp)],
                                 writes=[("pT", u, ts % 2, jc)])
                        for td in range(4):
                            pend.append((ei, ts, td, buf))
            for unit in pend:
                emit_td_unit(unit)
            pend = []

            for td in range(16):
                p = td % 2
                xkey = ("xt", u, p)
                xtile = xt[:, p, :]
                r.dma("sync", xtile, xin[(t0 + td) * 128:(t0 + td + 1) * 128, :],
                      writes=[xkey], sem="d_xt%d%s" % (p, su))
                r.op("vector", lambda e, xtile=xtile, td=td, p=p: e.tensor_add(
                    xo[:, p, :], xtile, y_acc[:, td, :]),
                     reads=[xkey, ("ya", u, td)], writes=[("xo", u, p)])
                r.dma("sync", xout[(t0 + td) * 128:(t0 + td + 1) * 128, :], xo[:, p, :],
                      reads=[("xo", u, p)], sem="d_so%d%s" % (p, su))
            r.barrier()
        r.emit(r.block)


def _decl_l0ffn_weights(nc, pref=""):
    W = {
        "Wg": nc.dram_tensor(pref + "Wg", [D, NE], F32, kind="ExternalInput").ap(),
        "W0": nc.dram_tensor(pref + "W0", [NE, 8, 128, 8, 512], F32, kind="ExternalInput").ap(),
        "W1": nc.dram_tensor(pref + "W1", [NE, 8, 128, 8, 512], F32, kind="ExternalInput").ap(),
        "W2": nc.dram_tensor(pref + "W2", [NE, 8, 128, 4, 1024], F32, kind="ExternalInput").ap(),
        "b2": nc.dram_tensor(pref + "b2", [NE, D], F32, kind="ExternalInput").ap(),
        "bg_bc": nc.dram_tensor(pref + "bg_bc", [128, 8], F32, kind="ExternalInput").ap(),
        "b0_pp": nc.dram_tensor(pref + "b0_pp", [128, 8, 32], F32, kind="ExternalInput").ap(),
        "b1_pp": nc.dram_tensor(pref + "b1_pp", [128, 8, 32], F32, kind="ExternalInput").ap(),
    }
    return W


def _host_l0ffn_weights(inp, pref=""):
    b0 = np.ascontiguousarray(inp["L0_ffn_b0"].reshape(8, 32, 128).transpose(2, 0, 1))
    b1 = np.ascontiguousarray(inp["L0_ffn_b1"].reshape(8, 32, 128).transpose(2, 0, 1))
    w0h = np.ascontiguousarray(
        inp["L0_ffn_W0"].reshape(8, 8, 128, 8, 512).transpose(0, 3, 2, 1, 4))
    w1h = np.ascontiguousarray(
        inp["L0_ffn_W1"].reshape(8, 8, 128, 8, 512).transpose(0, 3, 2, 1, 4))
    w2h = np.ascontiguousarray(
        inp["L0_ffn_W2"].reshape(8, 8, 4, 128, 1024).transpose(0, 1, 3, 2, 4))
    return {
        pref + "Wg": inp["L0_ffn_Wg"],
        pref + "W0": w0h,
        pref + "W1": w1h,
        pref + "W2": w2h,
        pref + "b2": inp["L0_ffn_b2"],
        pref + "bg_bc": np.ascontiguousarray(np.broadcast_to(inp["L0_ffn_bg"], (128, 8))),
        pref + "b0_pp": b0,
        pref + "b1_pp": b1,
    }


def _globals_setup(nc, es, r):
    g = {
        "idsb": es.enter_context(nc.sbuf_tensor("idsb", [128, 128], BF16)),
        "idsf": es.enter_context(nc.sbuf_tensor("idsf", [128, 128], F32)),
    }
    idsb_d = nc.dram_tensor("idsb_d", [128, 128], BF16, kind="ExternalInput").ap()
    idsf_d = nc.dram_tensor("idsf_d", [128, 128], F32, kind="ExternalInput").ap()
    r.dma("sync", g["idsb"][:], idsb_d, writes=["idsb"])
    r.dma("sync", g["idsf"][:], idsf_d, writes=["idsf"])
    return g


def _host_globals():
    return {
        "idsb_d": np.eye(128, dtype=np.float32).astype(BF),
        "idsf_d": np.eye(128, dtype=np.float32),
    }


def build_stageA(nexperts=NE):
    nc = bass.Bass("TRN2", target_bir_lowering=False)
    x = nc.dram_tensor("x", [T, D], F32, kind="ExternalInput").ap()
    out = nc.dram_tensor("out", [T, D], F32, kind="ExternalOutput").ap()
    W = _decl_l0ffn_weights(nc)
    with ExitStack() as es:
        r = Rec(nc, es)
        r.block = es.enter_context(nc.Block())
        g = _globals_setup(nc, es, r)
        emit_l0ffn(r, nc, g, x, out, W, "A", nexperts=nexperts)
        r.barrier()
        r.emit(r.block)
    return nc




def emit_l0attn(r, nc, g, xin, xout, W, u):
    with ExitStack() as es:
        sb = lambda n, s, d: es.enter_context(nc.sbuf_tensor(n + u, s, d))
        ps = lambda n, s, d: es.enter_context(nc.psum_tensor(n + u, s, d))

        xn = sb("xn", [128, 32, 1024], BF16)
        kT = sb("kT", [128, 8, 4096], BF16)
        pS = [ps("pS%d" % i, [128, 512], F32) for i in range(4)]
        pt_b = ps("ptb", [128, 128], BF16)
        py = ps("py", [128, 1024], F32)

        with ExitStack() as es1:
            sb1 = lambda n, s, d: es1.enter_context(nc.sbuf_tensor(n + u, s, d))
            wqk = sb1("wqk", [128, 8, 2048], BF16)
            bqk = sb1("bqk", [128, 16], F32)
            r.dma("sync", bqk[:], W["bqk_pp"], writes=[("bqk", u)])
            xnTg = sb1("xnTg", [128, 2, 8, 512], BF16)
            qstg = sb1("qstg", [128, 1, 8, 512], BF16)
            xt = sb1("xt", [128, 2, 1024], F32)
            sqb = sb1("sq", [128, 1, 1024], F32)
            xnt = sb1("xnt", [128, 2, 1024], BF16)
            stt = sb1("stt", [128, 2, 8], F32)

            r.dma("gpsimd", wqk[:], W["wqkT"].rearrange("(c p) n -> p c n", p=128),
                  writes=[("wqk", u)])
            for tg in range(8):
                gp = tg % 2
                for tt4 in range(4):
                    tt = tg * 4 + tt4
                    p = tt % 2
                    xkey = ("xt", u, p)
                    xtile = xt[:, p, :]
                    r.dma("sync", xtile, xin[tt * 128:(tt + 1) * 128, :], writes=[xkey],
                          sem="d_xt%d%s" % (p, u))
                    layernorm_tile(r, u, xtile, stt[:, p, :], sqb[:, 0, :], ("stt", u, p), ("sq", u, 0), xkey)
                    r.op("vector", lambda e, p=p: e.tensor_scalar(
                        xnt[:, p, :], xt[:, p, :], stt[:, p, 7:8], stt[:, p, 6:7], ALU.add, ALU.mult),
                         reads=[xkey, ("stt", u, p)], writes=[("xnt", u, p)])
                    r.op("vector", lambda e, p=p, tt=tt: e.tensor_copy(xn[:, tt, :], xnt[:, p, :]),
                         reads=[("xnt", u, p)], writes=[("xn", u)])
                    for dc in range(8):
                        r.op("tensor", lambda e, p=p, dc=dc: e.transpose(
                            pt_b[:], xnt[:, p, dc * 128:(dc + 1) * 128], g["idsb"][:]),
                             reads=[("xnt", u, p), "idsb"], writes=[("ptb", u)])
                        r.op("vector", lambda e, dc=dc, tt4=tt4, gp=gp: e.tensor_copy(
                            xnTg[:, gp, dc, tt4 * 128:(tt4 + 1) * 128], pt_b[:]),
                             reads=[("ptb", u)], writes=[("xnTg", u, gp)])
                for m in range(16):
                    bank = m % 2
                    pk = ("pS%d" % bank, u)
                    r.mmgroup([
                        (lambda e, dc=dc, m=m, gp=gp, bank=bank: e.matmul(
                            pS[bank][:], wqk[:, dc, m * 128:(m + 1) * 128],
                            xnTg[:, gp, dc, :], start=(dc == 0), stop=(dc == 7)))
                        for dc in range(8)],
                        reads=[("wqk", u), ("xnTg", u, gp)], writes=[pk])
                    if m < 8:
                        r.op("vector", lambda e, m=m, bank=bank: e.tensor_scalar(
                            qstg[:, 0, m, :], pS[bank][:], bqk[:, m:m + 1], None, ALU.add),
                             reads=[pk, ("bqk", u)], writes=[("qstg", u)])
                    else:
                        r.op("vector", lambda e, m=m, gp=gp, tg=tg, bank=bank: e.tensor_scalar(
                            kT[:, m - 8, tg * 512:(tg + 1) * 512], pS[bank][:],
                            bqk[:, m:m + 1], None, ALU.add),
                             reads=[pk, ("bqk", u)], writes=[("kT", u)])
                r.dma("sync", W["qTs"][:, :, tg * 512:(tg + 1) * 512].rearrange("c p n -> p c n"),
                      qstg[:, 0, :, :], reads=[("qstg", u)], sem="d_qs%s" % u)
            r.barrier()
            r.emit(r.block)

        with ExitStack() as es2:
            sb2 = lambda n, s, d: es2.enter_context(nc.sbuf_tensor(n + u, s, d))
            qTt = sb2("qTt", [128, 2, 8, 128], BF16)
            Eb = sb2("Eb", [128, 2, 4096], BF16)
            tri = sb2("tri", [128, 4, 512], BF16)
            r.dma("sync", tri[:], W["tri4"].rearrange("r p n -> p r n"), writes=[("tri", u)])
            avg = sb2("avg", [128, 4096], F32)
            gum = sb2("gum", [128, 4096], F32)
            dwb = sb2("dwb", [128, 4096], BF16)
            dwT = sb2("dwT", [128, 2, 128], BF16)
            zz = sb2("zz", [128, 8], F32)
            zcv = sb2("zcv", [128, 2, 8], F32)
            scb = sb2("scb", [128, 2], F32)
            xt2 = sb2("xt2", [128, 1, 1024], F32)
            xo2 = sb2("xo2", [128, 1, 1024], F32)

            r.op("vector", lambda e: e.memset(scb[:, 0:1], 0.125), writes=[("scb", u)])
            r.op("vector", lambda e: e.memset(scb[:, 1:2], 0.5), writes=[("scb", u)])

            for qt in range(32):
                nk = qt // 4 + 1
                span = nk * 512
                qp = qt % 2
                r.dma("sync", qTt[:, qp, :, :],
                      W["qTs"][:, :, qt * 128:(qt + 1) * 128].rearrange("c p n -> p c n"),
                      writes=[("qTt", u, qp)], sem="d_qt%d%s" % (qp, u))
                r.dma("sync", gum[:, 0:span], W["gum"][qt * 128:(qt + 1) * 128, 0:span],
                      writes=[("gum", u)], sem="d_gum%s" % u)
                for hp in range(8):
                    for h2 in range(2):
                        h = 2 * hp + h2
                        ek = ("Eb", u, h2)
                        zk = ("zz", u, h2)
                        for kt in range(nk):
                            bank = 2 * (kt % 2) + h2
                            pk = ("pS%d" % bank, u)
                            base = 64 * h2
                            r.mmgroup([(lambda e, h=h, kt=kt, bank=bank, base=base, qp=qp: e.matmul(
                                pS[bank][:], qTt[base:base + 64, qp, h // 2, :],
                                kT[base:base + 64, h // 2, kt * 512:(kt + 1) * 512],
                                start=True, stop=True))],
                                reads=[("qTt", u, qp), ("kT", u)], writes=[pk])
                            if kt == nk - 1:
                                r.op("vector", lambda e, bank=bank, qt=qt: e.tensor_tensor(
                                    pS[bank][:], pS[bank][:], tri[:, qt % 4, :], ALU.add),
                                     reads=[("tri", u), pk], writes=[pk])
                            r.op("scalar", lambda e, bank=bank, kt=kt, h2=h2: e.activation(
                                Eb[:, h2, kt * 512:(kt + 1) * 512], pS[bank][:], AF.Exp,
                                scale=scb[:, 0:1], accum_out=zcv[:, h2, kt:kt + 1]),
                                 reads=[pk, ("scb", u)], writes=[ek, ("zzc", u, h2)])
                        r.op("vector", lambda e, h2=h2, nk=nk: e.reduce_sum(
                            zz[:, h2:h2 + 1], zcv[:, h2, 0:nk], axis=AX),
                             reads=[("zzc", u, h2)], writes=[zk])
                        r.op("vector", lambda e, h2=h2: e.tensor_scalar(
                            zz[:, 2 + h2:3 + h2], zz[:, h2:h2 + 1], 16.0, 1e-30, ALU.mult, ALU.add),
                             reads=[zk], writes=[zk])
                        r.op("vector", lambda e, h2=h2: e.reciprocal(
                            zz[:, 2 + h2:3 + h2], zz[:, 2 + h2:3 + h2]),
                             reads=[zk], writes=[zk])
                        if hp == 0 and h2 == 0:
                            r.op("vector", lambda e, h2=h2, span=span: e.tensor_scalar(
                                avg[:, 0:span], Eb[:, h2, 0:span], zz[:, 2 + h2:3 + h2], None, ALU.mult),
                                 reads=[ek, zk], writes=[("avg", u)])
                        else:
                            r.op("vector", lambda e, h2=h2, span=span: e.scalar_tensor_tensor(
                                avg[:, 0:span], Eb[:, h2, 0:span], zz[:, 2 + h2:3 + h2],
                                avg[:, 0:span], ALU.mult, ALU.add),
                                 reads=[ek, zk, ("avg", u)], writes=[("avg", u)])
                r.op("vector", lambda e, span=span: e.tensor_scalar(
                    avg[:, 0:span], avg[:, 0:span], -40.0, 40.0, ALU.max, ALU.min),
                     reads=[("avg", u)], writes=[("avg", u)])
                r.op("vector", lambda e, span=span: e.tensor_add(
                    avg[:, 0:span], avg[:, 0:span], gum[:, 0:span]),
                     reads=[("avg", u), ("gum", u)], writes=[("avg", u)])
                r.op("vector", lambda e, span=span, qt=qt: e.tensor_tensor(
                    avg[:, span - 512:span], avg[:, span - 512:span], tri[:, qt % 4, :], ALU.add),
                     reads=[("avg", u), ("tri", u)], writes=[("avg", u)])
                r.op("scalar", lambda e, span=span: e.activation(
                    dwb[:, 0:span], avg[:, 0:span], AF.Exp, scale=scb[:, 1:2],
                    accum_out=zz[:, 6:7]),
                     reads=[("avg", u), ("scb", u)], writes=[("dwb", u), ("zz2", u)])
                r.op("vector", lambda e: e.tensor_scalar(
                    zz[:, 7:8], zz[:, 6:7], 1.0, 1e-30, ALU.mult, ALU.add),
                     reads=[("zz2", u)], writes=[("zz2", u)])
                r.op("vector", lambda e: e.reciprocal(zz[:, 7:8], zz[:, 7:8]),
                     reads=[("zz2", u)], writes=[("zz2", u)])
                for ktile in range(nk * 4):
                    dp = ktile % 2
                    r.op("tensor", lambda e, ktile=ktile: e.transpose(
                        pt_b[:], dwb[:, ktile * 128:(ktile + 1) * 128], g["idsb"][:]),
                         reads=[("dwb", u), "idsb"], writes=[("ptb", u)])
                    r.op("vector", lambda e, dp=dp: e.tensor_copy(dwT[:, dp, :], pt_b[:]),
                         reads=[("ptb", u)], writes=[("dwT", u, dp)])
                    for half in range(2):
                        r.mmgroup([(lambda e, ktile=ktile, half=half, dp=dp, nk=nk: e.matmul(
                            py[:, half * 512:(half + 1) * 512], dwT[:, dp, :],
                            xn[:, ktile, half * 512:(half + 1) * 512],
                            start=(ktile == 0), stop=(ktile == nk * 4 - 1)))],
                            reads=[("dwT", u, dp), ("xn", u)], writes=[("py", u)])
                p2 = 0
                xkey = ("xt2", u, p2)
                r.dma("sync", xt2[:, p2, :], xin[qt * 128:(qt + 1) * 128, :], writes=[xkey],
                      sem="d_x2%s" % u)
                r.op("vector", lambda e, p2=p2: e.scalar_tensor_tensor(
                    xo2[:, p2, :], py[:], zz[:, 7:8], xt2[:, p2, :], ALU.mult, ALU.add),
                     reads=[("py", u), ("zz2", u), xkey], writes=[("xo2", u, p2)])
                r.dma("sync", xout[qt * 128:(qt + 1) * 128, :], xo2[:, p2, :],
                      reads=[("xo2", u, p2)], sem="d_o2%s" % u)
            r.barrier()
        r.emit(r.block)


def _decl_l0attn_weights(nc, pref=""):
    return {
        "wqkT": nc.dram_tensor(pref + "wqkT", [D, 2 * D], F32, kind="ExternalInput").ap(),
        "bqk_pp": nc.dram_tensor(pref + "bqk_pp", [128, 16], F32, kind="ExternalInput").ap(),
        "tri4": nc.dram_tensor(pref + "tri4", [4, 128, 512], BF16, kind="ExternalInput").ap(),
        "gum": nc.dram_tensor(pref + "gum", [T, T], F32, kind="ExternalInput").ap(),
        "qTs": nc.dram_tensor(pref + "qTs", [8, 128, T], BF16, kind="Internal").ap(),
    }


def _host_l0attn_weights(inp, pref=""):
    w = inp["L0_attn_in_proj_weight"]
    b = inp["L0_attn_in_proj_bias"]
    wqkT = np.ascontiguousarray(w[:2 * D].T)
    bqk = np.ascontiguousarray(b[:2 * D].reshape(16, 128).T)
    tri4 = np.empty((4, 128, 512), np.float32)
    for rr in range(4):
        qr = np.arange(128)[:, None]
        kc = np.arange(512)[None, :]
        tri4[rr] = np.where(kc < rr * 128 + qr, 0.0, NEG).astype(np.float32)
    return {pref + "wqkT": wqkT, pref + "bqk_pp": bqk, pref + "tri4": tri4.astype(BF)}


def _host_gumbel():
    def _rotl(x, d):
        return (x << np.uint32(d)) | (x >> np.uint32(32 - d))

    n = CORES * T * T
    minval, maxval = np.float32(1e-06), np.float32(1.0 - 1e-06)
    idx = np.arange(n, dtype=np.uint64)
    x0 = (idx >> np.uint64(32)).astype(np.uint32)
    x1 = (idx & np.uint64(0xFFFFFFFF)).astype(np.uint32)
    del idx
    ks0, ks1 = np.uint32(0), np.uint32(42)
    ks2 = ks0 ^ ks1 ^ np.uint32(0x1BD11BDA)
    rotations = [(13, 15, 26, 6), (17, 29, 16, 24)]
    x0 = x0 + ks0
    x1 = x1 + ks1
    ks = [ks1, ks2, ks0]
    for i in range(5):
        for rr in rotations[i % 2]:
            x0 += x1
            x1 = _rotl(x1, rr)
            x1 = x0 ^ x1
        x0 += ks[i % 3]
        x1 += ks[(i + 1) % 3] + np.uint32(i + 1)
    bits = x0 ^ x1
    del x0, x1
    fl = ((bits >> np.uint32(9)) | np.uint32(0x3F800000)).view(np.float32) - np.float32(1.0)
    del bits
    u = np.maximum(minval, fl * (maxval - minval) + minval)
    del fl
    gm = -np.log(-np.log(u))
    return gm.reshape(CORES, T, T)


def build_stageB():
    nc = bass.Bass("TRN2", target_bir_lowering=False)
    x = nc.dram_tensor("x", [T, D], F32, kind="ExternalInput").ap()
    out = nc.dram_tensor("out", [T, D], F32, kind="ExternalOutput").ap()
    W = _decl_l0attn_weights(nc)
    with ExitStack() as es:
        r = Rec(nc, es)
        r.block = es.enter_context(nc.Block())
        g = _globals_setup(nc, es, r)
        emit_l0attn(r, nc, g, x, out, W, "B")
        r.barrier()
        r.emit(r.block)
    return nc




def bcast16(ap2d, inner):
    return bass.AP(ap2d.tensor, ap2d.offset, [ap2d.ap[0], [0, 16], [1, inner]])


def emit_l1ffn(r, nc, g, xin, xout, W, u, su=None):
    su = su or u
    with ExitStack() as es:
        sb = lambda n, s, d: es.enter_context(nc.sbuf_tensor(n + u, s, d))
        scb = sb("scb", [128, 2], F32)
        r.op("vector", lambda e: e.memset(scb[:, 0:1], 0.125), writes=[("scb", u)])
        esA = es.enter_context(ExitStack())
        kT = esA.enter_context(nc.sbuf_tensor("kT" + u, [128, 8, 4096], BF16))
        Vaug = esA.enter_context(nc.sbuf_tensor("Vaug" + u, [128, 32, 1040], BF16))
        Vv = Vaug[:].rearrange("p t (h c) -> p t h c", c=65)
        r.op("vector", lambda e: e.memset(Vv[:, :, :, 64:65], 1.0), writes=[("Vones", u)])

        with ExitStack() as e1:
            sb1 = lambda n, s, d: e1.enter_context(nc.sbuf_tensor(n + u, s, d))
            ps1 = lambda n, s, d: e1.enter_context(nc.psum_tensor(n + u, s, d))
            pt_b = ps1("ptb", [128, 128], BF16)
            xt = sb1("xt", [128, 2, 1024], F32)
            sqb = sb1("sq", [128, 1024], F32)
            xnt = sb1("xnt", [128, 2, 1024], BF16)
            stt = sb1("stt", [128, 2, 8], F32)
            xnTg = sb1("xnTg", [128, 8, 512], BF16)
            for tg in range(8):
                for tt4 in range(4):
                    tt = tg * 4 + tt4
                    p = tt % 2
                    xkey = ("xt", u, p)
                    xtile = xt[:, p, :]
                    r.dma("sync", xtile, xin[tt * 128:(tt + 1) * 128, :], writes=[xkey],
                          sem="d_xt%d%s" % (p, su))
                    layernorm_tile(r, u, xtile, stt[:, p, :], sqb[:], ("stt", u, p), ("sq", u), xkey)
                    r.op("vector", lambda e, p=p: e.tensor_scalar(
                        xnt[:, p, :], xt[:, p, :], stt[:, p, 7:8], stt[:, p, 6:7], ALU.add, ALU.mult),
                         reads=[xkey, ("stt", u, p)], writes=[("xnt", u, p)])
                    r.dma("sync", W["xns"][tt * 128:(tt + 1) * 128, :], xnt[:, p, :],
                          reads=[("xnt", u, p)], sem="d_xns%d%s" % (p, su))
                    for dc in range(8):
                        r.op("tensor", lambda e, p=p, dc=dc: e.transpose(
                            pt_b[:], xnt[:, p, dc * 128:(dc + 1) * 128], g["idsb"][:]),
                             reads=[("xnt", u, p), "idsb"], writes=[("ptb", u)])
                        r.op("vector", lambda e, dc=dc, tt4=tt4: e.tensor_copy(
                            xnTg[:, dc, tt4 * 128:(tt4 + 1) * 128], pt_b[:]),
                             reads=[("ptb", u)], writes=[("xnTg", u)])
                r.dma("sync", W["xnTs"][:, :, tg * 512:(tg + 1) * 512].rearrange("c p n -> p c n"),
                      xnTg[:], reads=[("xnTg", u)], sem="d_xnT%s" % su)
            r.barrier()
            r.emit(r.block)

        with ExitStack() as e1:
            sb1 = lambda n, s, d: e1.enter_context(nc.sbuf_tensor(n + u, s, d))
            ps1 = lambda n, s, d: e1.enter_context(nc.psum_tensor(n + u, s, d))
            pt_b = ps1("ptbb", [128, 128], BF16)
            pP = [ps1("pP%d" % i, [128, 512], F32) for i in range(2)]
            wpr = sb1("wpr", [128, 2, 8, 1024], BF16)
            xnTg = sb1("xnTg2", [128, 2, 8, 512], BF16)
            bias_bc = sb1("bbc", [128, 1024], F32)
            tabs = sb1("tabs", [128, 2, 2, 64], F32)
            work = sb1("work", [128, 1024], F32)
            tmp1 = sb1("tmp1", [128, 512], F32)
            tmp2 = sb1("tmp2", [128, 512], F32)
            rro = sb1("rro", [128, 1024], BF16)
            qstg = sb1("qstg", [128, 8, 512], BF16)

            for ip, pr in enumerate(["q", "k", "v"]):
                wb_ = ip % 2
                r.dma("gpsimd", wpr[:, wb_, :, :],
                      W["qkvwT"][:, ip * 1024:(ip + 1) * 1024].rearrange("(c p) n -> p c n", p=128),
                      writes=[("wpr", u, wb_)], sem="d_wp%d%s" % (wb_, su))
                r.dma("sync", bias_bc[:], W["bqkv_bc"][:, ip * 1024:(ip + 1) * 1024],
                      writes=[("bbc", u)], sem="d_bbc%s" % su)
                for tg in range(8):
                    gp = tg % 2
                    r.dma("sync", xnTg[:, gp, :, :],
                          W["xnTs"][:, :, tg * 512:(tg + 1) * 512].rearrange("c p n -> p c n"),
                          writes=[("xnTg2", u, gp)], sem="d_xg%d%s" % (gp, su))
                    for tt4 in range(4):
                        tt = tg * 4 + tt4
                        if pr != "v":
                            r.dma("sync", tabs[:, tt % 2, 0, :], W["cosF"][:, tt, :],
                                  writes=[("tabs", u, tt % 2)], sem="d_tc%d%s" % (tt % 2, su))
                            r.dma("sync", tabs[:, tt % 2, 1, :], W["sinF"][:, tt, :],
                                  writes=[("tabs", u, tt % 2)], sem="d_ts%d%s" % (tt % 2, su))
                        for half in range(2):
                            bk = (tt4 * 2 + half) % 2
                            pk = ("pP%d" % bk, u)
                            r.mmgroup([
                                (lambda e, dc=dc, tt4=tt4, half=half, gp=gp, wb_=wb_, bk=bk: e.matmul(
                                    pP[bk][:], xnTg[:, gp, dc, tt4 * 128:(tt4 + 1) * 128],
                                    wpr[:, wb_, dc, half * 512:(half + 1) * 512],
                                    start=(dc == 0), stop=(dc == 7)))
                                for dc in range(8)],
                                reads=[("xnTg2", u, gp), ("wpr", u, wb_)], writes=[pk])
                            if pr == "v":
                                pv = pP[bk][:].rearrange("p (h c) -> p h c", c=64)
                                bv = bias_bc[:, half * 512:(half + 1) * 512].rearrange(
                                    "p (h c) -> p h c", c=64)
                                r.op("vector", lambda e, tt=tt, half=half, pv=pv, bv=bv: e.tensor_tensor(
                                    Vv[:, tt, half * 8:(half + 1) * 8, 0:64], pv, bv, ALU.add),
                                     reads=[pk, ("bbc", u)], writes=[("V", u)])
                            else:
                                r.op("vector", lambda e, half=half, bk=bk: e.tensor_tensor(
                                    work[:, half * 512:(half + 1) * 512], pP[bk][:],
                                    bias_bc[:, half * 512:(half + 1) * 512], ALU.add),
                                     reads=[pk, ("bbc", u)], writes=[("work", u)])
                        if pr != "v":
                            tp = tt % 2
                            wv = work[:].rearrange("p (h c) -> p h c", c=64)
                            rv = rro[:].rearrange("p (h c) -> p h c", c=64)
                            t1v = tmp1[:].rearrange("p (h c) -> p h c", c=32)
                            t2v = tmp2[:].rearrange("p (h c) -> p h c", c=32)
                            c1 = bcast16(tabs[:, tp, 0, 0:32], 32)
                            c2 = bcast16(tabs[:, tp, 0, 32:64], 32)
                            s1 = bcast16(tabs[:, tp, 1, 0:32], 32)
                            s2 = bcast16(tabs[:, tp, 1, 32:64], 32)
                            tk = ("tabs", u, tp)
                            r.op("vector", lambda e, c1=c1, wv=wv, t1v=t1v: e.tensor_tensor(
                                t1v, wv[:, :, 0:32], c1, ALU.mult),
                                 reads=[("work", u), tk], writes=[("tmp1", u)])
                            r.op("vector", lambda e, s1=s1, wv=wv, t2v=t2v: e.tensor_tensor(
                                t2v, wv[:, :, 32:64], s1, ALU.mult),
                                 reads=[("work", u), tk], writes=[("tmp2", u)])
                            r.op("vector", lambda e, rv=rv, t1v=t1v, t2v=t2v: e.tensor_tensor(
                                rv[:, :, 0:32], t1v, t2v, ALU.subtract),
                                 reads=[("tmp1", u), ("tmp2", u)], writes=[("rro", u)])
                            r.op("vector", lambda e, c2=c2, wv=wv, t1v=t1v: e.tensor_tensor(
                                t1v, wv[:, :, 32:64], c2, ALU.mult),
                                 reads=[("work", u), tk], writes=[("tmp1", u)])
                            r.op("vector", lambda e, s2=s2, wv=wv, t2v=t2v: e.tensor_tensor(
                                t2v, wv[:, :, 0:32], s2, ALU.mult),
                                 reads=[("work", u), tk], writes=[("tmp2", u)])
                            r.op("vector", lambda e, rv=rv, t1v=t1v, t2v=t2v: e.tensor_tensor(
                                rv[:, :, 32:64], t1v, t2v, ALU.add),
                                 reads=[("tmp1", u), ("tmp2", u)], writes=[("rro", u)])
                            for dc in range(8):
                                r.op("tensor", lambda e, dc=dc: e.transpose(
                                    pt_b[:], rro[:, dc * 128:(dc + 1) * 128], g["idsb"][:]),
                                     reads=[("rro", u), "idsb"], writes=[("ptb", u)])
                                if pr == "q":
                                    r.op("vector", lambda e, dc=dc, tt4=tt4: e.tensor_copy(
                                        qstg[:, dc, tt4 * 128:(tt4 + 1) * 128], pt_b[:]),
                                         reads=[("ptb", u)], writes=[("qstg", u)])
                                else:
                                    r.op("vector", lambda e, dc=dc, tt=tt: e.tensor_copy(
                                        kT[:, dc, tt * 128:(tt + 1) * 128], pt_b[:]),
                                         reads=[("ptb", u)], writes=[("kT", u)])
                    if pr == "q":
                        r.dma("sync", W["qTs"][:, :, tg * 512:(tg + 1) * 512].rearrange("c p n -> p c n"),
                              qstg[:], reads=[("qstg", u)], sem="d_qs%s" % su)
            r.barrier()
            r.emit(r.block)

        with ExitStack() as e2:
            sb2 = lambda n, s, d: e2.enter_context(nc.sbuf_tensor(n + u, s, d))
            ps2 = lambda n, s, d: e2.enter_context(nc.psum_tensor(n + u, s, d))
            pS = [ps2("pS%d" % i, [128, 1024], F32) for i in range(2)]
            pOT = [ps2("pOT%d" % i, [128, 512], F32) for i in range(2)]
            pt_f = ps2("ptf", [128, 128], F32)
            pt_b2 = ps2("ptb2", [128, 128], BF16)
            qTg = sb2("qTg", [128, 1, 8, 512], BF16)
            ET = sb2("ET", [128, 2, 1024], BF16)
            oT = sb2("oT", [128, 2, 512], F32)
            rz = sb2("rz", [128, 2], F32)
            ao = sb2("ao", [128, 4, 1024], BF16)
            aostg = sb2("aostg", [128, 8, 512], BF16)

            for qg in range(8):
                gp = 0
                r.dma("sync", qTg[:, 0, :, :],
                      W["qTs"][:, :, qg * 512:(qg + 1) * 512].rearrange("c p n -> p c n"),
                      writes=[("qTg", u, gp)], sem="d_qg%s" % u)
                for hp in range(8):
                    for kt in range(32):
                        bank = kt % 2
                        for h2 in range(2):
                            base = 64 * h2
                            r.mmgroup([(lambda e, hp=hp, kt=kt, bank=bank, base=base, h2=h2: e.matmul(
                                pS[bank][:, h2 * 512:(h2 + 1) * 512],
                                kT[base:base + 64, hp, kt * 128:(kt + 1) * 128],
                                qTg[base:base + 64, 0, hp, :], start=True, stop=True))],
                                reads=[("qTg", u, gp), ("kT", u)], writes=[("pS", u, bank, h2)])
                        ekey = ("ET", u, bank)
                        r.op("scalar", lambda e, bank=bank: e.activation(
                            ET[:, bank, :], pS[bank][:], AF.Exp, scale=scb[:, 0:1]),
                             reads=[("pS", u, bank, 0), ("pS", u, bank, 1), ("scb", u)],
                             writes=[ekey])
                        if kt > 0:
                            pb = 1 - bank
                            for h2 in range(2):
                                h = 2 * hp + h2
                                r.mmgroup([(lambda e, h=h, kt=kt - 1, pb=pb, h2=h2: e.matmul(
                                    pOT[h2][0:65, :], Vaug[:, kt, h * 65:(h + 1) * 65],
                                    ET[:, pb, h2 * 512:(h2 + 1) * 512],
                                    start=(kt == 0), stop=False))],
                                    reads=[("ET", u, pb), ("V", u), ("Vones", u)],
                                    writes=[("pOT%d" % h2, u)])
                    for h2 in range(2):
                        h = 2 * hp + h2
                        r.mmgroup([(lambda e, h=h, h2=h2: e.matmul(
                            pOT[h2][0:65, :], Vaug[:, 31, h * 65:(h + 1) * 65],
                            ET[:, 1, h2 * 512:(h2 + 1) * 512],
                            start=False, stop=True))],
                            reads=[("ET", u, 1), ("V", u), ("Vones", u)],
                            writes=[("pOT%d" % h2, u)])
                    for h2 in range(2):
                        h = 2 * hp + h2
                        ok = ("pOT%d" % h2, u)
                        r.op("vector", lambda e, h2=h2: e.tensor_copy(oT[0:65, h2, :], pOT[h2][0:65, :]),
                             reads=[ok], writes=[("oT", u, h2)])
                        for qs in range(4):
                            r.op("tensor", lambda e, h2=h2, qs=qs: e.transpose(
                                pt_f[:, 0:65], oT[0:65, h2, qs * 128:(qs + 1) * 128],
                                g["idsf"][0:65, 0:65]),
                                 reads=[("oT", u, h2), "idsf"], writes=[("ptf", u)])
                            r.op("vector", lambda e, h2=h2: e.reciprocal(
                                rz[:, h2:h2 + 1], pt_f[:, 64:65]),
                                 reads=[("ptf", u)], writes=[("rz", u, h2)])
                            r.op("vector", lambda e, h2=h2, qs=qs, h=h: e.tensor_scalar(
                                ao[:, qs, h * 64:(h + 1) * 64], pt_f[:, 0:64],
                                rz[:, h2:h2 + 1], None, ALU.mult),
                                 reads=[("ptf", u), ("rz", u, h2)], writes=[("ao", u)])
                for qs in range(4):
                    for dc in range(8):
                        r.op("tensor", lambda e, qs=qs, dc=dc: e.transpose(
                            pt_b2[:], ao[:, qs, dc * 128:(dc + 1) * 128], g["idsb"][:]),
                             reads=[("ao", u), "idsb"], writes=[("ptb2", u)])
                        r.op("vector", lambda e, qs=qs, dc=dc: e.tensor_copy(
                            aostg[:, dc, qs * 128:(qs + 1) * 128], pt_b2[:]),
                             reads=[("ptb2", u)], writes=[("aostg", u)])
                r.dma("sync", W["aoTs"][:, :, qg * 512:(qg + 1) * 512].rearrange("c p n -> p c n"),
                      aostg[:], reads=[("aostg", u)], sem="d_ao%s" % su)
            r.barrier()
            r.emit(r.block)
        esA.close()

        with ExitStack() as e3:
            sb3 = lambda n, s, d: e3.enter_context(nc.sbuf_tensor(n + u, s, d))
            ps3 = lambda n, s, d: e3.enter_context(nc.psum_tensor(n + u, s, d))
            pG = ps3("pG", [128, 512], F32)
            pyE2 = [ps3("pyE%d" % i, [128, 1024], F32) for i in range(2)]
            pyE = pyE2[0]
            pyF = ps3("pyF", [128, 1024], F32)
            pt_b3 = ps3("ptb3", [128, 128], BF16)
            aoTg = sb3("aoTg", [128, 8, 2048], BF16)
            moe = sb3("moe", [128, 16, 1024], F32)
            ewb = sb3("ewb", [128, 2, 8, 1024], BF16)
            fwb = sb3("fwb", [128, 8, 1024], BF16)
            gwb = sb3("gwb", [128, 8, 8], BF16)
            gbias = sb3("gbias", [128, 8], F32)
            fbias = sb3("fbias", [128, 1024], F32)
            eb_sb = sb3("ebsb", [8, 1024], F32)
            cm = sb3("cm", [128, 16, 8], F32)
            cmT = sb3("cmT", [8, 2048], F32)
            gs = sb3("gs", [128, 48], F32)
            mb3 = sb3("mb3", [128, 1024], BF16)
            moeT = sb3("moeT", [128, 8, 128], BF16)
            xnb3 = sb3("xnb3", [128, 2, 1024], BF16)
            xt3 = sb3("xt3", [128, 2, 1024], F32)
            xo3 = sb3("xo3", [128, 2, 1024], F32)

            r.dma("gpsimd", gwb[:], W["gwT"].rearrange("(c p) n -> p c n", p=128),
                  writes=[("gwb", u)], sem="d_gw%s" % su)
            r.dma("gpsimd", fwb[:], W["fwT"].rearrange("(c p) n -> p c n", p=128),
                  writes=[("fwb", u)], sem="d_fw%s" % su)
            r.dma("sync", gbias[:], W["gb_bc"], writes=[("gbias", u)], sem="d_gb%s" % su)
            r.dma("sync", fbias[:], W["fb_bc"], writes=[("fbias", u)], sem="d_fb%s" % su)
            r.dma("sync", eb_sb[:], W["eb"], writes=[("ebsb", u)], sem="d_eb%s" % su)

            for G in range(2):
                t0 = G * 16
                r.dma("sync", aoTg[:],
                      W["aoTs"][:, :, G * 2048:(G + 1) * 2048].rearrange("c p n -> p c n"),
                      writes=[("aoTg", u)], sem="d_aog%s" % su)
                for tt in range(16):
                    r.mmgroup([
                        (lambda e, dc=dc, tt=tt: e.matmul(
                            pG[:, 0:8], aoTg[:, dc, tt * 128:(tt + 1) * 128], gwb[:, dc, :],
                            start=(dc == 0), stop=(dc == 7)))
                        for dc in range(8)],
                        reads=[("aoTg", u), ("gwb", u)], writes=[("pG", u)])
                    gk = ("gs", u)
                    r.op("vector", lambda e: e.tensor_add(gs[:, 0:8], pG[:, 0:8], gbias[:]),
                         reads=[("pG", u), ("gbias", u)], writes=[gk])
                    r.op("scalar", lambda e: e.activation(
                        gs[:, 8:16], gs[:, 0:8], AF.Exp, accum_out=gs[:, 16:17]),
                         reads=[gk], writes=[gk])
                    r.op("vector", lambda e: e.reciprocal(gs[:, 17:18], gs[:, 16:17]),
                         reads=[gk], writes=[gk])
                    r.op("vector", lambda e: e.tensor_scalar(
                        gs[:, 24:32], gs[:, 8:16], gs[:, 17:18], None, ALU.mult),
                         reads=[gk], writes=[gk])
                    r.op("vector", lambda e: e.reduce_max(gs[:, 18:19], gs[:, 24:32], axis=AX),
                         reads=[gk], writes=[gk])
                    r.op("vector", lambda e: e.tensor_scalar(
                        gs[:, 32:40], gs[:, 24:32], gs[:, 18:19], None, ALU.is_equal),
                         reads=[gk], writes=[gk])
                    r.op("vector", lambda e: e.scalar_tensor_tensor(
                        gs[:, 40:48], gs[:, 32:40], -1e30, gs[:, 24:32], ALU.mult, ALU.add),
                         reads=[gk], writes=[gk])
                    r.op("vector", lambda e: e.reduce_max(gs[:, 19:20], gs[:, 40:48], axis=AX),
                         reads=[gk], writes=[gk])
                    r.op("vector", lambda e: e.tensor_scalar(
                        gs[:, 32:40], gs[:, 24:32], gs[:, 19:20], None, ALU.is_ge),
                         reads=[gk], writes=[gk])
                    r.op("vector", lambda e, tt=tt: e.tensor_tensor(
                        cm[:, tt, :], gs[:, 24:32], gs[:, 32:40], ALU.mult),
                         reads=[gk], writes=[("cm", u)])
                    r.op("tensor", lambda e, tt=tt: e.transpose(
                        pG[0:8, 0:128], cm[:, tt, :], g["idsf"][:]),
                         reads=[("cm", u), "idsf"], writes=[("pG", u)])
                    r.op("vector", lambda e, tt=tt: e.tensor_copy(
                        cmT[:, tt * 128:(tt + 1) * 128], pG[0:8, 0:128]),
                         reads=[("pG", u)], writes=[("cmT", u)])
                for tt in range(16):
                    pp = tt % 2
                    pE = pyE2[pp]
                    for half in range(2):
                        r.mmgroup([(lambda e, tt=tt, half=half, pE=pE: e.matmul(
                            pE[:, half * 512:(half + 1) * 512], cmT[:, tt * 128:(tt + 1) * 128],
                            eb_sb[:, half * 512:(half + 1) * 512], start=True, stop=True))],
                            reads=[("cmT", u), ("ebsb", u)], writes=[("pyE", u, pp)])
                    r.op("vector", lambda e, tt=tt, pE=pE: e.tensor_copy(moe[:, tt, :], pE[:]),
                         reads=[("pyE", u, tt % 2)], writes=[("moe", u, tt)])
                for ei in range(NE):
                    wb_ = ei % 2
                    r.dma("gpsimd", ewb[:, wb_, :, :],
                          W["ewT"][ei].rearrange("(c p) n -> p c n", p=128),
                          writes=[("ewb", u, wb_)], sem="d_ew%d%s" % (wb_, su))
                    for tt in range(16):
                        pp = tt % 2
                        pE = pyE2[pp]
                        for half in range(2):
                            r.mmgroup([
                                (lambda e, dc=dc, tt=tt, half=half, wb_=wb_, pE=pE: e.matmul(
                                    pE[:, half * 512:(half + 1) * 512],
                                    aoTg[:, dc, tt * 128:(tt + 1) * 128],
                                    ewb[:, wb_, dc, half * 512:(half + 1) * 512],
                                    start=(dc == 0), stop=(dc == 7)))
                                for dc in range(8)],
                                reads=[("aoTg", u), ("ewb", u, wb_)], writes=[("pyE", u, pp)])
                        r.op("vector", lambda e, tt=tt, ei=ei, pE=pE: e.scalar_tensor_tensor(
                            moe[:, tt, :], pE[:], cm[:, tt:tt + 1, ei:ei + 1], moe[:, tt, :],
                            ALU.mult, ALU.add),
                             reads=[("pyE", u, tt % 2), ("cm", u)], writes=[("moe", u, tt)])
                for tt in range(16):
                    p = tt % 2
                    ta = t0 + tt
                    r.op("vector", lambda e, tt=tt: e.tensor_copy(mb3[:], moe[:, tt, :]),
                         reads=[("moe", u, tt)], writes=[("mb3", u)])
                    for dc in range(8):
                        r.op("tensor", lambda e, dc=dc: e.transpose(
                            pt_b3[:], mb3[:, dc * 128:(dc + 1) * 128], g["idsb"][:]),
                             reads=[("mb3", u), "idsb"], writes=[("ptb3", u)])
                        r.op("vector", lambda e, dc=dc: e.tensor_copy(moeT[:, dc, :], pt_b3[:]),
                             reads=[("ptb3", u)], writes=[("moeT", u)])
                    for half in range(2):
                        r.mmgroup([
                            (lambda e, dc=dc, half=half: e.matmul(
                                pyF[:, half * 512:(half + 1) * 512], moeT[:, dc, :],
                                fwb[:, dc, half * 512:(half + 1) * 512],
                                start=(dc == 0), stop=(dc == 7)))
                            for dc in range(8)],
                            reads=[("moeT", u), ("fwb", u)], writes=[("pyF", u)])
                    r.dma("sync", xnb3[:, p, :], W["xns"][ta * 128:(ta + 1) * 128, :],
                          writes=[("xnb3", u, p)], sem="d_xn3%d%s" % (p, su))
                    r.dma("sync", xt3[:, p, :], xin[ta * 128:(ta + 1) * 128, :],
                          writes=[("xt3", u, p)], sem="d_xt3%d%s" % (p, su))
                    r.op("vector", lambda e, p=p: e.tensor_add(xo3[:, p, :], pyF[:], fbias[:]),
                         reads=[("pyF", u), ("fbias", u)], writes=[("xo3", u, p)])
                    r.op("vector", lambda e, p=p: e.tensor_add(
                        xo3[:, p, :], xo3[:, p, :], xnb3[:, p, :]),
                         reads=[("xo3", u, p), ("xnb3", u, p)], writes=[("xo3", u, p)])
                    r.op("vector", lambda e, p=p: e.scalar_tensor_tensor(
                        xo3[:, p, :], xo3[:, p, :], 0.5, xt3[:, p, :], ALU.mult, ALU.add),
                         reads=[("xo3", u, p), ("xt3", u, p)], writes=[("xo3", u, p)])
                    r.dma("sync", xout[ta * 128:(ta + 1) * 128, :], xo3[:, p, :],
                          reads=[("xo3", u, p)], sem="d_of%d%s" % (p, su))
            r.barrier()
        r.emit(r.block)


def _decl_l1ffn_weights(nc, pref=""):
    return {
        "qkvwT": nc.dram_tensor(pref + "qkvwT", [D, 3 * D], F32, kind="ExternalInput").ap(),
        "bqkv_bc": nc.dram_tensor(pref + "bqkv_bc", [128, 3 * D], F32, kind="ExternalInput").ap(),
        "cosF": nc.dram_tensor(pref + "cosF", [128, 32, 64], F32, kind="ExternalInput").ap(),
        "sinF": nc.dram_tensor(pref + "sinF", [128, 32, 64], F32, kind="ExternalInput").ap(),
        "gwT": nc.dram_tensor(pref + "gwT", [D, 8], F32, kind="ExternalInput").ap(),
        "gb_bc": nc.dram_tensor(pref + "gb_bc", [128, 8], F32, kind="ExternalInput").ap(),
        "ewT": nc.dram_tensor(pref + "ewT", [NE, D, D], F32, kind="ExternalInput").ap(),
        "eb": nc.dram_tensor(pref + "eb", [NE, D], F32, kind="ExternalInput").ap(),
        "fwT": nc.dram_tensor(pref + "fwT", [D, D], F32, kind="ExternalInput").ap(),
        "fb_bc": nc.dram_tensor(pref + "fb_bc", [128, D], F32, kind="ExternalInput").ap(),
        "xns": nc.dram_tensor(pref + "xns", [T, D], BF16, kind="Internal").ap(),
        "xnTs": nc.dram_tensor(pref + "xnTs", [8, 128, T], BF16, kind="Internal").ap(),
        "qTs": nc.dram_tensor(pref + "qTs_f", [8, 128, T], BF16, kind="Internal").ap(),
        "aoTs": nc.dram_tensor(pref + "aoTs", [8, 128, T], BF16, kind="Internal").ap(),
    }


def _host_l1ffn_weights(inp, pref=""):
    qkvwT = np.ascontiguousarray(np.concatenate(
        [inp["L1_ffn_q_w"].T, inp["L1_ffn_k_w"].T, inp["L1_ffn_v_w"].T], axis=1))
    bqkv = np.concatenate([inp["L1_ffn_q_b"], inp["L1_ffn_k_b"], inp["L1_ffn_v_b"]])
    bqkv_bc = np.ascontiguousarray(np.broadcast_to(bqkv, (128, 3 * D)))
    inv = 1.0 / 10000.0 ** (np.arange(0, HD, 2, dtype=np.float32) / HD)
    freqs = np.outer(np.arange(T, dtype=np.float32), inv)
    emb = np.concatenate([freqs, freqs], axis=-1)
    cos = np.cos(emb).astype(np.float32).reshape(32, 128, 64).transpose(1, 0, 2)
    sin = np.sin(emb).astype(np.float32).reshape(32, 128, 64).transpose(1, 0, 2)
    return {
        pref + "qkvwT": qkvwT,
        pref + "bqkv_bc": bqkv_bc,
        pref + "cosF": np.ascontiguousarray(cos),
        pref + "sinF": np.ascontiguousarray(sin),
        pref + "gwT": np.ascontiguousarray(inp["L1_ffn_gate_w"].T),
        pref + "gb_bc": np.ascontiguousarray(np.broadcast_to(inp["L1_ffn_gate_b"], (128, 8))),
        pref + "ewT": np.ascontiguousarray(inp["L1_ffn_expert_w"].transpose(0, 2, 1)),
        pref + "eb": inp["L1_ffn_expert_b"],
        pref + "fwT": np.ascontiguousarray(inp["L1_ffn_ffn_w"].T),
        pref + "fb_bc": np.ascontiguousarray(np.broadcast_to(inp["L1_ffn_ffn_b"], (128, D))),
    }


def build_stageD():
    nc = bass.Bass("TRN2", target_bir_lowering=False)
    x = nc.dram_tensor("x", [T, D], F32, kind="ExternalInput").ap()
    out = nc.dram_tensor("out", [T, D], F32, kind="ExternalOutput").ap()
    W = _decl_l1ffn_weights(nc)
    with ExitStack() as es:
        r = Rec(nc, es)
        r.block = es.enter_context(nc.Block())
        g = _globals_setup(nc, es, r)
        emit_l1ffn(r, nc, g, x, out, W, "D")
        r.barrier()
        r.emit(r.block)
    return nc




def bcast16_g(ap2d, inner):
    return bass.AP(ap2d.tensor, ap2d.offset, [ap2d.ap[0], [0, 16], [1, inner]])


def emit_l1attn(r, nc, g, xin, xout, W, u):
    with ExitStack() as es:
        sb = lambda n, s, d: es.enter_context(nc.sbuf_tensor(n + u, s, d))
        scb = sb("scb", [128, 2], F32)
        r.op("vector", lambda e: e.memset(scb[:, 0:1], 0.125), writes=[("scb", u)])
        r.op("vector", lambda e: e.memset(scb[:, 1:2], -20.0), writes=[("scb", u)])
        kT = sb("kT", [128, 8, 4096], BF16)
        Vaug = sb("Vaug", [128, 32, 1040], BF16)
        Vv = Vaug[:].rearrange("p t (h c) -> p t h c", c=65)
        r.op("vector", lambda e: e.memset(Vv[:, :, :, 64:65], 1.0), writes=[("Vones", u)])

        with ExitStack() as e1:
            sb1 = lambda n, s, d: e1.enter_context(nc.sbuf_tensor(n + u, s, d))
            ps1 = lambda n, s, d: e1.enter_context(nc.psum_tensor(n + u, s, d))
            pt_b = ps1("ptb", [128, 128], BF16)
            xt = sb1("xt", [128, 2, 1024], F32)
            sqb = sb1("sq", [128, 1024], F32)
            xnt = sb1("xnt", [128, 2, 1024], BF16)
            stt = sb1("stt", [128, 2, 8], F32)
            xnTg = sb1("xnTg", [128, 8, 512], BF16)
            zrow = sb1("zrow", [1, 1024], BF16)
            r.op("vector", lambda e: e.memset(zrow[:], 0.0), writes=[("zrow", u)])
            r.dma("sync", W["vs"][0:1, :], zrow[:], reads=[("zrow", u)], sem="d_zr%s" % u)
            r.dma("sync", W["vs"][T + 1:T + 2, :], zrow[:], reads=[("zrow", u)], sem="d_zr%s" % u)
            for tg in range(8):
                for tt4 in range(4):
                    tt = tg * 4 + tt4
                    p = tt % 2
                    xkey = ("xt", u, p)
                    xtile = xt[:, p, :]
                    r.dma("sync", xtile, xin[tt * 128:(tt + 1) * 128, :], writes=[xkey],
                          sem="d_xt%d%s" % (p, u))
                    layernorm_tile(r, u, xtile, stt[:, p, :], sqb[:], ("stt", u, p), ("sq", u), xkey)
                    r.op("vector", lambda e, p=p: e.tensor_scalar(
                        xnt[:, p, :], xt[:, p, :], stt[:, p, 7:8], stt[:, p, 6:7], ALU.add, ALU.mult),
                         reads=[xkey, ("stt", u, p)], writes=[("xnt", u, p)])
                    for dc in range(8):
                        r.op("tensor", lambda e, p=p, dc=dc: e.transpose(
                            pt_b[:], xnt[:, p, dc * 128:(dc + 1) * 128], g["idsb"][:]),
                             reads=[("xnt", u, p), "idsb"], writes=[("ptb", u)])
                        r.op("vector", lambda e, dc=dc, tt4=tt4: e.tensor_copy(
                            xnTg[:, dc, tt4 * 128:(tt4 + 1) * 128], pt_b[:]),
                             reads=[("ptb", u)], writes=[("xnTg", u)])
                r.dma("sync", W["xnTs"][:, :, tg * 512:(tg + 1) * 512].rearrange("c p n -> p c n"),
                      xnTg[:], reads=[("xnTg", u)], sem="d_xnT%s" % u)
            r.barrier()
            r.emit(r.block)

        with ExitStack() as e1:
            sb1 = lambda n, s, d: e1.enter_context(nc.sbuf_tensor(n + u, s, d))
            ps1 = lambda n, s, d: e1.enter_context(nc.psum_tensor(n + u, s, d))
            pt_b = ps1("ptbb", [128, 128], BF16)
            pP = [ps1("pP%d" % i, [128, 512], F32) for i in range(2)]
            wpr = sb1("wpr", [128, 8, 1024], BF16)
            xnTg = sb1("xnTg2", [128, 2, 8, 512], BF16)
            bias_bc = sb1("bbc", [128, 1024], F32)
            tabs = sb1("tabs", [128, 2, 2, 32], F32)
            work = sb1("work", [128, 1024], F32)
            elu1 = sb1("elu1", [128, 1024], F32)
            tmp1 = sb1("tmp1", [128, 512], F32)
            tmp2 = sb1("tmp2", [128, 512], F32)
            rro = sb1("rro", [128, 1024], BF16)
            qstg = sb1("qstg", [128, 8, 512], BF16)
            vtile = sb1("vtile", [128, 2, 1024], BF16)

            for ip, pr in enumerate(["q", "k", "v"]):
                r.dma("gpsimd", wpr[:],
                      W["qkvwT"][:, ip * 1024:(ip + 1) * 1024].rearrange("(c p) n -> p c n", p=128),
                      writes=[("wpr", u)], sem="d_wp%s" % u)
                r.dma("sync", bias_bc[:], W["bqkv_bc"][:, ip * 1024:(ip + 1) * 1024],
                      writes=[("bbc", u)], sem="d_bbc%s" % u)
                for tg in range(8):
                    gp = tg % 2
                    r.dma("sync", xnTg[:, gp, :, :],
                          W["xnTs"][:, :, tg * 512:(tg + 1) * 512].rearrange("c p n -> p c n"),
                          writes=[("xnTg2", u, gp)], sem="d_xg%d%s" % (gp, u))
                    for tt4 in range(4):
                        tt = tg * 4 + tt4
                        if pr != "v":
                            r.dma("sync", tabs[:, tt % 2, 0, :], W["cosA"][:, tt, :],
                                  writes=[("tabs", u, tt % 2)], sem="d_tc%d%s" % (tt % 2, u))
                            r.dma("sync", tabs[:, tt % 2, 1, :], W["sinA"][:, tt, :],
                                  writes=[("tabs", u, tt % 2)], sem="d_ts%d%s" % (tt % 2, u))
                        for half in range(2):
                            bk = (tt4 * 2 + half) % 2
                            pk = ("pP%d" % bk, u)
                            r.mmgroup([
                                (lambda e, dc=dc, tt4=tt4, half=half, gp=gp, bk=bk: e.matmul(
                                    pP[bk][:], xnTg[:, gp, dc, tt4 * 128:(tt4 + 1) * 128],
                                    wpr[:, dc, half * 512:(half + 1) * 512],
                                    start=(dc == 0), stop=(dc == 7)))
                                for dc in range(8)],
                                reads=[("xnTg2", u, gp), ("wpr", u)], writes=[pk])
                            if pr == "v":
                                pv = pP[bk][:].rearrange("p (h c) -> p h c", c=64)
                                bv = bias_bc[:, half * 512:(half + 1) * 512].rearrange(
                                    "p (h c) -> p h c", c=64)
                                r.op("vector", lambda e, tt=tt, half=half, pv=pv, bv=bv: e.tensor_tensor(
                                    Vv[:, tt, half * 8:(half + 1) * 8, 0:64], pv, bv, ALU.add),
                                     reads=[pk, ("bbc", u)], writes=[("V", u)])
                                r.op("vector", lambda e, tt=tt, half=half, bk=bk, p=tt % 2: e.tensor_tensor(
                                    vtile[:, p, half * 512:(half + 1) * 512], pP[bk][:],
                                    bias_bc[:, half * 512:(half + 1) * 512], ALU.add),
                                     reads=[pk, ("bbc", u)], writes=[("vtile", u, tt % 2)])
                            else:
                                r.op("vector", lambda e, half=half, bk=bk: e.tensor_tensor(
                                    work[:, half * 512:(half + 1) * 512], pP[bk][:],
                                    bias_bc[:, half * 512:(half + 1) * 512], ALU.add),
                                     reads=[pk, ("bbc", u)], writes=[("work", u)])
                        if pr == "v":
                            p = tt % 2
                            r.dma("sync", W["vs"][1 + tt * 128:1 + (tt + 1) * 128, :],
                                  vtile[:, p, :], reads=[("vtile", u, p)],
                                  sem="d_vs%d%s" % (p, u))
                        else:
                            r.op("vector", lambda e: e.tensor_scalar_min(elu1[:], work[:], 0.0),
                                 reads=[("work", u)], writes=[("elu1", u)])
                            r.op("scalar", lambda e: e.activation(elu1[:], elu1[:], AF.Exp),
                                 reads=[("elu1", u)], writes=[("elu1", u)])
                            r.op("vector", lambda e: e.tensor_scalar_max(work[:], work[:], 0.0),
                                 reads=[("work", u)], writes=[("work", u)])
                            r.op("vector", lambda e: e.tensor_add(work[:], work[:], elu1[:]),
                                 reads=[("work", u), ("elu1", u)], writes=[("work", u)])
                            tp = tt % 2
                            wv = work[:].rearrange("p (h c two) -> p h c two", two=2, c=32)
                            rv = rro[:].rearrange("p (h c two) -> p h c two", two=2, c=32)
                            t1v = tmp1[:].rearrange("p (h c) -> p h c", c=32)
                            t2v = tmp2[:].rearrange("p (h c) -> p h c", c=32)
                            cb = bcast16_g(tabs[:, tp, 0, :], 32)
                            sb_ = bcast16_g(tabs[:, tp, 1, :], 32)
                            tk = ("tabs", u, tp)
                            r.op("vector", lambda e, cb=cb, wv=wv, t1v=t1v: e.tensor_tensor(
                                t1v, wv[:, :, :, 0], cb, ALU.mult),
                                 reads=[("work", u), tk], writes=[("tmp1", u)])
                            r.op("vector", lambda e, sb_=sb_, wv=wv, t2v=t2v: e.tensor_tensor(
                                t2v, wv[:, :, :, 1], sb_, ALU.mult),
                                 reads=[("work", u), tk], writes=[("tmp2", u)])
                            r.op("vector", lambda e, rv=rv, t1v=t1v, t2v=t2v: e.tensor_tensor(
                                rv[:, :, :, 0], t1v, t2v, ALU.subtract),
                                 reads=[("tmp1", u), ("tmp2", u)], writes=[("rro", u)])
                            r.op("vector", lambda e, sb_=sb_, wv=wv, t1v=t1v: e.tensor_tensor(
                                t1v, wv[:, :, :, 0], sb_, ALU.mult),
                                 reads=[("work", u), tk], writes=[("tmp1", u)])
                            r.op("vector", lambda e, cb=cb, wv=wv, t2v=t2v: e.tensor_tensor(
                                t2v, wv[:, :, :, 1], cb, ALU.mult),
                                 reads=[("work", u), tk], writes=[("tmp2", u)])
                            r.op("vector", lambda e, rv=rv, t1v=t1v, t2v=t2v: e.tensor_tensor(
                                rv[:, :, :, 1], t1v, t2v, ALU.add),
                                 reads=[("tmp1", u), ("tmp2", u)], writes=[("rro", u)])
                            for dc in range(8):
                                r.op("tensor", lambda e, dc=dc: e.transpose(
                                    pt_b[:], rro[:, dc * 128:(dc + 1) * 128], g["idsb"][:]),
                                     reads=[("rro", u), "idsb"], writes=[("ptb", u)])
                                if pr == "q":
                                    r.op("vector", lambda e, dc=dc, tt4=tt4: e.tensor_copy(
                                        qstg[:, dc, tt4 * 128:(tt4 + 1) * 128], pt_b[:]),
                                         reads=[("ptb", u)], writes=[("qstg", u)])
                                else:
                                    r.op("vector", lambda e, dc=dc, tt=tt: e.tensor_copy(
                                        kT[:, dc, tt * 128:(tt + 1) * 128], pt_b[:]),
                                         reads=[("ptb", u)], writes=[("kT", u)])
                    if pr == "q":
                        r.dma("sync", W["qTs"][:, :, tg * 512:(tg + 1) * 512].rearrange("c p n -> p c n"),
                              qstg[:], reads=[("qstg", u)], sem="d_qs%s" % u)
            r.barrier()
            r.emit(r.block)

        with ExitStack() as e2:
            sb2 = lambda n, s, d: e2.enter_context(nc.sbuf_tensor(n + u, s, d))
            ps2 = lambda n, s, d: e2.enter_context(nc.psum_tensor(n + u, s, d))
            pS = [ps2("pS%d" % i, [128, 1024], F32) for i in range(2)]
            pOT = [ps2("pOT%d" % i, [128, 512], F32) for i in range(2)]
            pt_f = ps2("ptf", [128, 128], F32)
            qTg = sb2("qTg", [128, 1, 8, 512], BF16)
            ET = sb2("ET", [128, 2, 1024], BF16)
            oT = sb2("oT", [128, 2, 512], F32)
            rz = sb2("rz", [128, 2], F32)
            ao = sb2("ao", [128, 4, 1024], BF16)
            vle = sb2("vle", [128, 2, 3, 1024], BF16)
            wl = sb2("wl", [128, 3, 1024], BF16)
            bl = sb2("bl", [128, 1024], BF16)
            lep = sb2("lep", [128, 2, 1024], F32)
            xt2 = sb2("xt2", [128, 2, 1024], F32)
            xo2 = sb2("xo2", [128, 2, 1024], F32)

            r.dma("sync", wl[:], W["wl_bc"], writes=[("wl", u)], sem="d_wl%s" % u)
            r.dma("sync", bl[:], W["bl_bc"], writes=[("bl", u)], sem="d_bl%s" % u)

            for qg in range(8):
                gp = 0
                r.dma("sync", qTg[:, 0, :, :],
                      W["qTs"][:, :, qg * 512:(qg + 1) * 512].rearrange("c p n -> p c n"),
                      writes=[("qTg", u, gp)], sem="d_qg%s" % u)
                for hp in range(8):
                    for kt in range(32):
                        bank = kt % 2
                        for h2 in range(2):
                            base = 64 * h2
                            r.mmgroup([(lambda e, hp=hp, kt=kt, bank=bank, base=base, h2=h2: e.matmul(
                                pS[bank][:, h2 * 512:(h2 + 1) * 512],
                                kT[base:base + 64, hp, kt * 128:(kt + 1) * 128],
                                qTg[base:base + 64, 0, hp, :], start=True, stop=True))],
                                reads=[("qTg", u, gp), ("kT", u)], writes=[("pS", u, bank, h2)])
                        ekey = ("ET", u, bank)
                        r.op("scalar", lambda e, bank=bank: e.activation(
                            ET[:, bank, :], pS[bank][:], AF.Exp, scale=scb[:, 0:1],
                            bias=scb[:, 1:2]),
                             reads=[("pS", u, bank, 0), ("pS", u, bank, 1), ("scb", u)],
                             writes=[ekey])
                        if kt > 0:
                            pb = 1 - bank
                            for h2 in range(2):
                                h = 2 * hp + h2
                                r.mmgroup([(lambda e, h=h, kt=kt - 1, pb=pb, h2=h2: e.matmul(
                                    pOT[h2][0:65, :], Vaug[:, kt, h * 65:(h + 1) * 65],
                                    ET[:, pb, h2 * 512:(h2 + 1) * 512],
                                    start=(kt == 0), stop=False))],
                                    reads=[("ET", u, pb), ("V", u), ("Vones", u)],
                                    writes=[("pOT%d" % h2, u)])
                    for h2 in range(2):
                        h = 2 * hp + h2
                        r.mmgroup([(lambda e, h=h, h2=h2: e.matmul(
                            pOT[h2][0:65, :], Vaug[:, 31, h * 65:(h + 1) * 65],
                            ET[:, 1, h2 * 512:(h2 + 1) * 512],
                            start=False, stop=True))],
                            reads=[("ET", u, 1), ("V", u), ("Vones", u)],
                            writes=[("pOT%d" % h2, u)])
                    for h2 in range(2):
                        h = 2 * hp + h2
                        ok = ("pOT%d" % h2, u)
                        r.op("vector", lambda e, h2=h2: e.tensor_copy(oT[0:65, h2, :], pOT[h2][0:65, :]),
                             reads=[ok], writes=[("oT", u, h2)])
                        for qs in range(4):
                            r.op("tensor", lambda e, h2=h2, qs=qs: e.transpose(
                                pt_f[:, 0:65], oT[0:65, h2, qs * 128:(qs + 1) * 128],
                                g["idsf"][0:65, 0:65]),
                                 reads=[("oT", u, h2), "idsf"], writes=[("ptf", u)])
                            r.op("vector", lambda e, h2=h2: e.reciprocal(
                                rz[:, h2:h2 + 1], pt_f[:, 64:65]),
                                 reads=[("ptf", u)], writes=[("rz", u, h2)])
                            r.op("vector", lambda e, h2=h2, qs=qs, h=h: e.tensor_scalar(
                                ao[:, qs, h * 64:(h + 1) * 64], pt_f[:, 0:64],
                                rz[:, h2:h2 + 1], None, ALU.mult),
                                 reads=[("ptf", u), ("rz", u, h2)], writes=[("ao", u)])
                for qs in range(4):
                    tt = qg * 4 + qs
                    p = tt % 2
                    lk = ("vle", u, p)
                    r.dma("sync", vle[:, p, 0, :], W["vs"][tt * 128:(tt + 1) * 128, :],
                          writes=[lk], sem="d_vm%d%s" % (p, u))
                    r.dma("sync", vle[:, p, 1, :], W["vs"][1 + tt * 128:1 + (tt + 1) * 128, :],
                          writes=[lk], sem="d_v0%d%s" % (p, u))
                    r.dma("sync", vle[:, p, 2, :], W["vs"][2 + tt * 128:2 + (tt + 1) * 128, :],
                          writes=[lk], sem="d_vp%d%s" % (p, u))
                    r.dma("sync", xt2[:, p, :], xin[tt * 128:(tt + 1) * 128, :],
                          writes=[("xt2", u, p)], sem="d_x2%d%s" % (p, u))
                    r.op("vector", lambda e, p=p: e.tensor_tensor(
                        lep[:, p, :], vle[:, p, 0, :], wl[:, 0, :], ALU.mult),
                         reads=[lk, ("wl", u)], writes=[("lep", u, p)])
                    r.op("vector", lambda e, p=p: e.tensor_tensor(
                        xo2[:, p, :], vle[:, p, 1, :], wl[:, 1, :], ALU.mult),
                         reads=[lk, ("wl", u)], writes=[("xo2", u, p)])
                    r.op("vector", lambda e, p=p: e.tensor_add(
                        lep[:, p, :], lep[:, p, :], xo2[:, p, :]),
                         reads=[("lep", u, p), ("xo2", u, p)], writes=[("lep", u, p)])
                    r.op("vector", lambda e, p=p: e.tensor_tensor(
                        xo2[:, p, :], vle[:, p, 2, :], wl[:, 2, :], ALU.mult),
                         reads=[lk, ("wl", u)], writes=[("xo2", u, p)])
                    r.op("vector", lambda e, p=p: e.tensor_add(
                        lep[:, p, :], lep[:, p, :], xo2[:, p, :]),
                         reads=[("lep", u, p), ("xo2", u, p)], writes=[("lep", u, p)])
                    r.op("vector", lambda e, p=p: e.tensor_add(
                        lep[:, p, :], lep[:, p, :], bl[:]),
                         reads=[("lep", u, p), ("bl", u)], writes=[("lep", u, p)])
                    r.op("vector", lambda e, p=p, qs=qs: e.tensor_add(
                        lep[:, p, :], lep[:, p, :], ao[:, qs, :]),
                         reads=[("lep", u, p), ("ao", u)], writes=[("lep", u, p)])
                    r.op("vector", lambda e, p=p: e.tensor_add(
                        xo2[:, p, :], lep[:, p, :], xt2[:, p, :]),
                         reads=[("lep", u, p), ("xt2", u, p)], writes=[("xo2", u, p)])
                    r.dma("sync", xout[tt * 128:(tt + 1) * 128, :], xo2[:, p, :],
                          reads=[("xo2", u, p)], sem="d_oe%d%s" % (p, u))
            r.barrier()
            r.emit(r.block)


def _decl_l1attn_weights(nc, pref=""):
    return {
        "qkvwT": nc.dram_tensor(pref + "aqkvwT", [D, 3 * D], F32, kind="ExternalInput").ap(),
        "bqkv_bc": nc.dram_tensor(pref + "abqkv_bc", [128, 3 * D], F32, kind="ExternalInput").ap(),
        "cosA": nc.dram_tensor(pref + "cosA", [128, 32, 32], F32, kind="ExternalInput").ap(),
        "sinA": nc.dram_tensor(pref + "sinA", [128, 32, 32], F32, kind="ExternalInput").ap(),
        "wl_bc": nc.dram_tensor(pref + "wl_bc", [128, 3, D], BF16, kind="ExternalInput").ap(),
        "bl_bc": nc.dram_tensor(pref + "bl_bc", [128, D], BF16, kind="ExternalInput").ap(),
        "xnTs": nc.dram_tensor(pref + "xnTs_e", [8, 128, T], BF16, kind="Internal").ap(),
        "qTs": nc.dram_tensor(pref + "qTs_e", [8, 128, T], BF16, kind="Internal").ap(),
        "vs": nc.dram_tensor(pref + "vs_e", [T + 2, D], BF16, kind="Internal").ap(),
    }


def _host_l1attn_weights(inp, pref=""):
    w = inp["L1_attn_w_qkv"]
    b = inp["L1_attn_b_qkv"]
    qkvwT = np.ascontiguousarray(w.T)
    bqkv_bc = np.ascontiguousarray(np.broadcast_to(b, (128, 3 * D)))
    pos = np.arange(T, dtype=np.float32)[:, None]
    div = np.exp(np.arange(0, HD, 2, dtype=np.float32) * (-np.log(10000.0) / HD))
    ang = (pos * div).astype(np.float32)
    cos = np.cos(ang).astype(np.float32).reshape(32, 128, 32).transpose(1, 0, 2)
    sin = np.sin(ang).astype(np.float32).reshape(32, 128, 32).transpose(1, 0, 2)
    wle = inp["L1_attn_w_lepe"].reshape(D, 3)
    wl_bc = np.ascontiguousarray(np.broadcast_to(wle.T[:, None, :], (3, 128, D)).transpose(1, 0, 2))
    bl_bc = np.ascontiguousarray(np.broadcast_to(inp["L1_attn_b_lepe"], (128, D)))
    return {
        pref + "aqkvwT": qkvwT,
        pref + "abqkv_bc": bqkv_bc,
        pref + "cosA": np.ascontiguousarray(cos),
        pref + "sinA": np.ascontiguousarray(sin),
        pref + "wl_bc": wl_bc.astype(BF),
        pref + "bl_bc": bl_bc.astype(BF),
    }


def build_stageE():
    nc = bass.Bass("TRN2", target_bir_lowering=False)
    x = nc.dram_tensor("x", [T, D], F32, kind="ExternalInput").ap()
    out = nc.dram_tensor("out", [T, D], F32, kind="ExternalOutput").ap()
    W = _decl_l1attn_weights(nc)
    with ExitStack() as es:
        r = Rec(nc, es)
        r.block = es.enter_context(nc.Block())
        g = _globals_setup(nc, es, r)
        emit_l1attn(r, nc, g, x, out, W, "E")
        r.barrier()
        r.emit(r.block)
    return nc


def build_fused():
    nc = bass.Bass("TRN2", target_bir_lowering=False)
    x = nc.dram_tensor("x", [T, D], F32, kind="ExternalInput").ap()
    out = nc.dram_tensor("out", [T, D], F32, kind="ExternalOutput").ap()
    xb0 = nc.dram_tensor("xb0", [T, D], F32, kind="Internal").ap()
    xb1 = nc.dram_tensor("xb1", [T, D], F32, kind="Internal").ap()
    WA = _decl_l0ffn_weights(nc)
    WB = _decl_l0attn_weights(nc)
    WD = _decl_l1ffn_weights(nc)
    WE = _decl_l1attn_weights(nc)
    with ExitStack() as es:
        r = Rec(nc, es)
        r.block = es.enter_context(nc.Block())
        g = _globals_setup(nc, es, r)
        emit_l0ffn(r, nc, g, x, xb0, WA, "A")
        emit_l0attn(r, nc, g, xb0, xb1, WB, "B")
        emit_l0ffn(r, nc, g, xb1, xb0, WA, "C", su="A")
        emit_l1ffn(r, nc, g, xb0, xb1, WD, "D")
        emit_l1attn(r, nc, g, xb1, xb0, WE, "E")
        emit_l1ffn(r, nc, g, xb0, out, WD, "F", su="D")
        r.barrier()
        r.emit(r.block)
    return nc


def kernel(**inputs):
    inputs = {k: np.asarray(v) for k, v in inputs.items()}
    wm = _host_globals()
    wm.update(_host_l0ffn_weights(inputs))
    wm.update(_host_l0attn_weights(inputs))
    wm.update(_host_l1ffn_weights(inputs))
    wm.update(_host_l1attn_weights(inputs))
    gum = _host_gumbel()
    nc = build_fused()
    in_maps = []
    for c in range(CORES):
        im = dict(wm)
        im["x"] = np.ascontiguousarray(inputs["L0_attn_hidden_states"][c])
        im["gum"] = gum[c]
        in_maps.append(im)
    res = run_bass_kernel_spmd(nc, in_maps, core_ids=list(range(CORES)))
    return np.stack([np.asarray(res.results[c]["out"]) for c in range(CORES)]).astype(np.float32)